# version 03
# baseline (speedup 1.0000x reference)
import math
import numpy as np
from contextlib import ExitStack

import concourse.bass as bass
import concourse.mybir as mybir
from concourse.bass_utils import run_bass_kernel_spmd

F32 = mybir.dt.float32
BF16 = mybir.dt.bfloat16
I32 = mybir.dt.int32
ALU = mybir.AluOpType
AF = mybir.ActivationFunctionType
AX = mybir.AxisListType.X

PI = math.pi
LN1E4 = math.log(10000.0)
SCALE = 1.0 / math.sqrt(128.0)
KAPPA = 1.702
LIMIT = 7.0
NEG = -1.0e30
EPS = 1e-5

NT_CTX, NT_OWN, FC, H, CD = 32, 8, 16, 16, 2048


def build(stage=3, dbg=()):
    nc = bass.Bass("TRN2", target_bir_lowering=False)

    x_ctx = nc.dram_tensor("x_ctx", [4096, CD], F32, kind="ExternalInput").ap()
    x_own = nc.dram_tensor("x_own", [1024, CD], F32, kind="ExternalInput").ap()
    maskd = nc.dram_tensor("maskd", [128, 4096], F32, kind="ExternalInput").ap()
    pos_own = nc.dram_tensor("pos_own", [128, 8], F32, kind="ExternalInput").ap()
    pos_ctx = nc.dram_tensor("pos_ctx", [128, 32], F32, kind="ExternalInput").ap()
    jrow = nc.dram_tensor("jrow", [128, 64], F32, kind="ExternalInput").ap()
    eye_f = nc.dram_tensor("eye_f", [128, 128], F32, kind="ExternalInput").ap()
    wq_d = nc.dram_tensor("wq_t", [128, FC * CD], F32, kind="ExternalInput").ap()
    kvw_d = nc.dram_tensor("kvw_t", [128, FC * 128], F32, kind="ExternalInput").ap()
    kexp_d = nc.dram_tensor("kexp_t", [128, H * 64], F32, kind="ExternalInput").ap()
    vexp_d = nc.dram_tensor("vexp_t", [64, H * 128], F32, kind="ExternalInput").ap()
    proj_d = nc.dram_tensor("proj_t", [128, 4 * FC * 512], F32, kind="ExternalInput").ap()
    rwT_d = nc.dram_tensor("rwT_t", [128, FC * 16], F32, kind="ExternalInput").ap()
    rbias_d = nc.dram_tensor("rbias", [1, 16], F32, kind="ExternalInput").ap()
    wg_d = nc.dram_tensor("wg_t", [16, 128, 4 * FC * 512], F32, kind="ExternalInput").ap()
    wu_d = nc.dram_tensor("wu_t", [16, 128, 4 * FC * 512], F32, kind="ExternalInput").ap()
    biasg_d = nc.dram_tensor("biasg", [16, CD], F32, kind="ExternalInput").ap()
    biasu_d = nc.dram_tensor("biasu", [16, CD], F32, kind="ExternalInput").ap()
    down_d = nc.dram_tensor("down_t", [16, 128, 4 * FC * 512], F32, kind="ExternalInput").ap()
    dbias_d = nc.dram_tensor("dbias", [16, CD], F32, kind="ExternalInput").ap()
    out_d = nc.dram_tensor("out", [1024, CD], F32, kind="ExternalOutput").ap()

    dbg_d = {}
    for name, shape in [
        ("d_klatT", [64, 4096]), ("d_vlat", [128, NT_CTX * 64]),
        ("d_xnT", [128, FC * 1024]), ("d_qtl", [128, 8 * 1024]),
        ("d_scoresP", [128, 128]), ("d_scoresT", [16, 1024]),
        ("d_acc", [128, NT_OWN * CD]), ("d_big", [128, FC * 1024]),
    ]:
        if name in dbg:
            dbg_d[name] = nc.dram_tensor(name, shape, F32, kind="ExternalOutput").ap()

    gst = ExitStack()
    open_stacks = [gst]

    def sb(stk, name, shape, dt):
        return stk.enter_context(nc.sbuf_tensor(name, shape, dt))

    def psmt(stk, name, shape, dt):
        return stk.enter_context(nc.psum_tensor(name, shape, dt))

    xnT = sb(gst, "xnT", [128, FC * 1024], BF16)
    accb = sb(gst, "accb", [128, NT_OWN * CD], F32)
    big = sb(gst, "big", [128, FC * 1024], BF16)
    eyeb = sb(gst, "eyeb", [128, 128], BF16)
    eyef = sb(gst, "eyef", [128, 128], F32)
    scoresP = sb(gst, "scoresP", [128, 128], F32)
    scoresR = sb(gst, "scoresR", [128, 16], F32)
    scoresT = sb(gst, "scoresT", [16, 1024], BF16)
    rbias_bc = sb(gst, "rbias_bc", [128, 16], F32)

    sems = {}
    for k in ("l", "w", "o", "t", "v", "s"):
        sems[k] = gst.enter_context(nc.semaphore("sem_" + k))
    cnt = {k: 0 for k in sems}

    block = gst.enter_context(nc.Block())

    def inc(key, ins, k=1):
        cnt[key] += k
        ins.then_inc(sems[key], k)
        return ins

    guards = {}

    def need(e, key):
        if key in guards:
            k, v = guards[key]
            if v > 0:
                e.wait_ge(sems[k], v)

    def set_guard(key, k):
        guards[key] = (k, cnt[k])

    def vseq(v, ins):
        inc("v", ins)
        v.wait_ge(sems["v"], cnt["v"])
        return ins

    def wait(e, *keys):
        for k in keys:
            if cnt[k] > 0:
                e.wait_ge(sems[k], cnt[k])

    def barrier():
        vals = dict(cnt)

        def wa(e):
            for k, v in vals.items():
                if v > 0:
                    e.wait_ge(sems[k], v)
        for name in ("sync", "gpsimd", "tensor", "vector", "scalar"):
            getattr(block, name)(wa)

    ast = ExitStack()
    open_stacks.append(ast)
    klatT = sb(ast, "klatT", [128, 4096], BF16)
    vlat = sb(ast, "vlat", [128, NT_CTX * 64], BF16)
    qtl = sb(ast, "qtl", [128, 8 * 1024], BF16)
    kexp = sb(ast, "kexp", [128, H * 64], BF16)
    vexpb = sb(ast, "vexpb", [64, H * 128], BF16)


    s12 = ExitStack()
    open_stacks.append(s12)
    kvw = sb(s12, "kvw", [128, FC * 128], BF16)
    rwT = sb(s12, "rwT", [128, FC * 16], BF16)
    invf = sb(s12, "invf", [128, 64], F32)
    invfk = sb(s12, "invfk", [128, 32], F32)
    jrow_sb = sb(s12, "jrow_sb", [128, 64], F32)
    poso = sb(s12, "poso", [128, 8], F32)
    posc = sb(s12, "posc", [128, 32], F32)
    red = sb(s12, "red", [128, 64], F32)
    kib = sb(s12, "kib", [128, 64], I32)
    cosqa = sb(s12, "cosqa", [128, 512], F32)
    sinqa = sb(s12, "sinqa", [128, 512], F32)
    cosk = sb(s12, "cosk", [128, 32], F32)
    sink = sb(s12, "sink", [128, 32], F32)
    lnst = sb(s12, "lnst", [128, 16], F32)
    tA = sb(s12, "tA", [128, 128], F32)
    tB = sb(s12, "tB", [128, 128], F32)
    xt = sb(s12, "xt", [128, 2 * CD], F32)
    xnt = sb(s12, "xnt", [128, CD], BF16)
    xnTs = sb(s12, "xnTs", [128, CD], BF16)
    qro4 = sb(s12, "qro4", [128, 512], BF16)
    qT4 = sb(s12, "qT4", [128, 512], BF16)
    kro = sb(s12, "kro", [128, 64], BF16)


    @block.sync
    def _(s):
        s.dma_start(eyef[:, :], eye_f).then_inc(sems["l"], 16)
        s.dma_start(jrow_sb[:, :], jrow).then_inc(sems["l"], 16)
        s.dma_start(poso[:, :], pos_own).then_inc(sems["l"], 16)
        s.dma_start(posc[:, :], pos_ctx).then_inc(sems["l"], 16)
        s.dma_start(rbias_bc[:, :], bass.AP(rbias_d.tensor, 0, [[0, 128], [1, 16]])).then_inc(sems["l"], 16)
    cnt["l"] += 16 * 5
    P0_L = cnt["l"]

    @block.scalar
    def _(sc):
        sc.wait_ge(sems["l"], P0_L)
        sc.activation(invf[:, :], jrow_sb[:, :], AF.Exp, scale=-LN1E4 / 64.0)
        inc("s", sc.activation(invfk[:, :], jrow_sb[:, :32], AF.Exp, scale=-LN1E4 / 32.0))

    def sincos(pos_col, invf_ap, n, cos_dst, sin_dst):
        @block.vector
        def _(v):
            wait(v, "s", "l")
            vseq(v, v.tensor_scalar(tA[:, :n], invf_ap, pos_col, None, ALU.mult))
            vseq(v, v.tensor_scalar_mul(red[:, :n], tA[:, :n], 1.0 / (2 * PI)))
            inc("v", v.tensor_copy(kib[:, :n], red[:, :n]))

        @block.scalar
        def _(sc):
            wait(sc, "v")
            inc("s", sc.copy(red[:, :n], kib[:, :n]))

        @block.vector
        def _(v):
            wait(v, "s")
            inc("v", v.scalar_tensor_tensor(red[:, :n], red[:, :n], -2 * PI, tA[:, :n], ALU.mult, ALU.add))

        @block.scalar
        def _(sc):
            wait(sc, "v")
            inc("s", sc.activation(sin_dst, red[:, :n], AF.Sin))

        @block.vector
        def _(v):
            wait(v, "s")
            vseq(v, v.tensor_scalar_add(tB[:, :n], tA[:, :n], PI / 2))
            vseq(v, v.tensor_scalar_mul(red[:, :n], tB[:, :n], 1.0 / (2 * PI)))
            inc("v", v.tensor_copy(kib[:, :n], red[:, :n]))

        @block.scalar
        def _(sc):
            wait(sc, "v")
            inc("s", sc.copy(red[:, :n], kib[:, :n]))

        @block.vector
        def _(v):
            wait(v, "s")
            inc("v", v.scalar_tensor_tensor(red[:, :n], red[:, :n], -2 * PI, tB[:, :n], ALU.mult, ALU.add))

        @block.scalar
        def _(sc):
            wait(sc, "v")
            inc("s", sc.activation(cos_dst, red[:, :n], AF.Sin))

    @block.gpsimd
    def _(g):
        g.dma_start(eyeb[:, :], eye_f).then_inc(sems["w"], 16)
        g.dma_start(kvw[:, :], kvw_d).then_inc(sems["w"], 16)
        g.dma_start(kexp[:, :], kexp_d).then_inc(sems["w"], 16)
        g.dma_start(vexpb[:, :], vexp_d).then_inc(sems["w"], 16)
        g.dma_start(rwT[:, :], rwT_d).then_inc(sems["w"], 16)
    cnt["w"] += 16 * 5
    P0_W = cnt["w"]

    p12 = ExitStack()
    open_stacks.append(p12)
    pk0 = psmt(p12, "pk0", [128, 1024], BF16)
    pk1 = psmt(p12, "pk1", [128, 1024], BF16)
    pq0 = psmt(p12, "pq0", [128, 512], F32)
    pq1 = psmt(p12, "pq1", [128, 512], F32)
    pqt = psmt(p12, "pqt", [128, 128], F32)
    prt = psmt(p12, "prt", [128, 128], F32)
    ptrash = psmt(p12, "ptrash", [128, 512], F32)

    def ln_tile(half):
        xv = xt[:, half * CD:(half + 1) * CD]

        @block.vector
        def _(v):
            wait(v, "l")
            inc("v", v.reduce_sum(lnst[:, 0:1], xv, axis=AX))

        @block.scalar
        def _(sc):
            wait(sc, "l")
            ins = None
            for c4 in range(4):
                ins = sc.activation(ptrash[:, :], xv[:, c4 * 512:(c4 + 1) * 512], AF.Square,
                                    accum_out=lnst[:, 8 + c4:9 + c4])
            inc("s", ins)

        @block.vector
        def _(v):
            wait(v, "s", "v")
            vseq(v, v.tensor_scalar_mul(lnst[:, 1:2], lnst[:, 0:1], 1.0 / CD))
            vseq(v, v.reduce_sum(lnst[:, 2:3], lnst[:, 8:12], axis=AX))
            vseq(v, v.tensor_scalar(lnst[:, 3:4], lnst[:, 2:3], 1.0 / CD, EPS, ALU.mult, ALU.add))
            vseq(v, v.tensor_mul(lnst[:, 4:5], lnst[:, 1:2], lnst[:, 1:2]))
            inc("v", v.scalar_tensor_tensor(lnst[:, 5:6], lnst[:, 4:5], -1.0, lnst[:, 3:4],
                                            ALU.mult, ALU.add))

        @block.scalar
        def _(sc):
            wait(sc, "v")
            sc.activation(lnst[:, 6:7], lnst[:, 5:6], AF.Ln)
            inc("s", sc.activation(lnst[:, 7:8], lnst[:, 6:7], AF.Exp, scale=-0.5))

        @block.vector
        def _(v):
            wait(v, "s")
            inc("v", v.tensor_scalar(xnt[:, :], xv, lnst[:, 1:2], lnst[:, 7:8],
                                     ALU.subtract, ALU.mult))

    def xn_transposes(own_j):
        @block.tensor
        def _(t):
            wait(t, "v")
            t.wait_ge(sems["w"], 16)
            need(t, "pk")
            ins = None
            for fc in range(FC):
                pk = pk0 if fc < 8 else pk1
                ins = t.transpose(pk[:, (fc % 8) * 128:(fc % 8 + 1) * 128],
                                  xnt[:, fc * 128:(fc + 1) * 128], eyeb[:, :])
            inc("t", ins)

        @block.vector
        def _(v):
            wait(v, "t")
            pv0 = pk0[:, :].rearrange("p (f t) -> p f t", f=8)
            pv1 = pk1[:, :].rearrange("p (f t) -> p f t", f=8)
            if own_j is None:
                v.tensor_copy(xnTs[:, 0:1024], pk0[:, :])
                inc("v", v.tensor_copy(xnTs[:, 1024:2048], pk1[:, :]))
            else:
                xv = xnT[:, :].rearrange("p (f jt t) -> p f jt t", f=FC, jt=8)
                v.tensor_copy(xv[:, 0:8, own_j, :], pv0)
                inc("v", v.tensor_copy(xv[:, 8:16, own_j, :], pv1))
        set_guard("pk", "v")

    if stage >= 1:
        for tau in range(NT_CTX):
            half = tau % 2

            @block.sync
            def _(s, tau=tau, half=half):
                need(s, ("xt", half))
                s.dma_start(xt[:, half * CD:(half + 1) * CD],
                            x_ctx[tau * 128:(tau + 1) * 128, :]).then_inc(sems["l"], 16)
            cnt["l"] += 16
            ln_tile(half)
            set_guard(("xt", half), "v")
            xn_transposes(None)

            @block.tensor
            def _(t):
                wait(t, "v")
                t.wait_ge(sems["w"], 32)
                need(t, ("pq", 0))
                ins = None
                for fc in range(FC):
                    ins = t.matmul(pq0[:, 0:128], xnTs[:, fc * 128:(fc + 1) * 128],
                                   kvw[:, fc * 128:(fc + 1) * 128],
                                   start=(fc == 0), stop=(fc == FC - 1))
                inc("t", ins)

            sincos(posc[:, tau:tau + 1], invfk[:, :], 32, cosk[:, :], sink[:, :])

            @block.vector
            def _(v, tau=tau):
                wait(v, "t", "s")
                v.tensor_mul(tA[:, 0:32], pq0[:, 0:64:2], cosk[:, :])
                vseq(v, v.tensor_mul(tB[:, 0:32], pq0[:, 1:64:2], sink[:, :]))
                v.tensor_sub(kro[:, 0:32], tA[:, 0:32], tB[:, 0:32])
                v.tensor_mul(tA[:, 32:64], pq0[:, 0:64:2], sink[:, :])
                vseq(v, v.tensor_mul(tB[:, 32:64], pq0[:, 1:64:2], cosk[:, :]))
                v.tensor_add(kro[:, 32:64], tA[:, 32:64], tB[:, 32:64])
                inc("v", v.tensor_copy(vlat[:, tau * 64:(tau + 1) * 64], pq0[:, 64:128]))
            set_guard(("pq", 0), "v")

            @block.tensor
            def _(t):
                wait(t, "v")
                need(t, "pk")
                t.transpose(pk0[0:64, 0:128], kro[:, :], eyeb[:, :])
                inc("t", t.transpose(pk0[64:128, 0:128], kro[:, :], eyeb[:, :],
                                     tile_position=(0, 64)))

            @block.vector
            def _(v, tau=tau):
                wait(v, "t")
                inc("v", v.tensor_copy(klatT[:, tau * 128:(tau + 1) * 128], pk0[:, 0:128]))
            set_guard("pk", "v")

    if stage >= 1:
        for j in range(NT_OWN):
            half = j % 2

            @block.sync
            def _(s, j=j, half=half):
                need(s, ("xt", half))
                s.dma_start(xt[:, half * CD:(half + 1) * CD],
                            x_own[j * 128:(j + 1) * 128, :]).then_inc(sems["l"], 16)
            cnt["l"] += 16
            ln_tile(half)
            set_guard(("xt", half), "v")
            xn_transposes(own_j=j)

            sincos(poso[:, j:j + 1], invf[:, :], 64,
                   cosqa[:, j * 64:(j + 1) * 64], sinqa[:, j * 64:(j + 1) * 64])

            @block.tensor
            def _(t, j=j):
                t.wait_ge(sems["w"], P0_W)
                wait(t, "v")
                need(t, "prt")
                ins = None
                for fc in range(FC):
                    ins = t.matmul(prt[:, 0:16], xnT[:, fc * 1024 + j * 128: fc * 1024 + (j + 1) * 128],
                                   rwT[:, fc * 16:(fc + 1) * 16],
                                   start=(fc == 0), stop=(fc == FC - 1))
                inc("t", ins)

            @block.vector
            def _(v):
                wait(v, "t", "v")
                v.wait_ge(sems["l"], P0_L)
                lg, cur = tA[:, 0:16], tA[:, 16:32]
                nm1, nmk, cmp = tA[:, 32:33], tA[:, 33:34], tA[:, 34:50]
                vseq(v, v.tensor_add(lg, prt[:, 0:16], rbias_bc[:, :]))
                vseq(v, v.reduce_max(nm1, lg, axis=AX, negate=True))
                vseq(v, v.tensor_copy(cur, lg))
                for it in range(3):
                    nm = nm1 if it == 0 else nmk
                    vseq(v, v.tensor_scalar(cmp, cur, nm, 0.0, ALU.add, ALU.is_ge))
                    vseq(v, v.scalar_tensor_tensor(cur, cmp, NEG, cur, ALU.mult, ALU.add))
                    vseq(v, v.reduce_max(nmk, cur, axis=AX, negate=True))
                inc("v", v.tensor_scalar(cmp, lg, nmk, 0.0, ALU.add, ALU.is_ge))

            @block.scalar
            def _(sc):
                wait(sc, "v")
                inc("s", sc.activation(tA[:, 50:66], tA[:, 0:16], AF.Exp, bias=tA[:, 32:33]))

            @block.vector
            def _(v, j=j):
                wait(v, "s")
                ew, ssum, rs = tA[:, 50:66], tA[:, 66:67], tA[:, 67:68]
                vseq(v, v.tensor_mul(ew, ew, tA[:, 34:50]))
                vseq(v, v.reduce_sum(ssum, ew, axis=AX))
                vseq(v, v.reciprocal(rs, ssum))
                vseq(v, v.tensor_scalar(scoresR[:, :], ew, rs, None, ALU.mult))
                inc("v", v.tensor_scalar_mul(scoresP[:, j * 16:(j + 1) * 16], scoresR[:, :],
                                             1.0 / KAPPA))

            @block.tensor
            def _(t):
                wait(t, "v")
                inc("t", t.transpose(prt[0:16, :], scoresR[:, :], eyef[:, :]))

            @block.vector
            def _(v, j=j):
                wait(v, "t")
                inc("v", v.tensor_copy(scoresT[:, j * 128:(j + 1) * 128], prt[0:16, :]))
            set_guard("prt", "v")

    if stage >= 1:
        XNT_DONE_V = cnt["v"]
        for cc in range(4):
            bh = cc % 2

            @block.gpsimd
            def _(g, cc=cc, bh=bh):
                need(g, ("bigq", bh))
                g.dma_start(big[:, bh * 8192:(bh + 1) * 8192],
                            wq_d[:, cc * 8192:(cc + 1) * 8192]).then_inc(sems["w"], 16)
            cnt["w"] += 16
            WQ_W = cnt["w"]

            for j in range(NT_OWN):
                ji = j % 2
                pq = pq0 if ji == 0 else pq1

                @block.tensor
                def _(t, j=j, cc=cc, bh=bh, pq=pq, ji=ji, WQ_W=WQ_W):
                    t.wait_ge(sems["w"], WQ_W)
                    t.wait_ge(sems["v"], XNT_DONE_V)
                    need(t, ("pq", ji))
                    ins = None
                    for fc in range(FC):
                        ins = t.matmul(pq[:, :], xnT[:, fc * 1024 + j * 128: fc * 1024 + (j + 1) * 128],
                                       big[:, bh * 8192 + fc * 512: bh * 8192 + (fc + 1) * 512],
                                       start=(fc == 0), stop=(fc == FC - 1))
                    inc("t", ins)

                @block.vector
                def _(v, j=j, pq=pq):
                    wait(v, "t")
                    need(v, "qro4")
                    cq = cosqa[:, j * 64:(j + 1) * 64]
                    sq = sinqa[:, j * 64:(j + 1) * 64]
                    for hh in range(4):
                        x1 = pq[:, hh * 128 + 0: hh * 128 + 128: 2]
                        x2 = pq[:, hh * 128 + 1: hh * 128 + 128: 2]
                        v.tensor_mul(tA[:, 0:64], x1, cq)
                        vseq(v, v.tensor_mul(tB[:, 0:64], x2, sq))
                        v.tensor_sub(qro4[:, hh * 128: hh * 128 + 64], tA[:, 0:64], tB[:, 0:64])
                        v.tensor_mul(tA[:, 64:128], x1, sq)
                        vseq(v, v.tensor_mul(tB[:, 64:128], x2, cq))
                        v.tensor_add(qro4[:, hh * 128 + 64: hh * 128 + 128], tA[:, 64:128], tB[:, 64:128])
                    inc("v", v.engine_nop())
                set_guard(("pq", ji), "v")

                @block.tensor
                def _(t):
                    wait(t, "v")
                    need(t, "pk")
                    ins = None
                    for hh in range(4):
                        ins = t.transpose(pk0[:, hh * 128:(hh + 1) * 128],
                                          qro4[:, hh * 128:(hh + 1) * 128], eyeb[:, :])
                    inc("t", ins)
                set_guard("qro4", "t")

                @block.vector
                def _(v):
                    wait(v, "t")
                    need(v, "qT4")
                    inc("v", v.tensor_copy(qT4[:, :], pk0[:, 0:512]))
                set_guard("pk", "v")

                for hh in range(4):
                    h = cc * 4 + hh
                    po = h % 2

                    @block.tensor
                    def _(t, h=h, hh=hh, po=po):
                        wait(t, "v")
                        need(t, "pqt")
                        dst = pqt[64:128, :] if po else pqt[0:64, :]
                        inc("t", t.matmul(dst, kexp[:, h * 64:(h + 1) * 64],
                                          qT4[:, hh * 128:(hh + 1) * 128], start=True, stop=True,
                                          tile_position=(0, 64) if po else None))

                    @block.vector
                    def _(v, h=h, j=j, po=po):
                        wait(v, "t")
                        src = pqt[64:128, :] if po else pqt[0:64, :]
                        dst = qtl[po * 64:po * 64 + 64,
                                  (h // 2) * 1024 + j * 128:(h // 2) * 1024 + (j + 1) * 128]
                        inc("v", v.tensor_copy(dst, src))
                    set_guard("pqt", "v")
                set_guard("qT4", "t")
            set_guard(("bigq", bh), "t")

    if stage >= 2:
        p12.close()
        s12.close()
        open_stacks.remove(p12)
        open_stacks.remove(s12)
        s3 = ExitStack()
        open_stacks.append(s3)
        mask_sb = sb(s3, "mask_sb", [128, 512], F32)
        scores2 = sb(s3, "scores2", [128, 2 * 4096], F32)
        att = [sb(s3, "att0", [128, 512], BF16), sb(s3, "att1", [128, 512], BF16)]
        attT = [sb(s3, "attT0", [128, 512], BF16), sb(s3, "attT1", [128, 512], BF16)]
        nm2 = sb(s3, "nm2", [128, 32], F32)
        ybarb = sb(s3, "ybarb", [128, 128], BF16)
        ybT_sb = sb(s3, "ybT_sb", [64, 128], BF16)

        p3 = ExitStack()
        open_stacks.append(p3)
        ps_s = [psmt(p3, "ps_s0", [128, 512], F32), psmt(p3, "ps_s1", [128, 512], F32)]
        ps_aT = [psmt(p3, "ps_aT0", [128, 512], BF16), psmt(p3, "ps_aT1", [128, 512], BF16)]
        ps_y = [psmt(p3, "ps_y0", [128, 64], F32), psmt(p3, "ps_y1", [128, 64], F32)]
        ps_bt = psmt(p3, "ps_bt", [64, 128], BF16)
        ps_yh = psmt(p3, "ps_yh", [128, 128], F32)

        barrier()

        for j in range(NT_OWN):
            ext = j + 1

            @block.sync
            def _(s_, j=j):
                need(s_, "mask_sb")
                s_.dma_start(mask_sb[:, :], maskd[:, j * 512:(j + 1) * 512]).then_inc(sems["l"], 16)
            cnt["l"] += 16
            MASK_L = cnt["l"]

            for hp in range(H // 2):
                strm = []
                for st_ in (0, 1):
                    h = 2 * hp + st_
                    strm.append({"h": h, "hpart": (h % 2) * 64,
                                 "hcol": (h // 2) * 1024 + j * 128,
                                 "sc": scores2[:, st_ * 4096:(st_ + 1) * 4096],
                                 "nmo": st_ * 16})

                for c in range(ext):
                    last = (c == j)
                    for st_ in (0, 1):
                        d = strm[st_]

                        @block.tensor
                        def _(t, d=d, c=c, st_=st_):
                            need(t, ("ps_s", st_))
                            need(t, ("scores", st_))
                            inc("t", t.matmul(ps_s[st_][:, :],
                                              qtl[d["hpart"]:d["hpart"] + 64, d["hcol"]:d["hcol"] + 128],
                                              klatT[d["hpart"]:d["hpart"] + 64, c * 512:(c + 1) * 512],
                                              start=True, stop=True))
                        d["mmv"] = cnt["t"]

                        @block.vector
                        def _(v, d=d, c=c, st_=st_, last=last, MASK_L=MASK_L):
                            v.wait_ge(sems["t"], d["mmv"])
                            dst = d["sc"][:, c * 512:(c + 1) * 512]
                            if last:
                                v.wait_ge(sems["l"], MASK_L)
                                inc("v", v.tensor_add(dst, ps_s[st_][:, :], mask_sb[:, :]))
                            else:
                                inc("v", v.tensor_copy(dst, ps_s[st_][:, :]))
                        d["evv"] = cnt["v"]
                        guards[("ps_s", st_)] = ("v", cnt["v"])

                for st_ in (0, 1):
                    d = strm[st_]
                    nmo = d["nmo"]

                    @block.vector
                    def _(v, d=d, nmo=nmo, ext=ext):
                        v.wait_ge(sems["v"], d["evv"])
                        need(v, ("nmg", d["nmo"]))
                        vseq(v, v.reduce_max(nm2[:, nmo:nmo + 1], d["sc"][:, :ext * 512],
                                             axis=AX, negate=True))
                        inc("v", v.tensor_scalar_mul(nm2[:, nmo + 1:nmo + 2], nm2[:, nmo:nmo + 1], SCALE))
                    d["biasv"] = cnt["v"]

                for c in range(ext):
                    for st_ in (0, 1):
                        d = strm[st_]
                        nmo = d["nmo"]

                        @block.scalar
                        def _(sc_, d=d, c=c, st_=st_, nmo=nmo):
                            sc_.wait_ge(sems["v"], d["biasv"])
                            need(sc_, ("att", st_))
                            inc("s", sc_.activation(att[st_][:, :], d["sc"][:, c * 512:(c + 1) * 512],
                                                    AF.Exp, scale=SCALE, bias=nm2[:, nmo + 1:nmo + 2],
                                                    accum_out=nm2[:, nmo + 8 + c:nmo + 9 + c]))
                        d["expv"] = cnt["s"]

                        @block.tensor
                        def _(t, d=d, st_=st_):
                            t.wait_ge(sems["s"], d["expv"])
                            need(t, ("ps_aT", st_))
                            ins = None
                            for kk in range(4):
                                ins = t.transpose(ps_aT[st_][:, kk * 128:(kk + 1) * 128],
                                                  att[st_][:, kk * 128:(kk + 1) * 128], eyeb[:, :])
                            inc("t", ins)
                        d["aTv"] = cnt["t"]
                        guards[("att", st_)] = ("t", cnt["t"])

                        @block.vector
                        def _(v, d=d, st_=st_):
                            v.wait_ge(sems["t"], d["aTv"])
                            need(v, ("attT", st_))
                            inc("v", v.tensor_copy(attT[st_][:, :], ps_aT[st_][:, :]))
                        d["evv2"] = cnt["v"]
                        guards[("ps_aT", st_)] = ("v", cnt["v"])

                        @block.tensor
                        def _(t, d=d, st_=st_, c=c, j=j):
                            t.wait_ge(sems["v"], d["evv2"])
                            if c == 0:
                                need(t, ("ps_y", st_))
                            ins = None
                            for kk in range(4):
                                ins = t.matmul(ps_y[st_][:, :], attT[st_][:, kk * 128:(kk + 1) * 128],
                                               vlat[:, (4 * c + kk) * 64:(4 * c + kk + 1) * 64],
                                               start=(c == 0 and kk == 0), stop=(c == j and kk == 3))
                            inc("t", ins)
                        d["ytv"] = cnt["t"]
                        guards[("attT", st_)] = ("t", cnt["t"])
                    if c == j:
                        for st_ in (0, 1):
                            guards[("scores", st_)] = ("s", strm[st_]["expv"])

                for st_ in (0, 1):
                    d = strm[st_]
                    nmo = d["nmo"]

                    @block.vector
                    def _(v, d=d, st_=st_, nmo=nmo, ext=ext):
                        v.wait_ge(sems["s"], d["expv"])
                        v.wait_ge(sems["t"], d["ytv"])
                        vseq(v, v.reduce_sum(nm2[:, nmo + 2:nmo + 3], nm2[:, nmo + 8:nmo + 8 + ext], axis=AX))
                        vseq(v, v.reciprocal(nm2[:, nmo + 3:nmo + 4], nm2[:, nmo + 2:nmo + 3]))
                        inc("v", v.tensor_scalar(ybarb[:, st_ * 64:(st_ + 1) * 64], ps_y[st_][:, :],
                                                 nm2[:, nmo + 3:nmo + 4], None, ALU.mult))
                    d["tl1"] = cnt["v"]
                    guards[("ps_y", st_)] = ("v", cnt["v"])
                    guards[("nmg", nmo)] = ("v", cnt["v"])

                    @block.tensor
                    def _(t, d=d, st_=st_):
                        t.wait_ge(sems["v"], d["tl1"])
                        need(t, "ps_bt")
                        inc("t", t.transpose(ps_bt[:, :], ybarb[:, st_ * 64:(st_ + 1) * 64], eyeb[:, :]))
                    d["tl2"] = cnt["t"]

                    @block.vector
                    def _(v, d=d):
                        v.wait_ge(sems["t"], d["tl2"])
                        inc("v", v.tensor_copy(ybT_sb[:, :], ps_bt[:, :]))
                    d["tl3"] = cnt["v"]
                    guards["ps_bt"] = ("v", cnt["v"])

                    @block.tensor
                    def _(t, d=d):
                        t.wait_ge(sems["v"], d["tl3"])
                        need(t, "ps_yh")
                        inc("t", t.matmul(ps_yh[:, :], vexpb[:, d["h"] * 128:(d["h"] + 1) * 128],
                                          ybT_sb[:, :], start=True, stop=True))
                    d["tl4"] = cnt["t"]

                    @block.vector
                    def _(v, d=d, j=j):
                        v.wait_ge(sems["t"], d["tl4"])
                        inc("v", v.tensor_copy(big[:, d["h"] * 1024 + j * 128: d["h"] * 1024 + (j + 1) * 128],
                                               ps_yh[:, :]))
                    guards["ps_yh"] = ("v", cnt["v"])
            set_guard("mask_sb", "v")

        p3.close()
        s3.close()
        open_stacks.remove(p3)
        open_stacks.remove(s3)
        barrier()
        s3p = ExitStack()
        open_stacks.append(s3p)
        pslot = sb(s3p, "pslot", [128, FC * 512], BF16)
        pso = [psmt(s3p, "pso0", [128, 512], F32), psmt(s3p, "pso1", [128, 512], F32)]
        for ccp in range(4):
            @block.gpsimd
            def _(g, ccp=ccp):
                need(g, "pslot")
                g.dma_start(pslot[:, :], proj_d[:, ccp * 8192:(ccp + 1) * 8192]).then_inc(sems["w"], 16)
            cnt["w"] += 16
            PW = cnt["w"]

            for j in range(NT_OWN):
                @block.tensor
                def _(t, j=j, PW=PW):
                    t.wait_ge(sems["w"], PW)
                    wait(t, "v")
                    need(t, ("pso", j % 2))
                    ins = None
                    for hh in range(H):
                        ins = t.matmul(pso[j % 2][:, :],
                                       big[:, hh * 1024 + j * 128: hh * 1024 + (j + 1) * 128],
                                       pslot[:, hh * 512:(hh + 1) * 512],
                                       start=(hh == 0), stop=(hh == H - 1))
                    inc("t", ins)

                @block.vector
                def _(v, j=j, ccp=ccp):
                    wait(v, "t")
                    inc("v", v.tensor_copy(accb[:, j * CD + ccp * 512: j * CD + (ccp + 1) * 512],
                                           pso[j % 2][:, :]))
                set_guard(("pso", j % 2), "v")
            set_guard("pslot", "t")

    if stage >= 3:
        s3p.close()
        ast.close()
        for st_ in (s3p, ast):
            open_stacks.remove(st_)
        s4 = ExitStack()
        open_stacks.append(s4)
        ws = sb(s4, "ws", [128, 3 * 8192], BF16)
        bbg = sb(s4, "bbg", [128, 2 * 512], F32)
        bbu = sb(s4, "bbu", [128, 2 * 512], F32)
        ga = sb(s4, "ga", [128, 512], F32)
        gs = sb(s4, "gs", [128, 512], F32)
        gs2 = sb(s4, "gs2", [128, 512], F32)
        ua = sb(s4, "ua", [128, 512], F32)
        actw = [sb(s4, "actw0", [128, 512], BF16), sb(s4, "actw1", [128, 512], BF16)]
        dbias_sb = sb(s4, "dbias_sb", [16, CD], BF16)

        p4 = ExitStack()
        open_stacks.append(p4)
        pg = [psmt(p4, "pg0", [128, 512], F32), psmt(p4, "pg1", [128, 512], F32)]
        pu = [psmt(p4, "pu0", [128, 512], F32), psmt(p4, "pu1", [128, 512], F32)]
        pat = [psmt(p4, "pat0", [128, 512], BF16), psmt(p4, "pat1", [128, 512], BF16)]
        pd = [psmt(p4, "pd0", [128, 512], F32), psmt(p4, "pd1", [128, 512], F32)]

        barrier()

        @block.gpsimd
        def _(g):
            g.dma_start(dbias_sb[:, :], dbias_d).then_inc(sems["w"], 16)
        cnt["w"] += 16

        slot_i = [0]

        def load_chunk(src_ap):
            si = slot_i[0] % 3
            slot_i[0] += 1

            @block.gpsimd
            def _(g, si=si):
                need(g, ("ws", si))
                g.dma_start(ws[:, si * 8192:(si + 1) * 8192], src_ap).then_inc(sems["w"], 16)
            cnt["w"] += 16
            return si, cnt["w"]

        for e in range(16):
            for dc in range(4):
                sg, _wn = load_chunk(wg_d[e, :, dc * 8192:(dc + 1) * 8192])
                su, wu_n = load_chunk(wu_d[e, :, dc * 8192:(dc + 1) * 8192])
                dcp = dc % 2

                @block.sync
                def _(s, e=e, dc=dc, dcp=dcp):
                    need(s, ("bb", dcp))
                    s.dma_start(bbg[:, dcp * 512:(dcp + 1) * 512],
                                bass.AP(biasg_d.tensor, e * CD + dc * 512, [[0, 128], [1, 512]])).then_inc(sems["l"], 16)
                    s.dma_start(bbu[:, dcp * 512:(dcp + 1) * 512],
                                bass.AP(biasu_d.tensor, e * CD + dc * 512, [[0, 128], [1, 512]])).then_inc(sems["l"], 16)
                cnt["l"] += 32
                BB_L = cnt["l"]

                def emit_aT(t, t8p):
                    ti = t8p % 2
                    need(t, ("pat", ti))
                    ins = None
                    for db in range(4):
                        ins = t.transpose(pat[ti][:, db * 128:(db + 1) * 128],
                                          actw[ti][:, db * 128:(db + 1) * 128], eyeb[:, :])
                    return ins

                def emit_aT_evac(dc, t8p):
                    ti = t8p % 2

                    @block.vector
                    def _(v, dc=dc, t8p=t8p, ti=ti):
                        wait(v, "t")
                        bv = big[:, :].rearrange("p (D jt t) -> p D jt t", D=FC, jt=8)
                        inc("v", v.tensor_copy(bv[:, dc * 4:(dc + 1) * 4, t8p, :],
                                               pat[ti][:, :].rearrange("p (db t) -> p db t", db=4)))
                    set_guard(("pat", ti), "v")
                    set_guard(("actw", ti), "v")

                for t8 in range(8):
                    ti = t8 % 2

                    @block.tensor
                    def _(t, t8=t8, ti=ti, sg=sg, su=su, wu_n=wu_n):
                        t.wait_ge(sems["w"], wu_n)
                        need(t, ("pg", ti))
                        need(t, ("pu", ti))
                        for fc in range(FC):
                            t.matmul(pg[ti][:, :],
                                     xnT[:, fc * 1024 + t8 * 128: fc * 1024 + (t8 + 1) * 128],
                                     ws[:, sg * 8192 + fc * 512: sg * 8192 + (fc + 1) * 512],
                                     start=(fc == 0), stop=(fc == FC - 1))
                        ins = None
                        for fc in range(FC):
                            ins = t.matmul(pu[ti][:, :],
                                           xnT[:, fc * 1024 + t8 * 128: fc * 1024 + (t8 + 1) * 128],
                                           ws[:, su * 8192 + fc * 512: su * 8192 + (fc + 1) * 512],
                                           start=(fc == 0), stop=(fc == FC - 1))
                        if t8 > 0:
                            ins = emit_aT(t, t8 - 1)
                        inc("t", ins)
                    if t8 > 0:
                        emit_aT_evac(dc, t8 - 1)
                    if t8 == 7:
                        set_guard(("ws", sg), "t")
                        set_guard(("ws", su), "t")

                    @block.vector
                    def _(v, ti=ti, dcp=dcp, BB_L=BB_L):
                        wait(v, "t")
                        v.wait_ge(sems["l"], BB_L)
                        need(v, ("actw", ti))
                        vseq(v, v.tensor_add(ga[:, :], pg[ti][:, :], bbg[:, dcp * 512:(dcp + 1) * 512]))
                        inc("v", v.tensor_scalar_min(ga[:, :], ga[:, :], LIMIT))
                    set_guard(("pg", ti), "v")

                    @block.scalar
                    def _(sc, e=e, t8=t8):
                        wait(sc, "v")
                        inc("s", sc.activation(gs[:, :], ga[:, :], AF.Silu, scale=KAPPA))

                    @block.vector
                    def _(v, ti=ti, dcp=dcp, e=e, t8=t8):
                        wait(v, "s")
                        v.tensor_scalar(gs2[:, :], gs[:, :],
                                        scoresP[:, t8 * 16 + e:t8 * 16 + e + 1], None, ALU.mult)
                        v.tensor_add(ua[:, :], pu[ti][:, :], bbu[:, dcp * 512:(dcp + 1) * 512])
                        vseq(v, v.tensor_scalar(ua[:, :], ua[:, :], LIMIT, -LIMIT, ALU.min, ALU.max))
                        inc("v", v.scalar_tensor_tensor(actw[ti][:, :], ua[:, :], 1.0, gs2[:, :],
                                                        ALU.add, ALU.mult))
                    set_guard(("pu", ti), "v")
                    if t8 == 7:
                        set_guard(("bb", dcp), "v")

                @block.tensor
                def _(t):
                    wait(t, "v")
                    inc("t", emit_aT(t, 7))
                emit_aT_evac(dc, 7)

            ACT_DONE_V = cnt["v"]
            for hc in range(4):
                sd, wd_n = load_chunk(down_d[e, :, hc * 8192:(hc + 1) * 8192])
                for t8 in range(8):
                    di = (hc * 8 + t8) % 2

                    @block.tensor
                    def _(t, t8=t8, di=di, sd=sd, wd_n=wd_n, ACT_DONE_V=ACT_DONE_V):
                        t.wait_ge(sems["w"], wd_n)
                        t.wait_ge(sems["v"], ACT_DONE_V)
                        need(t, ("pd", di))
                        ins = None
                        for D in range(FC):
                            ins = t.matmul(pd[di][:, :],
                                           big[:, D * 1024 + t8 * 128: D * 1024 + (t8 + 1) * 128],
                                           ws[:, sd * 8192 + D * 512: sd * 8192 + (D + 1) * 512],
                                           start=(D == 0), stop=(D == FC - 1))
                        inc("t", ins)
                    if t8 == 7:
                        set_guard(("ws", sd), "t")

                    @block.vector
                    def _(v, t8=t8, hc=hc, di=di):
                        wait(v, "t")
                        dst = accb[:, t8 * CD + hc * 512: t8 * CD + (hc + 1) * 512]
                        inc("v", v.tensor_add(dst, dst, pd[di][:, :]))
                    set_guard(("pd", di), "v")

        for t8 in range(8):
            for hc in range(4):
                di = (t8 * 4 + hc) % 2

                @block.tensor
                def _(t, t8=t8, hc=hc, di=di):
                    wait(t, "w", "v")
                    need(t, ("pd", di))
                    inc("t", t.matmul(pd[di][:, :], scoresT[:, t8 * 128:(t8 + 1) * 128],
                                      dbias_sb[:, hc * 512:(hc + 1) * 512], start=True, stop=True))

                @block.vector
                def _(v, t8=t8, hc=hc, di=di):
                    wait(v, "t")
                    dst = accb[:, t8 * CD + hc * 512: t8 * CD + (hc + 1) * 512]
                    inc("v", v.tensor_add(dst, dst, pd[di][:, :]))
                set_guard(("pd", di), "v")

    if stage >= 3:
        p4.close()
        s4.close()
        open_stacks.remove(p4)
        open_stacks.remove(s4)
        barrier()
        s5 = ExitStack()
        open_stacks.append(s5)
        xf = sb(s5, "xf", [128, 2 * CD], F32)
        ob = sb(s5, "ob", [128, 2 * CD], F32)

        for j in range(NT_OWN):
            half = j % 2

            @block.sync
            def _(s, j=j, half=half):
                need(s, ("xf", half))
                s.dma_start(xf[:, half * CD:(half + 1) * CD],
                            x_own[j * 128:(j + 1) * 128, :]).then_inc(sems["l"], 16)
            cnt["l"] += 16

            @block.vector
            def _(v, j=j, half=half):
                wait(v, "l")
                need(v, ("ob", half))
                inc("v", v.tensor_add(ob[:, half * CD:(half + 1) * CD],
                                      accb[:, j * CD:(j + 1) * CD],
                                      xf[:, half * CD:(half + 1) * CD]))
            set_guard(("xf", half), "v")

            @block.sync
            def _(s, j=j, half=half):
                wait(s, "v")
                s.dma_start(out_d[j * 128:(j + 1) * 128, :],
                            ob[:, half * CD:(half + 1) * CD]).then_inc(sems["o"], 16)
            cnt["o"] += 16
            set_guard(("ob", half), "o")

        @block.sync
        def _(s):
            s.wait_ge(sems["o"], cnt["o"])

    if dbg:
        barrier()
        avail = {}
        if stage < 3:
            avail.update({
                "d_klatT": (klatT[:, :], True), "d_vlat": (vlat[:, :], True),
                "d_qtl": (qtl[:, :], True),
            })
        avail.update({
            "d_xnT": (xnT[:, :], True),
            "d_scoresP": (scoresP[:, :], False),
            "d_scoresT": (scoresT[:, :], True),
            "d_acc": (accb[:, :], False),
            "d_big": (big[:, :], True),
        })
        for name in dbg:
            if name not in avail:
                continue
            src_ap, cast = avail[name]
            if cast:
                @block.gpsimd
                def _(g, name=name, src_ap=src_ap):
                    g.dma_start(dbg_d[name], src_ap).then_inc(sems["w"], 16)
                cnt["w"] += 16
            else:
                @block.sync
                def _(s, name=name, src_ap=src_ap):
                    s.dma_start(dbg_d[name], src_ap).then_inc(sems["l"], 16)
                cnt["l"] += 16

        @block.sync
        def _(s):
            s.wait_ge(sems["l"], cnt["l"])

        @block.gpsimd
        def _(g):
            g.wait_ge(sems["w"], cnt["w"])

    for st_ in reversed(open_stacks):
        st_.close()
    return nc


def tile_w(w):
    n = w.shape[1]
    a = w.reshape(16, 128, n // 512, 512)
    a = a.transpose(1, 2, 0, 3)
    return np.ascontiguousarray(a.reshape(128, -1))


def prep_inputs(ii):
    x = np.asarray(ii["L0_attn_x"], np.float32)
    wq = np.asarray(ii["L0_attn_wq"], np.float32)
    wk = np.asarray(ii["L0_attn_wk_lat"], np.float32)
    wv = np.asarray(ii["L0_attn_wv_lat"], np.float32)
    kexpand = np.asarray(ii["L0_attn_k_expand"], np.float32)
    vexpand = np.asarray(ii["L0_attn_v_expand"], np.float32)
    projw = np.asarray(ii["L0_attn_proj_w"], np.float32)
    rw = np.asarray(ii["L0_ffn_router_weight"], np.float32)
    rb = np.asarray(ii["L0_ffn_router_bias"], np.float32)
    gup = np.asarray(ii["L0_ffn_gate_up_proj"], np.float32)
    gupb = np.asarray(ii["L0_ffn_gate_up_proj_bias"], np.float32)
    dw = np.asarray(ii["L0_ffn_down_proj"], np.float32)
    db = np.asarray(ii["L0_ffn_down_proj_bias"], np.float32)

    P128 = np.concatenate([np.arange(0, 128, 2), np.arange(1, 128, 2)])
    P64 = np.concatenate([np.arange(0, 64, 2), np.arange(1, 64, 2)])

    shared = {
        "jrow": np.broadcast_to(np.arange(64, dtype=np.float32), (128, 64)).copy(),
        "eye_f": np.eye(128, dtype=np.float32),
        "pos_ctx": (np.arange(32, dtype=np.float32)[None, :] * 128
                    + np.arange(128, dtype=np.float32)[:, None]).astype(np.float32),
        "wq_t": tile_w(wq),
        "kvw_t": np.ascontiguousarray(np.concatenate([wk, wv], axis=1).reshape(16, 128, 128).transpose(1, 0, 2).reshape(128, 2048)),
        "kexp_t": np.ascontiguousarray(
            kexpand.transpose(0, 2, 1)[:, P128][:, :, P64].transpose(1, 0, 2).reshape(128, 16 * 64)),
        "vexp_t": np.ascontiguousarray(vexpand.transpose(1, 0, 2).reshape(64, 16 * 128)),
        "proj_t": tile_w(projw),
        "rwT_t": np.ascontiguousarray(rw.T.reshape(16, 128, 16).transpose(1, 0, 2).reshape(128, 256)),
        "rbias": rb.reshape(1, 16).copy(),
        "wg_t": np.stack([tile_w(np.ascontiguousarray(gup[e][:, 0::2])) for e in range(16)]),
        "wu_t": np.stack([tile_w(np.ascontiguousarray(gup[e][:, 1::2])) for e in range(16)]),
        "biasg": np.ascontiguousarray(gupb[:, 0::2]),
        "biasu": np.ascontiguousarray(gupb[:, 1::2]),
        "down_t": np.stack([tile_w(dw[e]) for e in range(16)]),
        "dbias": db.copy(),
    }

    in_maps = []
    own_tiles = []
    for c in range(8):
        b, cp = c // 4, c % 4
        tiles = [4 * jj + cp for jj in range(8)]
        own_tiles.append(tiles)
        rows = np.concatenate([np.arange(g * 128, (g + 1) * 128) for g in tiles])
        mask = np.zeros((128, 4096), np.float32)
        for jj, g in enumerate(tiles):
            gq = g * 128 + np.arange(128)[:, None]
            gk = jj * 512 + np.arange(512)[None, :]
            mask[:, jj * 512:(jj + 1) * 512] = np.where(gk <= gq, 0.0, NEG)
        posn = (np.array([g * 128 for g in tiles], np.float32)[None, :]
                + np.arange(128, dtype=np.float32)[:, None])
        m = dict(shared)
        m["x_ctx"] = np.ascontiguousarray(x[b])
        m["x_own"] = np.ascontiguousarray(x[b][rows])
        m["maskd"] = mask
        m["pos_own"] = posn.astype(np.float32)
        in_maps.append(m)
    return in_maps, own_tiles


_CACHE = {}


def kernel(**inputs):
    in_maps, own_tiles = prep_inputs(inputs)
    if "nc" not in _CACHE:
        _CACHE["nc"] = build(stage=3, dbg=())
    res = run_bass_kernel_spmd(_CACHE["nc"], in_maps, core_ids=list(range(8)))
    out = np.zeros((2, 4096, 2048), np.float32)
    for c in range(8):
        b = c // 4
        o = res.results[c]["out"].reshape(8, 128, 2048)
        for jj, g in enumerate(own_tiles[c]):
            out[b, g * 128:(g + 1) * 128, :] = o[jj]
    return out


# version 04
# speedup vs baseline: 1.0033x; 1.0033x over previous
import math
import numpy as np
from contextlib import ExitStack

import concourse.bass as bass
import concourse.mybir as mybir
from concourse.bass_utils import run_bass_kernel_spmd

F32 = mybir.dt.float32
BF16 = mybir.dt.bfloat16
I32 = mybir.dt.int32
ALU = mybir.AluOpType
AF = mybir.ActivationFunctionType
AX = mybir.AxisListType.X

PI = math.pi
LN1E4 = math.log(10000.0)
SCALE = 1.0 / math.sqrt(128.0)
KAPPA = 1.702
LIMIT = 7.0
NEG = -1.0e30
EPS = 1e-5

NT_CTX, NT_OWN, FC, H, CD = 32, 8, 16, 16, 2048


def build(stage=3, dbg=()):
    nc = bass.Bass("TRN2", target_bir_lowering=False)

    x_ctx = nc.dram_tensor("x_ctx", [4096, CD], F32, kind="ExternalInput").ap()
    x_own = nc.dram_tensor("x_own", [1024, CD], F32, kind="ExternalInput").ap()
    maskd = nc.dram_tensor("maskd", [128, 4096], F32, kind="ExternalInput").ap()
    pos_own = nc.dram_tensor("pos_own", [128, 8], F32, kind="ExternalInput").ap()
    pos_ctx = nc.dram_tensor("pos_ctx", [128, 32], F32, kind="ExternalInput").ap()
    jrow = nc.dram_tensor("jrow", [128, 64], F32, kind="ExternalInput").ap()
    eye_f = nc.dram_tensor("eye_f", [128, 128], F32, kind="ExternalInput").ap()
    wq_d = nc.dram_tensor("wq_t", [128, FC * CD], F32, kind="ExternalInput").ap()
    kvw_d = nc.dram_tensor("kvw_t", [128, FC * 128], F32, kind="ExternalInput").ap()
    kexp_d = nc.dram_tensor("kexp_t", [128, H * 64], F32, kind="ExternalInput").ap()
    vexp_d = nc.dram_tensor("vexp_t", [64, H * 128], F32, kind="ExternalInput").ap()
    proj_d = nc.dram_tensor("proj_t", [128, 4 * FC * 512], F32, kind="ExternalInput").ap()
    rwT_d = nc.dram_tensor("rwT_t", [128, FC * 16], F32, kind="ExternalInput").ap()
    rbias_d = nc.dram_tensor("rbias", [1, 16], F32, kind="ExternalInput").ap()
    wg_d = nc.dram_tensor("wg_t", [16, 128, 4 * FC * 512], F32, kind="ExternalInput").ap()
    wu_d = nc.dram_tensor("wu_t", [16, 128, 4 * FC * 512], F32, kind="ExternalInput").ap()
    biasg_d = nc.dram_tensor("biasg", [16, CD], F32, kind="ExternalInput").ap()
    biasu_d = nc.dram_tensor("biasu", [16, CD], F32, kind="ExternalInput").ap()
    down_d = nc.dram_tensor("down_t", [16, 128, 4 * FC * 512], F32, kind="ExternalInput").ap()
    dbias_d = nc.dram_tensor("dbias", [16, CD], F32, kind="ExternalInput").ap()
    out_d = nc.dram_tensor("out", [1024, CD], F32, kind="ExternalOutput").ap()

    dbg_d = {}
    for name, shape in [
        ("d_klatT", [64, 4096]), ("d_vlat", [128, NT_CTX * 64]),
        ("d_xnT", [128, FC * 1024]), ("d_qtl", [128, 8 * 1024]),
        ("d_scoresP", [128, 128]), ("d_scoresT", [16, 1024]),
        ("d_acc", [128, NT_OWN * CD]), ("d_big", [128, FC * 1024]),
    ]:
        if name in dbg:
            dbg_d[name] = nc.dram_tensor(name, shape, F32, kind="ExternalOutput").ap()

    gst = ExitStack()
    open_stacks = [gst]

    def sb(stk, name, shape, dt):
        return stk.enter_context(nc.sbuf_tensor(name, shape, dt))

    def psmt(stk, name, shape, dt):
        return stk.enter_context(nc.psum_tensor(name, shape, dt))

    xnT = sb(gst, "xnT", [128, FC * 1024], BF16)
    accb = sb(gst, "accb", [128, NT_OWN * CD], F32)
    big = sb(gst, "big", [128, FC * 1024], BF16)
    eyeb = sb(gst, "eyeb", [128, 128], BF16)
    eyef = sb(gst, "eyef", [128, 128], F32)
    scoresP = sb(gst, "scoresP", [128, 128], F32)
    scoresR = sb(gst, "scoresR", [128, 16], F32)
    scoresT = sb(gst, "scoresT", [16, 1024], BF16)
    rbias_bc = sb(gst, "rbias_bc", [128, 16], F32)

    sems = {}
    for k in ("l", "w", "o", "t", "v", "s"):
        sems[k] = gst.enter_context(nc.semaphore("sem_" + k))
    cnt = {k: 0 for k in sems}

    block = gst.enter_context(nc.Block())

    def inc(key, ins, k=1):
        cnt[key] += k
        ins.then_inc(sems[key], k)
        return ins

    guards = {}

    def need(e, key):
        if key in guards:
            k, v = guards[key]
            if v > 0:
                e.wait_ge(sems[k], v)

    def set_guard(key, k):
        guards[key] = (k, cnt[k])

    def vseq(v, ins):
        inc("v", ins)
        v.wait_ge(sems["v"], cnt["v"])
        return ins

    def wait(e, *keys):
        for k in keys:
            if cnt[k] > 0:
                e.wait_ge(sems[k], cnt[k])

    def barrier():
        vals = dict(cnt)

        def wa(e):
            for k, v in vals.items():
                if v > 0:
                    e.wait_ge(sems[k], v)
        for name in ("sync", "gpsimd", "tensor", "vector", "scalar"):
            getattr(block, name)(wa)

    ast = ExitStack()
    open_stacks.append(ast)
    klatT = sb(ast, "klatT", [128, 4096], BF16)
    vlat = sb(ast, "vlat", [128, NT_CTX * 64], BF16)
    qtl = sb(ast, "qtl", [128, 8 * 1024], BF16)
    kexp = sb(ast, "kexp", [128, H * 64], BF16)
    vexpb = sb(ast, "vexpb", [64, H * 128], BF16)


    s12 = ExitStack()
    open_stacks.append(s12)
    kvw = sb(s12, "kvw", [128, FC * 128], BF16)
    rwT = sb(s12, "rwT", [128, FC * 16], BF16)
    invf = sb(s12, "invf", [128, 64], F32)
    invfk = sb(s12, "invfk", [128, 32], F32)
    jrow_sb = sb(s12, "jrow_sb", [128, 64], F32)
    poso = sb(s12, "poso", [128, 8], F32)
    posc = sb(s12, "posc", [128, 32], F32)
    red = sb(s12, "red", [128, 64], F32)
    kib = sb(s12, "kib", [128, 64], I32)
    cosqa = sb(s12, "cosqa", [128, 512], F32)
    sinqa = sb(s12, "sinqa", [128, 512], F32)
    cosk = sb(s12, "cosk", [128, 32], F32)
    sink = sb(s12, "sink", [128, 32], F32)
    lnst = sb(s12, "lnst", [128, 16], F32)
    tA = sb(s12, "tA", [128, 128], F32)
    tB = sb(s12, "tB", [128, 128], F32)
    xt = sb(s12, "xt", [128, 2 * CD], F32)
    xnt = sb(s12, "xnt", [128, CD], BF16)
    xnTs = sb(s12, "xnTs", [128, CD], BF16)
    qro4 = sb(s12, "qro4", [128, 512], BF16)
    qT4 = sb(s12, "qT4", [128, 512], BF16)
    kro = sb(s12, "kro", [128, 64], BF16)


    @block.sync
    def _(s):
        s.dma_start(eyef[:, :], eye_f).then_inc(sems["l"], 16)
        s.dma_start(jrow_sb[:, :], jrow).then_inc(sems["l"], 16)
        s.dma_start(poso[:, :], pos_own).then_inc(sems["l"], 16)
        s.dma_start(posc[:, :], pos_ctx).then_inc(sems["l"], 16)
        s.dma_start(rbias_bc[:, :], bass.AP(rbias_d.tensor, 0, [[0, 128], [1, 16]])).then_inc(sems["l"], 16)
    cnt["l"] += 16 * 5
    P0_L = cnt["l"]

    @block.scalar
    def _(sc):
        sc.wait_ge(sems["l"], P0_L)
        sc.activation(invf[:, :], jrow_sb[:, :], AF.Exp, scale=-LN1E4 / 64.0)
        inc("s", sc.activation(invfk[:, :], jrow_sb[:, :32], AF.Exp, scale=-LN1E4 / 32.0))

    def sincos(pos_col, invf_ap, n, cos_dst, sin_dst):
        @block.vector
        def _(v):
            wait(v, "s", "l")
            vseq(v, v.tensor_scalar(tA[:, :n], invf_ap, pos_col, None, ALU.mult))
            vseq(v, v.tensor_scalar_mul(red[:, :n], tA[:, :n], 1.0 / (2 * PI)))
            inc("v", v.tensor_copy(kib[:, :n], red[:, :n]))

        @block.scalar
        def _(sc):
            wait(sc, "v")
            inc("s", sc.copy(red[:, :n], kib[:, :n]))

        @block.vector
        def _(v):
            wait(v, "s")
            inc("v", v.scalar_tensor_tensor(red[:, :n], red[:, :n], -2 * PI, tA[:, :n], ALU.mult, ALU.add))

        @block.scalar
        def _(sc):
            wait(sc, "v")
            inc("s", sc.activation(sin_dst, red[:, :n], AF.Sin))

        @block.vector
        def _(v):
            wait(v, "s")
            vseq(v, v.tensor_scalar_add(tB[:, :n], tA[:, :n], PI / 2))
            vseq(v, v.tensor_scalar_mul(red[:, :n], tB[:, :n], 1.0 / (2 * PI)))
            inc("v", v.tensor_copy(kib[:, :n], red[:, :n]))

        @block.scalar
        def _(sc):
            wait(sc, "v")
            inc("s", sc.copy(red[:, :n], kib[:, :n]))

        @block.vector
        def _(v):
            wait(v, "s")
            inc("v", v.scalar_tensor_tensor(red[:, :n], red[:, :n], -2 * PI, tB[:, :n], ALU.mult, ALU.add))

        @block.scalar
        def _(sc):
            wait(sc, "v")
            inc("s", sc.activation(cos_dst, red[:, :n], AF.Sin))

    @block.gpsimd
    def _(g):
        g.dma_start(eyeb[:, :], eye_f).then_inc(sems["w"], 16)
        g.dma_start(kvw[:, :], kvw_d).then_inc(sems["w"], 16)
        g.dma_start(kexp[:, :], kexp_d).then_inc(sems["w"], 16)
        g.dma_start(vexpb[:, :], vexp_d).then_inc(sems["w"], 16)
        g.dma_start(rwT[:, :], rwT_d).then_inc(sems["w"], 16)
    cnt["w"] += 16 * 5
    P0_W = cnt["w"]

    p12 = ExitStack()
    open_stacks.append(p12)
    pk0 = psmt(p12, "pk0", [128, 1024], BF16)
    pk1 = psmt(p12, "pk1", [128, 1024], BF16)
    pq0 = psmt(p12, "pq0", [128, 512], F32)
    pq1 = psmt(p12, "pq1", [128, 512], F32)
    pqt = psmt(p12, "pqt", [128, 128], F32)
    prt = psmt(p12, "prt", [128, 128], F32)
    ptrash = psmt(p12, "ptrash", [128, 512], F32)

    def ln_tile(half):
        xv = xt[:, half * CD:(half + 1) * CD]

        @block.vector
        def _(v):
            wait(v, "l")
            inc("v", v.reduce_sum(lnst[:, 0:1], xv, axis=AX))

        @block.scalar
        def _(sc):
            wait(sc, "l")
            ins = None
            for c4 in range(4):
                ins = sc.activation(ptrash[:, :], xv[:, c4 * 512:(c4 + 1) * 512], AF.Square,
                                    accum_out=lnst[:, 8 + c4:9 + c4])
            inc("s", ins)

        @block.vector
        def _(v):
            wait(v, "s", "v")
            vseq(v, v.tensor_scalar_mul(lnst[:, 1:2], lnst[:, 0:1], 1.0 / CD))
            vseq(v, v.reduce_sum(lnst[:, 2:3], lnst[:, 8:12], axis=AX))
            vseq(v, v.tensor_scalar(lnst[:, 3:4], lnst[:, 2:3], 1.0 / CD, EPS, ALU.mult, ALU.add))
            vseq(v, v.tensor_mul(lnst[:, 4:5], lnst[:, 1:2], lnst[:, 1:2]))
            inc("v", v.scalar_tensor_tensor(lnst[:, 5:6], lnst[:, 4:5], -1.0, lnst[:, 3:4],
                                            ALU.mult, ALU.add))

        @block.scalar
        def _(sc):
            wait(sc, "v")
            sc.activation(lnst[:, 6:7], lnst[:, 5:6], AF.Ln)
            inc("s", sc.activation(lnst[:, 7:8], lnst[:, 6:7], AF.Exp, scale=-0.5))

        @block.vector
        def _(v):
            wait(v, "s")
            inc("v", v.tensor_scalar(xnt[:, :], xv, lnst[:, 1:2], lnst[:, 7:8],
                                     ALU.subtract, ALU.mult))

    def xn_transposes(own_j):
        @block.tensor
        def _(t):
            wait(t, "v")
            t.wait_ge(sems["w"], 16)
            need(t, "pk")
            ins = None
            for fc in range(FC):
                pk = pk0 if fc < 8 else pk1
                ins = t.transpose(pk[:, (fc % 8) * 128:(fc % 8 + 1) * 128],
                                  xnt[:, fc * 128:(fc + 1) * 128], eyeb[:, :])
            inc("t", ins)

        @block.vector
        def _(v):
            wait(v, "t")
            pv0 = pk0[:, :].rearrange("p (f t) -> p f t", f=8)
            pv1 = pk1[:, :].rearrange("p (f t) -> p f t", f=8)
            if own_j is None:
                v.tensor_copy(xnTs[:, 0:1024], pk0[:, :])
                inc("v", v.tensor_copy(xnTs[:, 1024:2048], pk1[:, :]))
            else:
                xv = xnT[:, :].rearrange("p (f jt t) -> p f jt t", f=FC, jt=8)
                v.tensor_copy(xv[:, 0:8, own_j, :], pv0)
                inc("v", v.tensor_copy(xv[:, 8:16, own_j, :], pv1))
        set_guard("pk", "v")

    if stage >= 1:
        for tau in range(NT_CTX):
            half = tau % 2

            @block.sync
            def _(s, tau=tau, half=half):
                need(s, ("xt", half))
                s.dma_start(xt[:, half * CD:(half + 1) * CD],
                            x_ctx[tau * 128:(tau + 1) * 128, :]).then_inc(sems["l"], 16)
            cnt["l"] += 16
            ln_tile(half)
            set_guard(("xt", half), "v")
            xn_transposes(None)

            @block.tensor
            def _(t):
                wait(t, "v")
                t.wait_ge(sems["w"], 32)
                need(t, ("pq", 0))
                ins = None
                for fc in range(FC):
                    ins = t.matmul(pq0[:, 0:128], xnTs[:, fc * 128:(fc + 1) * 128],
                                   kvw[:, fc * 128:(fc + 1) * 128],
                                   start=(fc == 0), stop=(fc == FC - 1))
                inc("t", ins)

            sincos(posc[:, tau:tau + 1], invfk[:, :], 32, cosk[:, :], sink[:, :])

            @block.vector
            def _(v, tau=tau):
                wait(v, "t", "s")
                v.tensor_mul(tA[:, 0:32], pq0[:, 0:64:2], cosk[:, :])
                vseq(v, v.tensor_mul(tB[:, 0:32], pq0[:, 1:64:2], sink[:, :]))
                v.tensor_sub(kro[:, 0:32], tA[:, 0:32], tB[:, 0:32])
                v.tensor_mul(tA[:, 32:64], pq0[:, 0:64:2], sink[:, :])
                vseq(v, v.tensor_mul(tB[:, 32:64], pq0[:, 1:64:2], cosk[:, :]))
                v.tensor_add(kro[:, 32:64], tA[:, 32:64], tB[:, 32:64])
                inc("v", v.tensor_copy(vlat[:, tau * 64:(tau + 1) * 64], pq0[:, 64:128]))
            set_guard(("pq", 0), "v")

            @block.tensor
            def _(t):
                wait(t, "v")
                need(t, "pk")
                t.transpose(pk0[0:64, 0:128], kro[:, :], eyeb[:, :])
                inc("t", t.transpose(pk0[64:128, 0:128], kro[:, :], eyeb[:, :],
                                     tile_position=(0, 64)))

            @block.vector
            def _(v, tau=tau):
                wait(v, "t")
                inc("v", v.tensor_copy(klatT[:, tau * 128:(tau + 1) * 128], pk0[:, 0:128]))
            set_guard("pk", "v")

    if stage >= 1:
        for j in range(NT_OWN):
            half = j % 2

            @block.sync
            def _(s, j=j, half=half):
                need(s, ("xt", half))
                s.dma_start(xt[:, half * CD:(half + 1) * CD],
                            x_own[j * 128:(j + 1) * 128, :]).then_inc(sems["l"], 16)
            cnt["l"] += 16
            ln_tile(half)
            set_guard(("xt", half), "v")
            xn_transposes(own_j=j)

            sincos(poso[:, j:j + 1], invf[:, :], 64,
                   cosqa[:, j * 64:(j + 1) * 64], sinqa[:, j * 64:(j + 1) * 64])

            @block.tensor
            def _(t, j=j):
                t.wait_ge(sems["w"], P0_W)
                wait(t, "v")
                need(t, "prt")
                ins = None
                for fc in range(FC):
                    ins = t.matmul(prt[:, 0:16], xnT[:, fc * 1024 + j * 128: fc * 1024 + (j + 1) * 128],
                                   rwT[:, fc * 16:(fc + 1) * 16],
                                   start=(fc == 0), stop=(fc == FC - 1))
                inc("t", ins)

            @block.vector
            def _(v):
                wait(v, "t", "v")
                v.wait_ge(sems["l"], P0_L)
                lg, cur = tA[:, 0:16], tA[:, 16:32]
                nm1, nmk, cmp = tA[:, 32:33], tA[:, 33:34], tA[:, 34:50]
                vseq(v, v.tensor_add(lg, prt[:, 0:16], rbias_bc[:, :]))
                vseq(v, v.reduce_max(nm1, lg, axis=AX, negate=True))
                vseq(v, v.tensor_copy(cur, lg))
                for it in range(3):
                    nm = nm1 if it == 0 else nmk
                    vseq(v, v.tensor_scalar(cmp, cur, nm, 0.0, ALU.add, ALU.is_ge))
                    vseq(v, v.scalar_tensor_tensor(cur, cmp, NEG, cur, ALU.mult, ALU.add))
                    vseq(v, v.reduce_max(nmk, cur, axis=AX, negate=True))
                inc("v", v.tensor_scalar(cmp, lg, nmk, 0.0, ALU.add, ALU.is_ge))

            @block.scalar
            def _(sc):
                wait(sc, "v")
                inc("s", sc.activation(tA[:, 50:66], tA[:, 0:16], AF.Exp, bias=tA[:, 32:33]))

            @block.vector
            def _(v, j=j):
                wait(v, "s")
                ew, ssum, rs = tA[:, 50:66], tA[:, 66:67], tA[:, 67:68]
                vseq(v, v.tensor_mul(ew, ew, tA[:, 34:50]))
                vseq(v, v.reduce_sum(ssum, ew, axis=AX))
                vseq(v, v.reciprocal(rs, ssum))
                vseq(v, v.tensor_scalar(scoresR[:, :], ew, rs, None, ALU.mult))
                inc("v", v.tensor_scalar_mul(scoresP[:, j * 16:(j + 1) * 16], scoresR[:, :],
                                             1.0 / KAPPA))

            @block.tensor
            def _(t):
                wait(t, "v")
                inc("t", t.transpose(prt[0:16, :], scoresR[:, :], eyef[:, :]))

            @block.vector
            def _(v, j=j):
                wait(v, "t")
                inc("v", v.tensor_copy(scoresT[:, j * 128:(j + 1) * 128], prt[0:16, :]))
            set_guard("prt", "v")

    if stage >= 1:
        XNT_DONE_V = cnt["v"]
        for cc in range(4):
            bh = cc % 2

            @block.gpsimd
            def _(g, cc=cc, bh=bh):
                need(g, ("bigq", bh))
                g.dma_start(big[:, bh * 8192:(bh + 1) * 8192],
                            wq_d[:, cc * 8192:(cc + 1) * 8192]).then_inc(sems["w"], 16)
            cnt["w"] += 16
            WQ_W = cnt["w"]

            for j in range(NT_OWN):
                ji = j % 2
                pq = pq0 if ji == 0 else pq1

                @block.tensor
                def _(t, j=j, cc=cc, bh=bh, pq=pq, ji=ji, WQ_W=WQ_W):
                    t.wait_ge(sems["w"], WQ_W)
                    t.wait_ge(sems["v"], XNT_DONE_V)
                    need(t, ("pq", ji))
                    ins = None
                    for fc in range(FC):
                        ins = t.matmul(pq[:, :], xnT[:, fc * 1024 + j * 128: fc * 1024 + (j + 1) * 128],
                                       big[:, bh * 8192 + fc * 512: bh * 8192 + (fc + 1) * 512],
                                       start=(fc == 0), stop=(fc == FC - 1))
                    inc("t", ins)

                @block.vector
                def _(v, j=j, pq=pq):
                    wait(v, "t")
                    need(v, "qro4")
                    cq = cosqa[:, j * 64:(j + 1) * 64]
                    sq = sinqa[:, j * 64:(j + 1) * 64]
                    for hh in range(4):
                        x1 = pq[:, hh * 128 + 0: hh * 128 + 128: 2]
                        x2 = pq[:, hh * 128 + 1: hh * 128 + 128: 2]
                        v.tensor_mul(tA[:, 0:64], x1, cq)
                        vseq(v, v.tensor_mul(tB[:, 0:64], x2, sq))
                        v.tensor_sub(qro4[:, hh * 128: hh * 128 + 64], tA[:, 0:64], tB[:, 0:64])
                        v.tensor_mul(tA[:, 64:128], x1, sq)
                        vseq(v, v.tensor_mul(tB[:, 64:128], x2, cq))
                        v.tensor_add(qro4[:, hh * 128 + 64: hh * 128 + 128], tA[:, 64:128], tB[:, 64:128])
                    inc("v", v.engine_nop())
                set_guard(("pq", ji), "v")

                @block.tensor
                def _(t):
                    wait(t, "v")
                    need(t, "pk")
                    ins = None
                    for hh in range(4):
                        ins = t.transpose(pk0[:, hh * 128:(hh + 1) * 128],
                                          qro4[:, hh * 128:(hh + 1) * 128], eyeb[:, :])
                    inc("t", ins)
                set_guard("qro4", "t")

                @block.vector
                def _(v):
                    wait(v, "t")
                    need(v, "qT4")
                    inc("v", v.tensor_copy(qT4[:, :], pk0[:, 0:512]))
                set_guard("pk", "v")

                for hh in range(4):
                    h = cc * 4 + hh
                    po = h % 2

                    @block.tensor
                    def _(t, h=h, hh=hh, po=po):
                        wait(t, "v")
                        need(t, "pqt")
                        dst = pqt[64:128, :] if po else pqt[0:64, :]
                        inc("t", t.matmul(dst, kexp[:, h * 64:(h + 1) * 64],
                                          qT4[:, hh * 128:(hh + 1) * 128], start=True, stop=True,
                                          tile_position=(0, 64) if po else None))

                    @block.vector
                    def _(v, h=h, j=j, po=po):
                        wait(v, "t")
                        src = pqt[64:128, :] if po else pqt[0:64, :]
                        dst = qtl[po * 64:po * 64 + 64,
                                  (h // 2) * 1024 + j * 128:(h // 2) * 1024 + (j + 1) * 128]
                        inc("v", v.tensor_copy(dst, src))
                    set_guard("pqt", "v")
                set_guard("qT4", "t")
            set_guard(("bigq", bh), "t")

    if stage >= 2:
        p12.close()
        s12.close()
        open_stacks.remove(p12)
        open_stacks.remove(s12)
        s3 = ExitStack()
        open_stacks.append(s3)
        mask_sb = sb(s3, "mask_sb", [128, 512], F32)
        scores2 = sb(s3, "scores2", [128, 2 * 4096], F32)
        att = [sb(s3, "att0", [128, 512], BF16), sb(s3, "att1", [128, 512], BF16)]
        attT = [sb(s3, "attT0", [128, 512], BF16), sb(s3, "attT1", [128, 512], BF16)]
        nm2 = sb(s3, "nm2", [128, 32], F32)
        ybarb = sb(s3, "ybarb", [128, 128], BF16)
        ybT_sb = sb(s3, "ybT_sb", [64, 128], BF16)

        p3 = ExitStack()
        open_stacks.append(p3)
        ps_s = [psmt(p3, "ps_s0", [128, 512], F32), psmt(p3, "ps_s1", [128, 512], F32)]
        ps_aT = [psmt(p3, "ps_aT0", [128, 512], BF16), psmt(p3, "ps_aT1", [128, 512], BF16)]
        ps_y = [psmt(p3, "ps_y0", [128, 64], F32), psmt(p3, "ps_y1", [128, 64], F32)]
        ps_bt = psmt(p3, "ps_bt", [64, 128], BF16)
        ps_yh = psmt(p3, "ps_yh", [128, 128], F32)

        barrier()

        for j in range(NT_OWN):
            ext = j + 1

            @block.sync
            def _(s_, j=j):
                need(s_, "mask_sb")
                s_.dma_start(mask_sb[:, :], maskd[:, j * 512:(j + 1) * 512]).then_inc(sems["l"], 16)
            cnt["l"] += 16
            MASK_L = cnt["l"]

            for hp in range(H // 2):
                strm = []
                for st_ in (0, 1):
                    h = 2 * hp + st_
                    strm.append({"h": h, "hpart": (h % 2) * 64,
                                 "hcol": (h // 2) * 1024 + j * 128,
                                 "sc": scores2[:, st_ * 4096:(st_ + 1) * 4096],
                                 "nmo": st_ * 16})

                for c in range(ext):
                    last = (c == j)
                    for st_ in (0, 1):
                        d = strm[st_]

                        @block.tensor
                        def _(t, d=d, c=c, st_=st_):
                            need(t, ("ps_s", st_))
                            need(t, ("scores", st_))
                            inc("t", t.matmul(ps_s[st_][:, :],
                                              qtl[d["hpart"]:d["hpart"] + 64, d["hcol"]:d["hcol"] + 128],
                                              klatT[d["hpart"]:d["hpart"] + 64, c * 512:(c + 1) * 512],
                                              start=True, stop=True))
                        d["mmv"] = cnt["t"]

                        @block.vector
                        def _(v, d=d, c=c, st_=st_, last=last, MASK_L=MASK_L):
                            v.wait_ge(sems["t"], d["mmv"])
                            dst = d["sc"][:, c * 512:(c + 1) * 512]
                            if last:
                                v.wait_ge(sems["l"], MASK_L)
                                inc("v", v.tensor_add(dst, ps_s[st_][:, :], mask_sb[:, :]))
                            else:
                                inc("v", v.tensor_copy(dst, ps_s[st_][:, :]))
                        d["evv"] = cnt["v"]
                        guards[("ps_s", st_)] = ("v", cnt["v"])

                for st_ in (0, 1):
                    d = strm[st_]
                    nmo = d["nmo"]

                    @block.vector
                    def _(v, d=d, nmo=nmo, ext=ext):
                        v.wait_ge(sems["v"], d["evv"])
                        need(v, ("nmg", d["nmo"]))
                        vseq(v, v.reduce_max(nm2[:, nmo:nmo + 1], d["sc"][:, :ext * 512],
                                             axis=AX, negate=True))
                        inc("v", v.tensor_scalar_mul(nm2[:, nmo + 1:nmo + 2], nm2[:, nmo:nmo + 1], SCALE))
                    d["biasv"] = cnt["v"]

                for c in range(ext):
                    @block.scalar
                    def _(sc_, c=c, strm=strm):
                        for st_ in (0, 1):
                            d = strm[st_]
                            nmo = d["nmo"]
                            sc_.wait_ge(sems["v"], d["biasv"])
                            need(sc_, ("att", st_))
                            inc("s", sc_.activation(att[st_][:, :], d["sc"][:, c * 512:(c + 1) * 512],
                                                    AF.Exp, scale=SCALE, bias=nm2[:, nmo + 1:nmo + 2],
                                                    accum_out=nm2[:, nmo + 8 + c:nmo + 9 + c]))
                            d["expv"] = cnt["s"]
                    for st_ in (0, 1):
                        d = strm[st_]
                        nmo = d["nmo"]

                        @block.tensor
                        def _(t, d=d, st_=st_):
                            t.wait_ge(sems["s"], d["expv"])
                            need(t, ("ps_aT", st_))
                            ins = None
                            for kk in range(4):
                                ins = t.transpose(ps_aT[st_][:, kk * 128:(kk + 1) * 128],
                                                  att[st_][:, kk * 128:(kk + 1) * 128], eyeb[:, :])
                            inc("t", ins)
                        d["aTv"] = cnt["t"]
                        guards[("att", st_)] = ("t", cnt["t"])

                        @block.vector
                        def _(v, d=d, st_=st_):
                            v.wait_ge(sems["t"], d["aTv"])
                            need(v, ("attT", st_))
                            inc("v", v.tensor_copy(attT[st_][:, :], ps_aT[st_][:, :]))
                        d["evv2"] = cnt["v"]
                        guards[("ps_aT", st_)] = ("v", cnt["v"])

                        @block.tensor
                        def _(t, d=d, st_=st_, c=c, j=j):
                            t.wait_ge(sems["v"], d["evv2"])
                            if c == 0:
                                need(t, ("ps_y", st_))
                            ins = None
                            for kk in range(4):
                                ins = t.matmul(ps_y[st_][:, :], attT[st_][:, kk * 128:(kk + 1) * 128],
                                               vlat[:, (4 * c + kk) * 64:(4 * c + kk + 1) * 64],
                                               start=(c == 0 and kk == 0), stop=(c == j and kk == 3))
                            inc("t", ins)
                        d["ytv"] = cnt["t"]
                        guards[("attT", st_)] = ("t", cnt["t"])
                    if c == j:
                        for st_ in (0, 1):
                            guards[("scores", st_)] = ("s", strm[st_]["expv"])

                for st_ in (0, 1):
                    d = strm[st_]
                    nmo = d["nmo"]

                    @block.vector
                    def _(v, d=d, st_=st_, nmo=nmo, ext=ext):
                        v.wait_ge(sems["s"], d["expv"])
                        v.wait_ge(sems["t"], d["ytv"])
                        vseq(v, v.reduce_sum(nm2[:, nmo + 2:nmo + 3], nm2[:, nmo + 8:nmo + 8 + ext], axis=AX))
                        vseq(v, v.reciprocal(nm2[:, nmo + 3:nmo + 4], nm2[:, nmo + 2:nmo + 3]))
                        inc("v", v.tensor_scalar(ybarb[:, st_ * 64:(st_ + 1) * 64], ps_y[st_][:, :],
                                                 nm2[:, nmo + 3:nmo + 4], None, ALU.mult))
                    d["tl1"] = cnt["v"]
                    guards[("ps_y", st_)] = ("v", cnt["v"])
                    guards[("nmg", nmo)] = ("v", cnt["v"])

                    @block.tensor
                    def _(t, d=d, st_=st_):
                        t.wait_ge(sems["v"], d["tl1"])
                        need(t, "ps_bt")
                        inc("t", t.transpose(ps_bt[:, :], ybarb[:, st_ * 64:(st_ + 1) * 64], eyeb[:, :]))
                    d["tl2"] = cnt["t"]

                    @block.vector
                    def _(v, d=d):
                        v.wait_ge(sems["t"], d["tl2"])
                        inc("v", v.tensor_copy(ybT_sb[:, :], ps_bt[:, :]))
                    d["tl3"] = cnt["v"]
                    guards["ps_bt"] = ("v", cnt["v"])

                    @block.tensor
                    def _(t, d=d):
                        t.wait_ge(sems["v"], d["tl3"])
                        need(t, "ps_yh")
                        inc("t", t.matmul(ps_yh[:, :], vexpb[:, d["h"] * 128:(d["h"] + 1) * 128],
                                          ybT_sb[:, :], start=True, stop=True))
                    d["tl4"] = cnt["t"]

                    @block.vector
                    def _(v, d=d, j=j):
                        v.wait_ge(sems["t"], d["tl4"])
                        inc("v", v.tensor_copy(big[:, d["h"] * 1024 + j * 128: d["h"] * 1024 + (j + 1) * 128],
                                               ps_yh[:, :]))
                    guards["ps_yh"] = ("v", cnt["v"])
            set_guard("mask_sb", "v")

        p3.close()
        s3.close()
        open_stacks.remove(p3)
        open_stacks.remove(s3)
        barrier()
        s3p = ExitStack()
        open_stacks.append(s3p)
        pslot = sb(s3p, "pslot", [128, FC * 512], BF16)
        pso = [psmt(s3p, "pso0", [128, 512], F32), psmt(s3p, "pso1", [128, 512], F32)]
        for ccp in range(4):
            @block.gpsimd
            def _(g, ccp=ccp):
                need(g, "pslot")
                g.dma_start(pslot[:, :], proj_d[:, ccp * 8192:(ccp + 1) * 8192]).then_inc(sems["w"], 16)
            cnt["w"] += 16
            PW = cnt["w"]

            for j in range(NT_OWN):
                @block.tensor
                def _(t, j=j, PW=PW):
                    t.wait_ge(sems["w"], PW)
                    wait(t, "v")
                    need(t, ("pso", j % 2))
                    ins = None
                    for hh in range(H):
                        ins = t.matmul(pso[j % 2][:, :],
                                       big[:, hh * 1024 + j * 128: hh * 1024 + (j + 1) * 128],
                                       pslot[:, hh * 512:(hh + 1) * 512],
                                       start=(hh == 0), stop=(hh == H - 1))
                    inc("t", ins)

                @block.vector
                def _(v, j=j, ccp=ccp):
                    wait(v, "t")
                    inc("v", v.tensor_copy(accb[:, j * CD + ccp * 512: j * CD + (ccp + 1) * 512],
                                           pso[j % 2][:, :]))
                set_guard(("pso", j % 2), "v")
            set_guard("pslot", "t")

    if stage >= 3:
        s3p.close()
        ast.close()
        for st_ in (s3p, ast):
            open_stacks.remove(st_)
        s4 = ExitStack()
        open_stacks.append(s4)
        ws = sb(s4, "ws", [128, 3 * 8192], BF16)
        bbg = sb(s4, "bbg", [128, 2 * 512], F32)
        bbu = sb(s4, "bbu", [128, 2 * 512], F32)
        ga = sb(s4, "ga", [128, 512], F32)
        gs = sb(s4, "gs", [128, 512], F32)
        gs2 = sb(s4, "gs2", [128, 512], F32)
        ua = sb(s4, "ua", [128, 512], F32)
        actw = [sb(s4, "actw0", [128, 512], BF16), sb(s4, "actw1", [128, 512], BF16)]
        dbias_sb = sb(s4, "dbias_sb", [16, CD], BF16)

        p4 = ExitStack()
        open_stacks.append(p4)
        pg = [psmt(p4, "pg0", [128, 512], F32), psmt(p4, "pg1", [128, 512], F32)]
        pu = [psmt(p4, "pu0", [128, 512], F32), psmt(p4, "pu1", [128, 512], F32)]
        pat = [psmt(p4, "pat0", [128, 512], BF16), psmt(p4, "pat1", [128, 512], BF16)]
        pd = [psmt(p4, "pd0", [128, 512], F32), psmt(p4, "pd1", [128, 512], F32)]

        barrier()

        @block.gpsimd
        def _(g):
            g.dma_start(dbias_sb[:, :], dbias_d).then_inc(sems["w"], 16)
        cnt["w"] += 16

        slot_i = [0]

        def load_chunk(src_ap):
            si = slot_i[0] % 3
            slot_i[0] += 1

            @block.gpsimd
            def _(g, si=si):
                need(g, ("ws", si))
                g.dma_start(ws[:, si * 8192:(si + 1) * 8192], src_ap).then_inc(sems["w"], 16)
            cnt["w"] += 16
            return si, cnt["w"]

        for e in range(16):
            for dc in range(4):
                sg, _wn = load_chunk(wg_d[e, :, dc * 8192:(dc + 1) * 8192])
                su, wu_n = load_chunk(wu_d[e, :, dc * 8192:(dc + 1) * 8192])
                dcp = dc % 2

                @block.sync
                def _(s, e=e, dc=dc, dcp=dcp):
                    need(s, ("bb", dcp))
                    s.dma_start(bbg[:, dcp * 512:(dcp + 1) * 512],
                                bass.AP(biasg_d.tensor, e * CD + dc * 512, [[0, 128], [1, 512]])).then_inc(sems["l"], 16)
                    s.dma_start(bbu[:, dcp * 512:(dcp + 1) * 512],
                                bass.AP(biasu_d.tensor, e * CD + dc * 512, [[0, 128], [1, 512]])).then_inc(sems["l"], 16)
                cnt["l"] += 32
                BB_L = cnt["l"]

                def emit_aT(t, t8p):
                    ti = t8p % 2
                    need(t, ("pat", ti))
                    ins = None
                    for db in range(4):
                        ins = t.transpose(pat[ti][:, db * 128:(db + 1) * 128],
                                          actw[ti][:, db * 128:(db + 1) * 128], eyeb[:, :])
                    return ins

                def emit_aT_evac(dc, t8p):
                    ti = t8p % 2

                    @block.vector
                    def _(v, dc=dc, t8p=t8p, ti=ti):
                        wait(v, "t")
                        bv = big[:, :].rearrange("p (D jt t) -> p D jt t", D=FC, jt=8)
                        inc("v", v.tensor_copy(bv[:, dc * 4:(dc + 1) * 4, t8p, :],
                                               pat[ti][:, :].rearrange("p (db t) -> p db t", db=4)))
                    set_guard(("pat", ti), "v")
                    set_guard(("actw", ti), "v")

                for t8 in range(8):
                    ti = t8 % 2

                    @block.tensor
                    def _(t, t8=t8, ti=ti, sg=sg, su=su, wu_n=wu_n):
                        t.wait_ge(sems["w"], wu_n)
                        need(t, ("pg", ti))
                        need(t, ("pu", ti))
                        for fc in range(FC):
                            t.matmul(pg[ti][:, :],
                                     xnT[:, fc * 1024 + t8 * 128: fc * 1024 + (t8 + 1) * 128],
                                     ws[:, sg * 8192 + fc * 512: sg * 8192 + (fc + 1) * 512],
                                     start=(fc == 0), stop=(fc == FC - 1))
                        ins = None
                        for fc in range(FC):
                            ins = t.matmul(pu[ti][:, :],
                                           xnT[:, fc * 1024 + t8 * 128: fc * 1024 + (t8 + 1) * 128],
                                           ws[:, su * 8192 + fc * 512: su * 8192 + (fc + 1) * 512],
                                           start=(fc == 0), stop=(fc == FC - 1))
                        if t8 > 0:
                            ins = emit_aT(t, t8 - 1)
                        inc("t", ins)
                    if t8 > 0:
                        emit_aT_evac(dc, t8 - 1)
                    if t8 == 7:
                        set_guard(("ws", sg), "t")
                        set_guard(("ws", su), "t")

                    @block.vector
                    def _(v, ti=ti, dcp=dcp, BB_L=BB_L):
                        wait(v, "t")
                        v.wait_ge(sems["l"], BB_L)
                        need(v, ("actw", ti))
                        vseq(v, v.tensor_add(ga[:, :], pg[ti][:, :], bbg[:, dcp * 512:(dcp + 1) * 512]))
                        inc("v", v.tensor_scalar_min(ga[:, :], ga[:, :], LIMIT))
                    set_guard(("pg", ti), "v")

                    @block.scalar
                    def _(sc, e=e, t8=t8):
                        wait(sc, "v")
                        inc("s", sc.activation(gs[:, :], ga[:, :], AF.Silu, scale=KAPPA))

                    @block.vector
                    def _(v, ti=ti, dcp=dcp, e=e, t8=t8):
                        wait(v, "s")
                        v.tensor_scalar(gs2[:, :], gs[:, :],
                                        scoresP[:, t8 * 16 + e:t8 * 16 + e + 1], None, ALU.mult)
                        v.tensor_add(ua[:, :], pu[ti][:, :], bbu[:, dcp * 512:(dcp + 1) * 512])
                        vseq(v, v.tensor_scalar(ua[:, :], ua[:, :], LIMIT, -LIMIT, ALU.min, ALU.max))
                        inc("v", v.scalar_tensor_tensor(actw[ti][:, :], ua[:, :], 1.0, gs2[:, :],
                                                        ALU.add, ALU.mult))
                    set_guard(("pu", ti), "v")
                    if t8 == 7:
                        set_guard(("bb", dcp), "v")

                @block.tensor
                def _(t):
                    wait(t, "v")
                    inc("t", emit_aT(t, 7))
                emit_aT_evac(dc, 7)

            ACT_DONE_V = cnt["v"]
            for hc in range(4):
                sd, wd_n = load_chunk(down_d[e, :, hc * 8192:(hc + 1) * 8192])
                for t8 in range(8):
                    di = (hc * 8 + t8) % 2

                    @block.tensor
                    def _(t, t8=t8, di=di, sd=sd, wd_n=wd_n, ACT_DONE_V=ACT_DONE_V):
                        t.wait_ge(sems["w"], wd_n)
                        t.wait_ge(sems["v"], ACT_DONE_V)
                        need(t, ("pd", di))
                        ins = None
                        for D in range(FC):
                            ins = t.matmul(pd[di][:, :],
                                           big[:, D * 1024 + t8 * 128: D * 1024 + (t8 + 1) * 128],
                                           ws[:, sd * 8192 + D * 512: sd * 8192 + (D + 1) * 512],
                                           start=(D == 0), stop=(D == FC - 1))
                        inc("t", ins)
                    if t8 == 7:
                        set_guard(("ws", sd), "t")

                    @block.vector
                    def _(v, t8=t8, hc=hc, di=di):
                        wait(v, "t")
                        dst = accb[:, t8 * CD + hc * 512: t8 * CD + (hc + 1) * 512]
                        inc("v", v.tensor_add(dst, dst, pd[di][:, :]))
                    set_guard(("pd", di), "v")

        for t8 in range(8):
            for hc in range(4):
                di = (t8 * 4 + hc) % 2

                @block.tensor
                def _(t, t8=t8, hc=hc, di=di):
                    wait(t, "w", "v")
                    need(t, ("pd", di))
                    inc("t", t.matmul(pd[di][:, :], scoresT[:, t8 * 128:(t8 + 1) * 128],
                                      dbias_sb[:, hc * 512:(hc + 1) * 512], start=True, stop=True))

                @block.vector
                def _(v, t8=t8, hc=hc, di=di):
                    wait(v, "t")
                    dst = accb[:, t8 * CD + hc * 512: t8 * CD + (hc + 1) * 512]
                    inc("v", v.tensor_add(dst, dst, pd[di][:, :]))
                set_guard(("pd", di), "v")

    if stage >= 3:
        p4.close()
        s4.close()
        open_stacks.remove(p4)
        open_stacks.remove(s4)
        barrier()
        s5 = ExitStack()
        open_stacks.append(s5)
        xf = sb(s5, "xf", [128, 2 * CD], F32)
        ob = sb(s5, "ob", [128, 2 * CD], F32)

        for j in range(NT_OWN):
            half = j % 2

            @block.sync
            def _(s, j=j, half=half):
                need(s, ("xf", half))
                s.dma_start(xf[:, half * CD:(half + 1) * CD],
                            x_own[j * 128:(j + 1) * 128, :]).then_inc(sems["l"], 16)
            cnt["l"] += 16

            @block.vector
            def _(v, j=j, half=half):
                wait(v, "l")
                need(v, ("ob", half))
                inc("v", v.tensor_add(ob[:, half * CD:(half + 1) * CD],
                                      accb[:, j * CD:(j + 1) * CD],
                                      xf[:, half * CD:(half + 1) * CD]))
            set_guard(("xf", half), "v")

            @block.sync
            def _(s, j=j, half=half):
                wait(s, "v")
                s.dma_start(out_d[j * 128:(j + 1) * 128, :],
                            ob[:, half * CD:(half + 1) * CD]).then_inc(sems["o"], 16)
            cnt["o"] += 16
            set_guard(("ob", half), "o")

        @block.sync
        def _(s):
            s.wait_ge(sems["o"], cnt["o"])

    if dbg:
        barrier()
        avail = {}
        if stage < 3:
            avail.update({
                "d_klatT": (klatT[:, :], True), "d_vlat": (vlat[:, :], True),
                "d_qtl": (qtl[:, :], True),
            })
        avail.update({
            "d_xnT": (xnT[:, :], True),
            "d_scoresP": (scoresP[:, :], False),
            "d_scoresT": (scoresT[:, :], True),
            "d_acc": (accb[:, :], False),
            "d_big": (big[:, :], True),
        })
        for name in dbg:
            if name not in avail:
                continue
            src_ap, cast = avail[name]
            if cast:
                @block.gpsimd
                def _(g, name=name, src_ap=src_ap):
                    g.dma_start(dbg_d[name], src_ap).then_inc(sems["w"], 16)
                cnt["w"] += 16
            else:
                @block.sync
                def _(s, name=name, src_ap=src_ap):
                    s.dma_start(dbg_d[name], src_ap).then_inc(sems["l"], 16)
                cnt["l"] += 16

        @block.sync
        def _(s):
            s.wait_ge(sems["l"], cnt["l"])

        @block.gpsimd
        def _(g):
            g.wait_ge(sems["w"], cnt["w"])

    for st_ in reversed(open_stacks):
        st_.close()
    return nc


def tile_w(w):
    n = w.shape[1]
    a = w.reshape(16, 128, n // 512, 512)
    a = a.transpose(1, 2, 0, 3)
    return np.ascontiguousarray(a.reshape(128, -1))


def prep_inputs(ii):
    x = np.asarray(ii["L0_attn_x"], np.float32)
    wq = np.asarray(ii["L0_attn_wq"], np.float32)
    wk = np.asarray(ii["L0_attn_wk_lat"], np.float32)
    wv = np.asarray(ii["L0_attn_wv_lat"], np.float32)
    kexpand = np.asarray(ii["L0_attn_k_expand"], np.float32)
    vexpand = np.asarray(ii["L0_attn_v_expand"], np.float32)
    projw = np.asarray(ii["L0_attn_proj_w"], np.float32)
    rw = np.asarray(ii["L0_ffn_router_weight"], np.float32)
    rb = np.asarray(ii["L0_ffn_router_bias"], np.float32)
    gup = np.asarray(ii["L0_ffn_gate_up_proj"], np.float32)
    gupb = np.asarray(ii["L0_ffn_gate_up_proj_bias"], np.float32)
    dw = np.asarray(ii["L0_ffn_down_proj"], np.float32)
    db = np.asarray(ii["L0_ffn_down_proj_bias"], np.float32)

    P128 = np.concatenate([np.arange(0, 128, 2), np.arange(1, 128, 2)])
    P64 = np.concatenate([np.arange(0, 64, 2), np.arange(1, 64, 2)])

    shared = {
        "jrow": np.broadcast_to(np.arange(64, dtype=np.float32), (128, 64)).copy(),
        "eye_f": np.eye(128, dtype=np.float32),
        "pos_ctx": (np.arange(32, dtype=np.float32)[None, :] * 128
                    + np.arange(128, dtype=np.float32)[:, None]).astype(np.float32),
        "wq_t": tile_w(wq),
        "kvw_t": np.ascontiguousarray(np.concatenate([wk, wv], axis=1).reshape(16, 128, 128).transpose(1, 0, 2).reshape(128, 2048)),
        "kexp_t": np.ascontiguousarray(
            kexpand.transpose(0, 2, 1)[:, P128][:, :, P64].transpose(1, 0, 2).reshape(128, 16 * 64)),
        "vexp_t": np.ascontiguousarray(vexpand.transpose(1, 0, 2).reshape(64, 16 * 128)),
        "proj_t": tile_w(projw),
        "rwT_t": np.ascontiguousarray(rw.T.reshape(16, 128, 16).transpose(1, 0, 2).reshape(128, 256)),
        "rbias": rb.reshape(1, 16).copy(),
        "wg_t": np.stack([tile_w(np.ascontiguousarray(gup[e][:, 0::2])) for e in range(16)]),
        "wu_t": np.stack([tile_w(np.ascontiguousarray(gup[e][:, 1::2])) for e in range(16)]),
        "biasg": np.ascontiguousarray(gupb[:, 0::2]),
        "biasu": np.ascontiguousarray(gupb[:, 1::2]),
        "down_t": np.stack([tile_w(dw[e]) for e in range(16)]),
        "dbias": db.copy(),
    }

    in_maps = []
    own_tiles = []
    for c in range(8):
        b, cp = c // 4, c % 4
        tiles = [4 * jj + cp for jj in range(8)]
        own_tiles.append(tiles)
        rows = np.concatenate([np.arange(g * 128, (g + 1) * 128) for g in tiles])
        mask = np.zeros((128, 4096), np.float32)
        for jj, g in enumerate(tiles):
            gq = g * 128 + np.arange(128)[:, None]
            gk = jj * 512 + np.arange(512)[None, :]
            mask[:, jj * 512:(jj + 1) * 512] = np.where(gk <= gq, 0.0, NEG)
        posn = (np.array([g * 128 for g in tiles], np.float32)[None, :]
                + np.arange(128, dtype=np.float32)[:, None])
        m = dict(shared)
        m["x_ctx"] = np.ascontiguousarray(x[b])
        m["x_own"] = np.ascontiguousarray(x[b][rows])
        m["maskd"] = mask
        m["pos_own"] = posn.astype(np.float32)
        in_maps.append(m)
    return in_maps, own_tiles


_CACHE = {}


def kernel(**inputs):
    in_maps, own_tiles = prep_inputs(inputs)
    if "nc" not in _CACHE:
        _CACHE["nc"] = build(stage=3, dbg=())
    res = run_bass_kernel_spmd(_CACHE["nc"], in_maps, core_ids=list(range(8)))
    out = np.zeros((2, 4096, 2048), np.float32)
    for c in range(8):
        b = c // 4
        o = res.results[c]["out"].reshape(8, 128, 2048)
        for jj, g in enumerate(own_tiles[c]):
            out[b, g * 128:(g + 1) * 128, :] = o[jj]
    return out


# version 05
# speedup vs baseline: 1.0672x; 1.0637x over previous
import math
import numpy as np
from contextlib import ExitStack

import concourse.bass as bass
import concourse.mybir as mybir
from concourse.bass_utils import run_bass_kernel_spmd

F32 = mybir.dt.float32
BF16 = mybir.dt.bfloat16
I32 = mybir.dt.int32
ALU = mybir.AluOpType
AF = mybir.ActivationFunctionType
AX = mybir.AxisListType.X

PI = math.pi
LN1E4 = math.log(10000.0)
SCALE = 1.0 / math.sqrt(128.0)
KAPPA = 1.702
LIMIT = 7.0
NEG = -1.0e30
EPS = 1e-5

NT_CTX, NT_OWN, FC, H, CD = 32, 8, 16, 16, 2048


def build(stage=3, dbg=()):
    nc = bass.Bass("TRN2", target_bir_lowering=False)

    x_ctx = nc.dram_tensor("x_ctx", [4096, CD], F32, kind="ExternalInput").ap()
    x_own = nc.dram_tensor("x_own", [1024, CD], F32, kind="ExternalInput").ap()
    maskd = nc.dram_tensor("maskd", [128, 4096], F32, kind="ExternalInput").ap()
    pos_own = nc.dram_tensor("pos_own", [128, 8], F32, kind="ExternalInput").ap()
    pos_ctx = nc.dram_tensor("pos_ctx", [128, 32], F32, kind="ExternalInput").ap()
    jrow = nc.dram_tensor("jrow", [128, 64], F32, kind="ExternalInput").ap()
    eye_f = nc.dram_tensor("eye_f", [128, 128], F32, kind="ExternalInput").ap()
    wq_d = nc.dram_tensor("wq_t", [128, FC * CD], F32, kind="ExternalInput").ap()
    kvw_d = nc.dram_tensor("kvw_t", [128, FC * 128], F32, kind="ExternalInput").ap()
    kexp_d = nc.dram_tensor("kexp_t", [128, H * 64], F32, kind="ExternalInput").ap()
    vexp_d = nc.dram_tensor("vexp_t", [64, H * 128], F32, kind="ExternalInput").ap()
    proj_d = nc.dram_tensor("proj_t", [128, 4 * FC * 512], F32, kind="ExternalInput").ap()
    rwT_d = nc.dram_tensor("rwT_t", [128, FC * 16], F32, kind="ExternalInput").ap()
    rbias_d = nc.dram_tensor("rbias", [1, 16], F32, kind="ExternalInput").ap()
    wg_d = nc.dram_tensor("wg_t", [16, 128, 4 * FC * 512], F32, kind="ExternalInput").ap()
    wu_d = nc.dram_tensor("wu_t", [16, 128, 4 * FC * 512], F32, kind="ExternalInput").ap()
    biasg_d = nc.dram_tensor("biasg", [16, CD], F32, kind="ExternalInput").ap()
    biasu_d = nc.dram_tensor("biasu", [16, CD], F32, kind="ExternalInput").ap()
    down_d = nc.dram_tensor("down_t", [16, 128, 4 * FC * 512], F32, kind="ExternalInput").ap()
    dbias_d = nc.dram_tensor("dbias", [16, CD], F32, kind="ExternalInput").ap()
    ltri_d = nc.dram_tensor("ltri", [128, 128], F32, kind="ExternalInput").ap()
    iota_d = nc.dram_tensor("iota384", [128, 384], F32, kind="ExternalInput").ap()
    piotak_d = nc.dram_tensor("piotak", [128, 4], F32, kind="ExternalInput").ap()
    out_d = nc.dram_tensor("out", [1024, CD], F32, kind="ExternalOutput").ap()

    dbg_d = {}
    for name, shape in [
        ("d_klatT", [64, 4096]), ("d_vlat", [128, NT_CTX * 64]),
        ("d_xnT", [128, FC * 1024]), ("d_qtl", [128, 8 * 1024]),
        ("d_scoresP", [128, 128]), ("d_scoresT", [16, 1024]),
        ("d_acc", [128, NT_OWN * CD]), ("d_big", [128, FC * 1024]),
    ]:
        if name in dbg:
            dbg_d[name] = nc.dram_tensor(name, shape, F32, kind="ExternalOutput").ap()

    gst = ExitStack()
    open_stacks = [gst]

    def sb(stk, name, shape, dt):
        return stk.enter_context(nc.sbuf_tensor(name, shape, dt))

    def psmt(stk, name, shape, dt):
        return stk.enter_context(nc.psum_tensor(name, shape, dt))

    xnT = sb(gst, "xnT", [128, FC * 1024], BF16)
    accb = sb(gst, "accb", [128, NT_OWN * CD], F32)
    big = sb(gst, "big", [128, FC * 1024], BF16)
    eyeb = sb(gst, "eyeb", [128, 128], BF16)
    eyef = sb(gst, "eyef", [128, 128], F32)
    scoresP = sb(gst, "scoresP", [128, 128], F32)
    scoresR = sb(gst, "scoresR", [128, 16], F32)
    scoresT = sb(gst, "scoresT", [16, 1024], BF16)
    rbias_bc = sb(gst, "rbias_bc", [128, 16], F32)
    maskRf = sb(gst, "maskRf", [128, 128], F32)
    maskRb = sb(gst, "maskRb", [128, 128], BF16)

    sems = {}
    for k in ("l", "w", "o", "t", "v", "s"):
        sems[k] = gst.enter_context(nc.semaphore("sem_" + k))
    cnt = {k: 0 for k in sems}

    block = gst.enter_context(nc.Block())

    def inc(key, ins, k=1):
        cnt[key] += k
        ins.then_inc(sems[key], k)
        return ins

    guards = {}

    def need(e, key):
        if key in guards:
            k, v = guards[key]
            if v > 0:
                e.wait_ge(sems[k], v)

    def set_guard(key, k):
        guards[key] = (k, cnt[k])

    def vseq(v, ins):
        inc("v", ins)
        v.wait_ge(sems["v"], cnt["v"])
        return ins

    def wait(e, *keys):
        for k in keys:
            if cnt[k] > 0:
                e.wait_ge(sems[k], cnt[k])

    def barrier():
        vals = dict(cnt)

        def wa(e):
            for k, v in vals.items():
                if v > 0:
                    e.wait_ge(sems[k], v)
        for name in ("sync", "gpsimd", "tensor", "vector", "scalar"):
            getattr(block, name)(wa)

    ast = ExitStack()
    open_stacks.append(ast)
    klatT = sb(ast, "klatT", [128, 4096], BF16)
    vlat = sb(ast, "vlat", [128, NT_CTX * 64], BF16)
    qtl = sb(ast, "qtl", [128, 8 * 1024], BF16)
    kexp = sb(ast, "kexp", [128, H * 64], BF16)
    vexpb = sb(ast, "vexpb", [64, H * 128], BF16)


    s12 = ExitStack()
    open_stacks.append(s12)
    kvw = sb(s12, "kvw", [128, FC * 128], BF16)
    rwT = sb(s12, "rwT", [128, FC * 16], BF16)
    invf = sb(s12, "invf", [128, 64], F32)
    invfk = sb(s12, "invfk", [128, 32], F32)
    jrow_sb = sb(s12, "jrow_sb", [128, 64], F32)
    poso = sb(s12, "poso", [128, 8], F32)
    posc = sb(s12, "posc", [128, 32], F32)
    red = sb(s12, "red", [128, 64], F32)
    kib = sb(s12, "kib", [128, 64], I32)
    cosqa = sb(s12, "cosqa", [128, 512], F32)
    sinqa = sb(s12, "sinqa", [128, 512], F32)
    cosk = sb(s12, "cosk", [128, 32], F32)
    sink = sb(s12, "sink", [128, 32], F32)
    lnst = sb(s12, "lnst", [128, 16], F32)
    tA = sb(s12, "tA", [128, 128], F32)
    tB = sb(s12, "tB", [128, 128], F32)
    xt = sb(s12, "xt", [128, 2 * CD], F32)
    xnt = sb(s12, "xnt", [128, CD], BF16)
    xnTs = sb(s12, "xnTs", [128, CD], BF16)
    qro4 = sb(s12, "qro4", [128, 512], BF16)
    qT4 = sb(s12, "qT4", [128, 512], BF16)
    kro = sb(s12, "kro", [128, 64], BF16)


    @block.sync
    def _(s):
        s.dma_start(eyef[:, :], eye_f).then_inc(sems["l"], 16)
        s.dma_start(jrow_sb[:, :], jrow).then_inc(sems["l"], 16)
        s.dma_start(poso[:, :], pos_own).then_inc(sems["l"], 16)
        s.dma_start(posc[:, :], pos_ctx).then_inc(sems["l"], 16)
        s.dma_start(rbias_bc[:, :], bass.AP(rbias_d.tensor, 0, [[0, 128], [1, 16]])).then_inc(sems["l"], 16)
    cnt["l"] += 16 * 5
    P0_L = cnt["l"]

    @block.scalar
    def _(sc):
        sc.wait_ge(sems["l"], P0_L)
        sc.activation(invf[:, :], jrow_sb[:, :], AF.Exp, scale=-LN1E4 / 64.0)
        inc("s", sc.activation(invfk[:, :], jrow_sb[:, :32], AF.Exp, scale=-LN1E4 / 32.0))

    def sincos(pos_col, invf_ap, n, cos_dst, sin_dst):
        @block.vector
        def _(v):
            wait(v, "s", "l")
            vseq(v, v.tensor_scalar(tA[:, :n], invf_ap, pos_col, None, ALU.mult))
            vseq(v, v.tensor_scalar_mul(red[:, :n], tA[:, :n], 1.0 / (2 * PI)))
            inc("v", v.tensor_copy(kib[:, :n], red[:, :n]))

        @block.scalar
        def _(sc):
            wait(sc, "v")
            inc("s", sc.copy(red[:, :n], kib[:, :n]))

        @block.vector
        def _(v):
            wait(v, "s")
            inc("v", v.scalar_tensor_tensor(red[:, :n], red[:, :n], -2 * PI, tA[:, :n], ALU.mult, ALU.add))

        @block.scalar
        def _(sc):
            wait(sc, "v")
            inc("s", sc.activation(sin_dst, red[:, :n], AF.Sin))

        @block.vector
        def _(v):
            wait(v, "s")
            vseq(v, v.tensor_scalar_add(tB[:, :n], tA[:, :n], PI / 2))
            vseq(v, v.tensor_scalar_mul(red[:, :n], tB[:, :n], 1.0 / (2 * PI)))
            inc("v", v.tensor_copy(kib[:, :n], red[:, :n]))

        @block.scalar
        def _(sc):
            wait(sc, "v")
            inc("s", sc.copy(red[:, :n], kib[:, :n]))

        @block.vector
        def _(v):
            wait(v, "s")
            inc("v", v.scalar_tensor_tensor(red[:, :n], red[:, :n], -2 * PI, tB[:, :n], ALU.mult, ALU.add))

        @block.scalar
        def _(sc):
            wait(sc, "v")
            inc("s", sc.activation(cos_dst, red[:, :n], AF.Sin))

    @block.gpsimd
    def _(g):
        g.dma_start(eyeb[:, :], eye_f).then_inc(sems["w"], 16)
        g.dma_start(kvw[:, :], kvw_d).then_inc(sems["w"], 16)
        g.dma_start(kexp[:, :], kexp_d).then_inc(sems["w"], 16)
        g.dma_start(vexpb[:, :], vexp_d).then_inc(sems["w"], 16)
        g.dma_start(rwT[:, :], rwT_d).then_inc(sems["w"], 16)
    cnt["w"] += 16 * 5
    P0_W = cnt["w"]

    p12 = ExitStack()
    open_stacks.append(p12)
    pk0 = psmt(p12, "pk0", [128, 1024], BF16)
    pk1 = psmt(p12, "pk1", [128, 1024], BF16)
    pq0 = psmt(p12, "pq0", [128, 512], F32)
    pq1 = psmt(p12, "pq1", [128, 512], F32)
    pqt = psmt(p12, "pqt", [128, 128], F32)
    prt = psmt(p12, "prt", [128, 128], F32)
    ptrash = psmt(p12, "ptrash", [128, 512], F32)

    def ln_tile(half):
        xv = xt[:, half * CD:(half + 1) * CD]

        @block.vector
        def _(v):
            wait(v, "l")
            inc("v", v.reduce_sum(lnst[:, 0:1], xv, axis=AX))

        @block.scalar
        def _(sc):
            wait(sc, "l")
            ins = None
            for c4 in range(4):
                ins = sc.activation(ptrash[:, :], xv[:, c4 * 512:(c4 + 1) * 512], AF.Square,
                                    accum_out=lnst[:, 8 + c4:9 + c4])
            inc("s", ins)

        @block.vector
        def _(v):
            wait(v, "s", "v")
            vseq(v, v.tensor_scalar_mul(lnst[:, 1:2], lnst[:, 0:1], 1.0 / CD))
            vseq(v, v.reduce_sum(lnst[:, 2:3], lnst[:, 8:12], axis=AX))
            vseq(v, v.tensor_scalar(lnst[:, 3:4], lnst[:, 2:3], 1.0 / CD, EPS, ALU.mult, ALU.add))
            vseq(v, v.tensor_mul(lnst[:, 4:5], lnst[:, 1:2], lnst[:, 1:2]))
            inc("v", v.scalar_tensor_tensor(lnst[:, 5:6], lnst[:, 4:5], -1.0, lnst[:, 3:4],
                                            ALU.mult, ALU.add))

        @block.scalar
        def _(sc):
            wait(sc, "v")
            sc.activation(lnst[:, 6:7], lnst[:, 5:6], AF.Ln)
            inc("s", sc.activation(lnst[:, 7:8], lnst[:, 6:7], AF.Exp, scale=-0.5))

        @block.vector
        def _(v):
            wait(v, "s")
            inc("v", v.tensor_scalar(xnt[:, :], xv, lnst[:, 1:2], lnst[:, 7:8],
                                     ALU.subtract, ALU.mult))

    def xn_transposes(own_j):
        @block.tensor
        def _(t):
            wait(t, "v")
            t.wait_ge(sems["w"], 16)
            need(t, "pk")
            ins = None
            for fc in range(FC):
                pk = pk0 if fc < 8 else pk1
                ins = t.transpose(pk[:, (fc % 8) * 128:(fc % 8 + 1) * 128],
                                  xnt[:, fc * 128:(fc + 1) * 128], eyeb[:, :])
            inc("t", ins)

        @block.vector
        def _(v):
            wait(v, "t")
            pv0 = pk0[:, :].rearrange("p (f t) -> p f t", f=8)
            pv1 = pk1[:, :].rearrange("p (f t) -> p f t", f=8)
            if own_j is None:
                v.tensor_copy(xnTs[:, 0:1024], pk0[:, :])
                inc("v", v.tensor_copy(xnTs[:, 1024:2048], pk1[:, :]))
            else:
                xv = xnT[:, :].rearrange("p (f jt t) -> p f jt t", f=FC, jt=8)
                v.tensor_copy(xv[:, 0:8, own_j, :], pv0)
                inc("v", v.tensor_copy(xv[:, 8:16, own_j, :], pv1))
        set_guard("pk", "v")

    if stage >= 1:
        for tau in range(NT_CTX):
            half = tau % 2

            @block.sync
            def _(s, tau=tau, half=half):
                need(s, ("xt", half))
                s.dma_start(xt[:, half * CD:(half + 1) * CD],
                            x_ctx[tau * 128:(tau + 1) * 128, :]).then_inc(sems["l"], 16)
            cnt["l"] += 16
            ln_tile(half)
            set_guard(("xt", half), "v")
            xn_transposes(None)

            @block.tensor
            def _(t):
                wait(t, "v")
                t.wait_ge(sems["w"], 32)
                need(t, ("pq", 0))
                ins = None
                for fc in range(FC):
                    ins = t.matmul(pq0[:, 0:128], xnTs[:, fc * 128:(fc + 1) * 128],
                                   kvw[:, fc * 128:(fc + 1) * 128],
                                   start=(fc == 0), stop=(fc == FC - 1))
                inc("t", ins)

            sincos(posc[:, tau:tau + 1], invfk[:, :], 32, cosk[:, :], sink[:, :])

            @block.vector
            def _(v, tau=tau):
                wait(v, "t", "s")
                v.tensor_mul(tA[:, 0:32], pq0[:, 0:64:2], cosk[:, :])
                vseq(v, v.tensor_mul(tB[:, 0:32], pq0[:, 1:64:2], sink[:, :]))
                v.tensor_sub(kro[:, 0:32], tA[:, 0:32], tB[:, 0:32])
                v.tensor_mul(tA[:, 32:64], pq0[:, 0:64:2], sink[:, :])
                vseq(v, v.tensor_mul(tB[:, 32:64], pq0[:, 1:64:2], cosk[:, :]))
                v.tensor_add(kro[:, 32:64], tA[:, 32:64], tB[:, 32:64])
                inc("v", v.tensor_copy(vlat[:, tau * 64:(tau + 1) * 64], pq0[:, 64:128]))
            set_guard(("pq", 0), "v")

            @block.tensor
            def _(t):
                wait(t, "v")
                need(t, "pk")
                t.transpose(pk0[0:64, 0:128], kro[:, :], eyeb[:, :])
                inc("t", t.transpose(pk0[64:128, 0:128], kro[:, :], eyeb[:, :],
                                     tile_position=(0, 64)))

            @block.vector
            def _(v, tau=tau):
                wait(v, "t")
                inc("v", v.tensor_copy(klatT[:, tau * 128:(tau + 1) * 128], pk0[:, 0:128]))
            set_guard("pk", "v")

    if stage >= 1:
        for j in range(NT_OWN):
            half = j % 2

            @block.sync
            def _(s, j=j, half=half):
                need(s, ("xt", half))
                s.dma_start(xt[:, half * CD:(half + 1) * CD],
                            x_own[j * 128:(j + 1) * 128, :]).then_inc(sems["l"], 16)
            cnt["l"] += 16
            ln_tile(half)
            set_guard(("xt", half), "v")
            xn_transposes(own_j=j)

            sincos(poso[:, j:j + 1], invf[:, :], 64,
                   cosqa[:, j * 64:(j + 1) * 64], sinqa[:, j * 64:(j + 1) * 64])

            @block.tensor
            def _(t, j=j):
                t.wait_ge(sems["w"], P0_W)
                wait(t, "v")
                need(t, "prt")
                ins = None
                for fc in range(FC):
                    ins = t.matmul(prt[:, 0:16], xnT[:, fc * 1024 + j * 128: fc * 1024 + (j + 1) * 128],
                                   rwT[:, fc * 16:(fc + 1) * 16],
                                   start=(fc == 0), stop=(fc == FC - 1))
                inc("t", ins)

            @block.vector
            def _(v):
                wait(v, "t", "v")
                v.wait_ge(sems["l"], P0_L)
                lg, cur = tA[:, 0:16], tA[:, 16:32]
                nm1, nmk, cmp = tA[:, 32:33], tA[:, 33:34], tA[:, 34:50]
                vseq(v, v.tensor_add(lg, prt[:, 0:16], rbias_bc[:, :]))
                vseq(v, v.reduce_max(nm1, lg, axis=AX, negate=True))
                vseq(v, v.tensor_copy(cur, lg))
                for it in range(3):
                    nm = nm1 if it == 0 else nmk
                    vseq(v, v.tensor_scalar(cmp, cur, nm, 0.0, ALU.add, ALU.is_ge))
                    vseq(v, v.scalar_tensor_tensor(cur, cmp, NEG, cur, ALU.mult, ALU.add))
                    vseq(v, v.reduce_max(nmk, cur, axis=AX, negate=True))
                inc("v", v.tensor_scalar(cmp, lg, nmk, 0.0, ALU.add, ALU.is_ge))

            @block.scalar
            def _(sc):
                wait(sc, "v")
                inc("s", sc.activation(tA[:, 50:66], tA[:, 0:16], AF.Exp, bias=tA[:, 32:33]))

            @block.vector
            def _(v, j=j):
                wait(v, "s")
                ew, ssum, rs = tA[:, 50:66], tA[:, 66:67], tA[:, 67:68]
                vseq(v, v.tensor_mul(ew, ew, tA[:, 34:50]))
                vseq(v, v.reduce_sum(ssum, ew, axis=AX))
                vseq(v, v.reciprocal(rs, ssum))
                vseq(v, v.tensor_scalar(scoresR[:, :], ew, rs, None, ALU.mult))
                v.tensor_copy(maskRf[:, j * 16:(j + 1) * 16], tA[:, 34:50])
                v.tensor_copy(maskRb[:, j * 16:(j + 1) * 16], tA[:, 34:50])
                inc("v", v.tensor_scalar_mul(scoresP[:, j * 16:(j + 1) * 16], scoresR[:, :],
                                             1.0 / KAPPA))

            @block.tensor
            def _(t):
                wait(t, "v")
                inc("t", t.transpose(prt[0:16, :], scoresR[:, :], eyef[:, :]))

            @block.vector
            def _(v, j=j):
                wait(v, "t")
                inc("v", v.tensor_copy(scoresT[:, j * 128:(j + 1) * 128], prt[0:16, :]))
            set_guard("prt", "v")

    if stage >= 1:
        XNT_DONE_V = cnt["v"]
        for cc in range(4):
            bh = cc % 2

            @block.gpsimd
            def _(g, cc=cc, bh=bh):
                need(g, ("bigq", bh))
                g.dma_start(big[:, bh * 8192:(bh + 1) * 8192],
                            wq_d[:, cc * 8192:(cc + 1) * 8192]).then_inc(sems["w"], 16)
            cnt["w"] += 16
            WQ_W = cnt["w"]

            for j in range(NT_OWN):
                ji = j % 2
                pq = pq0 if ji == 0 else pq1

                @block.tensor
                def _(t, j=j, cc=cc, bh=bh, pq=pq, ji=ji, WQ_W=WQ_W):
                    t.wait_ge(sems["w"], WQ_W)
                    t.wait_ge(sems["v"], XNT_DONE_V)
                    need(t, ("pq", ji))
                    ins = None
                    for fc in range(FC):
                        ins = t.matmul(pq[:, :], xnT[:, fc * 1024 + j * 128: fc * 1024 + (j + 1) * 128],
                                       big[:, bh * 8192 + fc * 512: bh * 8192 + (fc + 1) * 512],
                                       start=(fc == 0), stop=(fc == FC - 1))
                    inc("t", ins)

                @block.vector
                def _(v, j=j, pq=pq):
                    wait(v, "t")
                    need(v, "qro4")
                    cq = cosqa[:, j * 64:(j + 1) * 64]
                    sq = sinqa[:, j * 64:(j + 1) * 64]
                    for hh in range(4):
                        x1 = pq[:, hh * 128 + 0: hh * 128 + 128: 2]
                        x2 = pq[:, hh * 128 + 1: hh * 128 + 128: 2]
                        v.tensor_mul(tA[:, 0:64], x1, cq)
                        vseq(v, v.tensor_mul(tB[:, 0:64], x2, sq))
                        v.tensor_sub(qro4[:, hh * 128: hh * 128 + 64], tA[:, 0:64], tB[:, 0:64])
                        v.tensor_mul(tA[:, 64:128], x1, sq)
                        vseq(v, v.tensor_mul(tB[:, 64:128], x2, cq))
                        v.tensor_add(qro4[:, hh * 128 + 64: hh * 128 + 128], tA[:, 64:128], tB[:, 64:128])
                    inc("v", v.engine_nop())
                set_guard(("pq", ji), "v")

                @block.tensor
                def _(t):
                    wait(t, "v")
                    need(t, "pk")
                    ins = None
                    for hh in range(4):
                        ins = t.transpose(pk0[:, hh * 128:(hh + 1) * 128],
                                          qro4[:, hh * 128:(hh + 1) * 128], eyeb[:, :])
                    inc("t", ins)
                set_guard("qro4", "t")

                @block.vector
                def _(v):
                    wait(v, "t")
                    need(v, "qT4")
                    inc("v", v.tensor_copy(qT4[:, :], pk0[:, 0:512]))
                set_guard("pk", "v")

                for hh in range(4):
                    h = cc * 4 + hh
                    po = h % 2

                    @block.tensor
                    def _(t, h=h, hh=hh, po=po):
                        wait(t, "v")
                        need(t, "pqt")
                        dst = pqt[64:128, :] if po else pqt[0:64, :]
                        inc("t", t.matmul(dst, kexp[:, h * 64:(h + 1) * 64],
                                          qT4[:, hh * 128:(hh + 1) * 128], start=True, stop=True,
                                          tile_position=(0, 64) if po else None))

                    @block.vector
                    def _(v, h=h, j=j, po=po):
                        wait(v, "t")
                        src = pqt[64:128, :] if po else pqt[0:64, :]
                        dst = qtl[po * 64:po * 64 + 64,
                                  (h // 2) * 1024 + j * 128:(h // 2) * 1024 + (j + 1) * 128]
                        inc("v", v.tensor_copy(dst, src))
                    set_guard("pqt", "v")
                set_guard("qT4", "t")
            set_guard(("bigq", bh), "t")

    if stage >= 2:
        p12.close()
        s12.close()
        open_stacks.remove(p12)
        open_stacks.remove(s12)
        s3 = ExitStack()
        open_stacks.append(s3)
        mask_sb = sb(s3, "mask_sb", [128, 512], F32)
        scores2 = sb(s3, "scores2", [128, 2 * 4096], F32)
        att = [sb(s3, "att0", [128, 512], BF16), sb(s3, "att1", [128, 512], BF16)]
        attT = [sb(s3, "attT0", [128, 512], BF16), sb(s3, "attT1", [128, 512], BF16)]
        nm2 = sb(s3, "nm2", [128, 32], F32)
        ybarb = sb(s3, "ybarb", [128, 128], BF16)
        ybT_sb = sb(s3, "ybT_sb", [64, 128], BF16)

        p3 = ExitStack()
        open_stacks.append(p3)
        ps_s = [psmt(p3, "ps_s0", [128, 512], F32), psmt(p3, "ps_s1", [128, 512], F32)]
        ps_aT = [psmt(p3, "ps_aT0", [128, 512], BF16), psmt(p3, "ps_aT1", [128, 512], BF16)]
        ps_y = [psmt(p3, "ps_y0", [128, 64], F32), psmt(p3, "ps_y1", [128, 64], F32)]
        ps_bt = psmt(p3, "ps_bt", [64, 128], BF16)
        ps_yh = psmt(p3, "ps_yh", [128, 128], F32)

        barrier()

        for j in range(NT_OWN):
            ext = j + 1

            @block.sync
            def _(s_, j=j):
                need(s_, "mask_sb")
                s_.dma_start(mask_sb[:, :], maskd[:, j * 512:(j + 1) * 512]).then_inc(sems["l"], 16)
            cnt["l"] += 16
            MASK_L = cnt["l"]

            for hp in range(H // 2):
                strm = []
                for st_ in (0, 1):
                    h = 2 * hp + st_
                    strm.append({"h": h, "hpart": (h % 2) * 64,
                                 "hcol": (h // 2) * 1024 + j * 128,
                                 "sc": scores2[:, st_ * 4096:(st_ + 1) * 4096],
                                 "nmo": st_ * 16})

                for c in range(ext):
                    last = (c == j)
                    for st_ in (0, 1):
                        d = strm[st_]

                        @block.tensor
                        def _(t, d=d, c=c, st_=st_):
                            need(t, ("ps_s", st_))
                            need(t, ("scores", st_))
                            inc("t", t.matmul(ps_s[st_][:, :],
                                              qtl[d["hpart"]:d["hpart"] + 64, d["hcol"]:d["hcol"] + 128],
                                              klatT[d["hpart"]:d["hpart"] + 64, c * 512:(c + 1) * 512],
                                              start=True, stop=True))
                        d["mmv"] = cnt["t"]

                        @block.vector
                        def _(v, d=d, c=c, st_=st_, last=last, MASK_L=MASK_L):
                            v.wait_ge(sems["t"], d["mmv"])
                            dst = d["sc"][:, c * 512:(c + 1) * 512]
                            if last:
                                v.wait_ge(sems["l"], MASK_L)
                                inc("v", v.tensor_add(dst, ps_s[st_][:, :], mask_sb[:, :]))
                            else:
                                inc("v", v.tensor_copy(dst, ps_s[st_][:, :]))
                        d["evv"] = cnt["v"]
                        guards[("ps_s", st_)] = ("v", cnt["v"])

                for st_ in (0, 1):
                    d = strm[st_]
                    nmo = d["nmo"]

                    @block.vector
                    def _(v, d=d, nmo=nmo, ext=ext):
                        v.wait_ge(sems["v"], d["evv"])
                        need(v, ("nmg", d["nmo"]))
                        vseq(v, v.reduce_max(nm2[:, nmo:nmo + 1], d["sc"][:, :ext * 512],
                                             axis=AX, negate=True))
                        inc("v", v.tensor_scalar_mul(nm2[:, nmo + 1:nmo + 2], nm2[:, nmo:nmo + 1], SCALE))
                    d["biasv"] = cnt["v"]

                for c in range(ext):
                    @block.scalar
                    def _(sc_, c=c, strm=strm):
                        for st_ in (0, 1):
                            d = strm[st_]
                            nmo = d["nmo"]
                            sc_.wait_ge(sems["v"], d["biasv"])
                            need(sc_, ("att", st_))
                            inc("s", sc_.activation(att[st_][:, :], d["sc"][:, c * 512:(c + 1) * 512],
                                                    AF.Exp, scale=SCALE, bias=nm2[:, nmo + 1:nmo + 2],
                                                    accum_out=nm2[:, nmo + 8 + c:nmo + 9 + c]))
                            d["expv"] = cnt["s"]
                    for st_ in (0, 1):
                        d = strm[st_]
                        nmo = d["nmo"]

                        @block.tensor
                        def _(t, d=d, st_=st_):
                            t.wait_ge(sems["s"], d["expv"])
                            need(t, ("ps_aT", st_))
                            ins = None
                            for kk in range(4):
                                ins = t.transpose(ps_aT[st_][:, kk * 128:(kk + 1) * 128],
                                                  att[st_][:, kk * 128:(kk + 1) * 128], eyeb[:, :])
                            inc("t", ins)
                        d["aTv"] = cnt["t"]
                        guards[("att", st_)] = ("t", cnt["t"])

                        @block.vector
                        def _(v, d=d, st_=st_):
                            v.wait_ge(sems["t"], d["aTv"])
                            need(v, ("attT", st_))
                            inc("v", v.tensor_copy(attT[st_][:, :], ps_aT[st_][:, :]))
                        d["evv2"] = cnt["v"]
                        guards[("ps_aT", st_)] = ("v", cnt["v"])

                        @block.tensor
                        def _(t, d=d, st_=st_, c=c, j=j):
                            t.wait_ge(sems["v"], d["evv2"])
                            if c == 0:
                                need(t, ("ps_y", st_))
                            ins = None
                            for kk in range(4):
                                ins = t.matmul(ps_y[st_][:, :], attT[st_][:, kk * 128:(kk + 1) * 128],
                                               vlat[:, (4 * c + kk) * 64:(4 * c + kk + 1) * 64],
                                               start=(c == 0 and kk == 0), stop=(c == j and kk == 3))
                            inc("t", ins)
                        d["ytv"] = cnt["t"]
                        guards[("attT", st_)] = ("t", cnt["t"])
                    if c == j:
                        for st_ in (0, 1):
                            guards[("scores", st_)] = ("s", strm[st_]["expv"])

                for st_ in (0, 1):
                    d = strm[st_]
                    nmo = d["nmo"]

                    @block.vector
                    def _(v, d=d, st_=st_, nmo=nmo, ext=ext):
                        v.wait_ge(sems["s"], d["expv"])
                        v.wait_ge(sems["t"], d["ytv"])
                        vseq(v, v.reduce_sum(nm2[:, nmo + 2:nmo + 3], nm2[:, nmo + 8:nmo + 8 + ext], axis=AX))
                        vseq(v, v.reciprocal(nm2[:, nmo + 3:nmo + 4], nm2[:, nmo + 2:nmo + 3]))
                        inc("v", v.tensor_scalar(ybarb[:, st_ * 64:(st_ + 1) * 64], ps_y[st_][:, :],
                                                 nm2[:, nmo + 3:nmo + 4], None, ALU.mult))
                    d["tl1"] = cnt["v"]
                    guards[("ps_y", st_)] = ("v", cnt["v"])
                    guards[("nmg", nmo)] = ("v", cnt["v"])

                    @block.tensor
                    def _(t, d=d, st_=st_):
                        t.wait_ge(sems["v"], d["tl1"])
                        need(t, "ps_bt")
                        inc("t", t.transpose(ps_bt[:, :], ybarb[:, st_ * 64:(st_ + 1) * 64], eyeb[:, :]))
                    d["tl2"] = cnt["t"]

                    @block.vector
                    def _(v, d=d):
                        v.wait_ge(sems["t"], d["tl2"])
                        inc("v", v.tensor_copy(ybT_sb[:, :], ps_bt[:, :]))
                    d["tl3"] = cnt["v"]
                    guards["ps_bt"] = ("v", cnt["v"])

                    @block.tensor
                    def _(t, d=d):
                        t.wait_ge(sems["v"], d["tl3"])
                        need(t, "ps_yh")
                        inc("t", t.matmul(ps_yh[:, :], vexpb[:, d["h"] * 128:(d["h"] + 1) * 128],
                                          ybT_sb[:, :], start=True, stop=True))
                    d["tl4"] = cnt["t"]

                    @block.vector
                    def _(v, d=d, j=j):
                        v.wait_ge(sems["t"], d["tl4"])
                        inc("v", v.tensor_copy(big[:, d["h"] * 1024 + j * 128: d["h"] * 1024 + (j + 1) * 128],
                                               ps_yh[:, :]))
                    guards["ps_yh"] = ("v", cnt["v"])
            set_guard("mask_sb", "v")

        p3.close()
        s3.close()
        open_stacks.remove(p3)
        open_stacks.remove(s3)
        barrier()
        s3p = ExitStack()
        open_stacks.append(s3p)
        pslot = sb(s3p, "pslot", [128, FC * 512], BF16)
        pso = [psmt(s3p, "pso0", [128, 512], F32), psmt(s3p, "pso1", [128, 512], F32)]
        for ccp in range(4):
            @block.gpsimd
            def _(g, ccp=ccp):
                need(g, "pslot")
                g.dma_start(pslot[:, :], proj_d[:, ccp * 8192:(ccp + 1) * 8192]).then_inc(sems["w"], 16)
            cnt["w"] += 16
            PW = cnt["w"]

            for j in range(NT_OWN):
                @block.tensor
                def _(t, j=j, PW=PW):
                    t.wait_ge(sems["w"], PW)
                    wait(t, "v")
                    need(t, ("pso", j % 2))
                    ins = None
                    for hh in range(H):
                        ins = t.matmul(pso[j % 2][:, :],
                                       big[:, hh * 1024 + j * 128: hh * 1024 + (j + 1) * 128],
                                       pslot[:, hh * 512:(hh + 1) * 512],
                                       start=(hh == 0), stop=(hh == H - 1))
                    inc("t", ins)

                @block.vector
                def _(v, j=j, ccp=ccp):
                    wait(v, "t")
                    inc("v", v.tensor_copy(accb[:, j * CD + ccp * 512: j * CD + (ccp + 1) * 512],
                                           pso[j % 2][:, :]))
                set_guard(("pso", j % 2), "v")
            set_guard("pslot", "t")

    if stage >= 3:
        s3p.close()
        ast.close()
        for st_ in (s3p, ast):
            open_stacks.remove(st_)
        s4 = ExitStack()
        open_stacks.append(s4)
        ws = sb(s4, "ws", [128, 2 * 8192], BF16)
        bbg = sb(s4, "bbg", [128, 2 * 512], F32)
        bbu = sb(s4, "bbu", [128, 2 * 512], F32)
        ga = sb(s4, "ga", [128, 512], F32)
        gs = sb(s4, "gs", [128, 512], F32)
        ua = sb(s4, "ua", [128, 512], F32)
        actw = [sb(s4, "actw0", [128, 512], BF16), sb(s4, "actw1", [128, 512], BF16)]
        dbias_sb = sb(s4, "dbias_sb", [16, CD], BF16)
        ltri_sb = sb(s4, "ltri_sb", [128, 128], BF16)
        allon = sb(s4, "allon", [128, 128], BF16)
        onesf = sb(s4, "onesf", [1, 128], F32)
        iota_sb = sb(s4, "iota_sb", [128, 384], F32)
        piotak_sb = sb(s4, "piotak_sb", [128, 4], F32)
        rank_tok = sb(s4, "rank_tok", [128, 128], F32)
        rm1m = sb(s4, "rm1m", [128, 128], F32)
        rm1row = sb(s4, "rm1row", [1, 1024], F32)
        scorow = sb(s4, "scorow", [1, 1024], F32)
        qbuf = sb(s4, "qbuf", [128, 8 * 384], BF16)
        dtb = sb(s4, "dtb", [128, 3 * 512], BF16)
        cmpb = sb(s4, "cmpb", [128, 512], F32)
        onesb = sb(s4, "onesb", [1, 128], BF16)
        actwT = xnT[:, 0:16 * 384]
        gfm = xnT[:, 16 * 384:16 * 384 + 16 * 384]
        qwt = xnT[:, 32 * 384:32 * 384 + 3 * 1024]

        p4 = ExitStack()
        open_stacks.append(p4)
        pg = [psmt(p4, "pg0", [128, 512], F32), psmt(p4, "pg1", [128, 512], F32)]
        pu = [psmt(p4, "pu0", [128, 512], F32), psmt(p4, "pu1", [128, 512], F32)]
        pat = [psmt(p4, "pat0", [128, 512], BF16), psmt(p4, "pat1", [128, 512], BF16)]
        pd = psmt(p4, "pd", [128, 512], F32)
        po = psmt(p4, "po", [128, 512], F32)

        barrier()

        @block.gpsimd
        def _(g):
            g.dma_start(dbias_sb[:, :], dbias_d).then_inc(sems["w"], 16)
            g.dma_start(ltri_sb[:, :], ltri_d).then_inc(sems["w"], 16)
        cnt["w"] += 32

        @block.sync
        def _(s_):
            s_.dma_start(iota_sb[:, :], iota_d).then_inc(sems["l"], 16)
            s_.dma_start(piotak_sb[:, :], piotak_d).then_inc(sems["l"], 16)
        cnt["l"] += 32
        P4_L = cnt["l"]

        @block.vector
        def _(v):
            v.memset(allon[:, :], 1.0)
            v.memset(onesf[:, :], 1.0)
            inc("v", v.memset(onesb[:, :], 1.0))
        P4INIT_V = cnt["v"]

        for j in range(NT_OWN):
            for fcg in range(4):
                pi4 = fcg % 2

                @block.tensor
                def _(t, j=j, fcg=fcg, pi4=pi4):
                    need(t, ("pat", pi4))
                    ins = None
                    for i4 in range(4):
                        fc = fcg * 4 + i4
                        ins = t.transpose(pat[pi4][:, i4 * 128:(i4 + 1) * 128],
                                          xnT[:, fc * 1024 + j * 128: fc * 1024 + (j + 1) * 128],
                                          eyeb[:, :])
                    inc("t", ins)
                XT_T = cnt["t"]

                @block.vector
                def _(v, j=j, fcg=fcg, pi4=pi4, XT_T=XT_T):
                    v.wait_ge(sems["t"], XT_T)
                    inc("v", v.tensor_copy(big[:, j * CD + fcg * 512: j * CD + (fcg + 1) * 512],
                                           pat[pi4][:, :]))
                guards[("pat", pi4)] = ("v", cnt["v"])
        XNTOK_V = cnt["v"]

        for J in range(NT_OWN):
            @block.tensor
            def _(t, J=J):
                t.wait_ge(sems["w"], cnt["w"])
                wait(t, "v")
                need(t, "pd")
                ins = None
                for I in range(J + 1):
                    rhs = maskRb[:, I * 16:(I + 1) * 16]
                    lhs = ltri_sb[:, :] if I == J else allon[:, :]
                    ins = t.matmul(pd[:, 0:16], lhs, rhs, start=(I == 0), stop=(I == J))
                inc("t", ins)
            RK_T = cnt["t"]

            @block.vector
            def _(v, J=J, RK_T=RK_T):
                v.wait_ge(sems["t"], RK_T)
                inc("v", v.tensor_copy(rank_tok[:, J * 16:(J + 1) * 16], pd[:, 0:16]))
            guards["pd"] = ("v", cnt["v"])

        @block.vector
        def _(v):
            wait(v, "v")
            vseq(v, v.scalar_tensor_tensor(rm1m[:, :], rank_tok[:, :], 1.0, maskRf[:, :],
                                           ALU.mult, ALU.mult))
            inc("v", v.tensor_scalar_add(rm1m[:, :], rm1m[:, :], -1.0))
        RM_V = cnt["v"]

        PREP_V = cnt["v"]

        slot_i = [0]

        def load_chunk(src_ap):
            si = slot_i[0] % 2
            slot_i[0] += 1

            @block.gpsimd
            def _(g, si=si):
                need(g, ("ws", si))
                g.dma_start(ws[:, si * 8192:(si + 1) * 8192], src_ap).then_inc(sems["w"], 16)
            cnt["w"] += 16
            return si, cnt["w"]

        for e in range(16):
            for Jg in range(2):
                @block.tensor
                def _(t, e=e, Jg=Jg):
                    wait(t, "v")
                    need(t, "pd")
                    need(t, "po")
                    ins = None
                    for i4 in range(4):
                        J = Jg * 4 + i4
                        t.transpose(pd[0:1, i4 * 128:(i4 + 1) * 128],
                                    rm1m[:, J * 16 + e:J * 16 + e + 1], eyef[:, :])
                        ins = t.transpose(po[0:1, i4 * 128:(i4 + 1) * 128],
                                          scoresP[:, J * 16 + e:J * 16 + e + 1], eyef[:, :])
                    inc("t", ins)
                RW_T = cnt["t"]

                @block.vector
                def _(v, Jg=Jg, RW_T=RW_T):
                    v.wait_ge(sems["t"], RW_T)
                    v.tensor_copy(rm1row[0:1, Jg * 512:(Jg + 1) * 512], pd[0:1, 0:512])
                    inc("v", v.tensor_copy(scorow[0:1, Jg * 512:(Jg + 1) * 512], po[0:1, 0:512]))
                guards["pd"] = ("v", cnt["v"])
                guards["po"] = ("v", cnt["v"])
            ROW_V = cnt["v"]

            @block.tensor
            def _(t, e=e, ROW_V=ROW_V):
                t.wait_ge(sems["v"], ROW_V)
                need(t, ("pg", 0))
                need(t, ("pg", 1))
                need(t, ("pu", 0))
                need(t, ("pu", 1))
                t.matmul(pg[0][:, :], onesf[:, :], rm1row[0:1, 0:512], start=True, stop=True)
                t.matmul(pg[1][:, :], onesf[:, :], rm1row[0:1, 512:1024], start=True, stop=True)
                t.matmul(pu[0][:, :], onesf[:, :], scorow[0:1, 0:512], start=True, stop=True)
                inc("t", t.matmul(pu[1][:, :], onesf[:, :], scorow[0:1, 512:1024],
                                  start=True, stop=True))
            BC_T = cnt["t"]

            @block.vector
            def _(v, e=e, BC_T=BC_T):
                v.wait_ge(sems["t"], BC_T)
                need(v, "qwt")
                for k in range(3):
                    for hf in range(2):
                        vseq(v, v.tensor_scalar(cmpb[:, :], pg[hf][:, :],
                                                piotak_sb[:, k:k + 1], 0.0,
                                                ALU.subtract, ALU.is_equal))
                        v.tensor_mul(qwt[:, k * 1024 + hf * 512: k * 1024 + (hf + 1) * 512],
                                     cmpb[:, :], pu[hf][:, :])
                inc("v", v.engine_nop())
            guards[("pg", 0)] = ("v", cnt["v"])
            guards[("pg", 1)] = ("v", cnt["v"])
            guards[("pu", 0)] = ("v", cnt["v"])
            guards[("pu", 1)] = ("v", cnt["v"])

            @block.vector
            def _(v, e=e):
                v.wait_ge(sems["l"], P4_L)
                need(v, "qbuf")
                ins = None
                for J in range(NT_OWN):
                    ins = v.tensor_scalar(qbuf[:, J * 384:(J + 1) * 384], iota_sb[:, :],
                                          rm1m[:, J * 16 + e:J * 16 + e + 1], 0.0,
                                          ALU.subtract, ALU.is_equal)
                inc("v", ins)
            QB_V = cnt["v"]

            for fc in range(FC):
                gi = fc % 2

                @block.tensor
                def _(t, e=e, fc=fc, gi=gi, QB_V=QB_V):
                    t.wait_ge(sems["v"], max(QB_V, XNTOK_V))
                    need(t, ("pg", gi))
                    ins = None
                    for J in range(NT_OWN):
                        ins = t.matmul(pg[gi][:, 0:384],
                                       big[:, J * CD + fc * 128: J * CD + (fc + 1) * 128],
                                       qbuf[:, J * 384:(J + 1) * 384],
                                       start=(J == 0), stop=(J == NT_OWN - 1))
                    inc("t", ins)
                GA_T = cnt["t"]

                @block.vector
                def _(v, fc=fc, gi=gi, GA_T=GA_T):
                    v.wait_ge(sems["t"], GA_T)
                    inc("v", v.tensor_copy(gfm[:, fc * 384:(fc + 1) * 384], pg[gi][:, 0:384]))
                guards[("pg", gi)] = ("v", cnt["v"])
            guards["qbuf"] = ("t", cnt["t"])
            GF_V = cnt["v"]

            for dc in range(4):
                sg, _wn = load_chunk(wg_d[e, :, dc * 8192:(dc + 1) * 8192])
                su, wu_n = load_chunk(wu_d[e, :, dc * 8192:(dc + 1) * 8192])
                dcp = dc % 2

                @block.sync
                def _(s_, e=e, dc=dc, dcp=dcp):
                    need(s_, ("bb", dcp))
                    s_.dma_start(bbg[:, dcp * 512:(dcp + 1) * 512],
                                 bass.AP(biasg_d.tensor, e * CD + dc * 512, [[0, 128], [1, 512]])).then_inc(sems["l"], 16)
                    s_.dma_start(bbu[:, dcp * 512:(dcp + 1) * 512],
                                 bass.AP(biasu_d.tensor, e * CD + dc * 512, [[0, 128], [1, 512]])).then_inc(sems["l"], 16)
                cnt["l"] += 32
                BB_L = cnt["l"]

                def emit_aT(t, kp):
                    ti = kp % 2
                    need(t, ("pat", ti))
                    ins = None
                    for db in range(4):
                        ins = t.transpose(pat[ti][:, db * 128:(db + 1) * 128],
                                          actw[ti][:, db * 128:(db + 1) * 128], eyeb[:, :])
                    return ins

                def emit_aT_evac(dc, kp):
                    ti = kp % 2

                    @block.vector
                    def _(v, dc=dc, kp=kp, ti=ti):
                        wait(v, "t")
                        bv = actwT.rearrange("p (D kt t) -> p D kt t", D=FC, kt=3)
                        inc("v", v.tensor_copy(bv[:, dc * 4:(dc + 1) * 4, kp, :],
                                               pat[ti][:, :].rearrange("p (db t) -> p db t", db=4)))
                    guards[("pat", ti)] = ("v", cnt["v"])
                    guards[("actw", ti)] = ("v", cnt["v"])

                for k in range(3):
                    ti = k % 2

                    @block.tensor
                    def _(t, k=k, ti=ti, sg=sg, su=su, wu_n=wu_n, GF_V=GF_V):
                        t.wait_ge(sems["w"], wu_n)
                        t.wait_ge(sems["v"], GF_V)
                        need(t, ("pg", ti))
                        need(t, ("pu", ti))
                        for fc in range(FC):
                            t.matmul(pg[ti][:, :],
                                     gfm[:, fc * 384 + k * 128: fc * 384 + (k + 1) * 128],
                                     ws[:, sg * 8192 + fc * 512: sg * 8192 + (fc + 1) * 512],
                                     start=(fc == 0), stop=(fc == FC - 1))
                        ins = None
                        for fc in range(FC):
                            ins = t.matmul(pu[ti][:, :],
                                           gfm[:, fc * 384 + k * 128: fc * 384 + (k + 1) * 128],
                                           ws[:, su * 8192 + fc * 512: su * 8192 + (fc + 1) * 512],
                                           start=(fc == 0), stop=(fc == FC - 1))
                        if k > 0:
                            ins = emit_aT(t, k - 1)
                        inc("t", ins)
                    if k > 0:
                        emit_aT_evac(dc, k - 1)
                    if k == 2:
                        set_guard(("ws", sg), "t")
                        set_guard(("ws", su), "t")

                    @block.vector
                    def _(v, ti=ti, dcp=dcp, BB_L=BB_L):
                        wait(v, "t")
                        v.wait_ge(sems["l"], BB_L)
                        need(v, ("actw", ti))
                        vseq(v, v.tensor_add(ga[:, :], pg[ti][:, :], bbg[:, dcp * 512:(dcp + 1) * 512]))
                        inc("v", v.tensor_scalar_min(ga[:, :], ga[:, :], LIMIT))
                    guards[("pg", ti)] = ("v", cnt["v"])

                    @block.scalar
                    def _(sc):
                        wait(sc, "v")
                        inc("s", sc.activation(gs[:, :], ga[:, :], AF.Silu, scale=KAPPA))

                    @block.vector
                    def _(v, ti=ti, dcp=dcp):
                        wait(v, "s")
                        v.tensor_add(ua[:, :], pu[ti][:, :], bbu[:, dcp * 512:(dcp + 1) * 512])
                        vseq(v, v.tensor_scalar(ua[:, :], ua[:, :], LIMIT, -LIMIT, ALU.min, ALU.max))
                        inc("v", v.scalar_tensor_tensor(actw[ti][:, :], ua[:, :], 1.0, gs[:, :],
                                                        ALU.add, ALU.mult))
                    guards[("pu", ti)] = ("v", cnt["v"])
                    if k == 2:
                        set_guard(("bb", dcp), "v")

                @block.tensor
                def _(t):
                    wait(t, "v")
                    inc("t", emit_aT(t, 2))
                emit_aT_evac(dc, 2)

            ACT_DONE_V = cnt["v"]
            for hc in range(4):
                sd, wd_n = load_chunk(down_d[e, :, hc * 8192:(hc + 1) * 8192])
                for k in range(3):
                    @block.tensor
                    def _(t, k=k, sd=sd, wd_n=wd_n, ACT_DONE_V=ACT_DONE_V):
                        t.wait_ge(sems["w"], wd_n)
                        t.wait_ge(sems["v"], ACT_DONE_V)
                        need(t, "pd")
                        ins = None
                        for D in range(FC):
                            ins = t.matmul(pd[:, :],
                                           actwT[:, D * 384 + k * 128: D * 384 + (k + 1) * 128],
                                           ws[:, sd * 8192 + D * 512: sd * 8192 + (D + 1) * 512],
                                           start=(D == 0), stop=(D == FC - 1))
                        inc("t", ins)
                    DW_T = cnt["t"]
                    if k == 2:
                        set_guard(("ws", sd), "t")

                    @block.vector
                    def _(v, k=k, DW_T=DW_T):
                        v.wait_ge(sems["t"], DW_T)
                        need(v, ("dtb", k))
                        inc("v", v.tensor_copy(dtb[:, k * 512:(k + 1) * 512], pd[:, :]))
                    guards["pd"] = ("v", cnt["v"])
                DT_V = cnt["v"]

                for t8 in range(NT_OWN):
                    @block.tensor
                    def _(t, t8=t8, DT_V=DT_V):
                        t.wait_ge(sems["v"], DT_V)
                        need(t, "po")
                        ins = None
                        for k in range(3):
                            ins = t.matmul(po[:, :],
                                           qwt[:, k * 1024 + t8 * 128: k * 1024 + (t8 + 1) * 128],
                                           dtb[:, k * 512:(k + 1) * 512],
                                           start=(k == 0), stop=(k == 2))
                        inc("t", ins)
                    SC_T = cnt["t"]
                    if t8 == NT_OWN - 1:
                        for k in range(3):
                            guards[("dtb", k)] = ("t", cnt["t"])

                    @block.vector
                    def _(v, t8=t8, hc=hc, SC_T=SC_T):
                        v.wait_ge(sems["t"], SC_T)
                        dst = accb[:, t8 * CD + hc * 512: t8 * CD + (hc + 1) * 512]
                        inc("v", v.tensor_add(dst, dst, po[:, :]))
                    guards["po"] = ("v", cnt["v"])
            guards["qwt"] = ("t", cnt["t"])

        for t8 in range(8):
            for hc in range(4):
                @block.tensor
                def _(t, t8=t8, hc=hc):
                    wait(t, "w", "v")
                    need(t, "pd")
                    inc("t", t.matmul(pd[:, :], scoresT[:, t8 * 128:(t8 + 1) * 128],
                                      dbias_sb[:, hc * 512:(hc + 1) * 512], start=True, stop=True))
                DB_T = cnt["t"]

                @block.vector
                def _(v, t8=t8, hc=hc, DB_T=DB_T):
                    v.wait_ge(sems["t"], DB_T)
                    dst = accb[:, t8 * CD + hc * 512: t8 * CD + (hc + 1) * 512]
                    inc("v", v.tensor_add(dst, dst, pd[:, :]))
                guards["pd"] = ("v", cnt["v"])

    if stage >= 3:
        p4.close()
        s4.close()
        open_stacks.remove(p4)
        open_stacks.remove(s4)
        barrier()
        s5 = ExitStack()
        open_stacks.append(s5)
        xf = sb(s5, "xf", [128, 2 * CD], F32)
        ob = sb(s5, "ob", [128, 2 * CD], F32)

        for j in range(NT_OWN):
            half = j % 2

            @block.sync
            def _(s, j=j, half=half):
                need(s, ("xf", half))
                s.dma_start(xf[:, half * CD:(half + 1) * CD],
                            x_own[j * 128:(j + 1) * 128, :]).then_inc(sems["l"], 16)
            cnt["l"] += 16

            @block.vector
            def _(v, j=j, half=half):
                wait(v, "l")
                need(v, ("ob", half))
                inc("v", v.tensor_add(ob[:, half * CD:(half + 1) * CD],
                                      accb[:, j * CD:(j + 1) * CD],
                                      xf[:, half * CD:(half + 1) * CD]))
            set_guard(("xf", half), "v")

            @block.sync
            def _(s, j=j, half=half):
                wait(s, "v")
                s.dma_start(out_d[j * 128:(j + 1) * 128, :],
                            ob[:, half * CD:(half + 1) * CD]).then_inc(sems["o"], 16)
            cnt["o"] += 16
            set_guard(("ob", half), "o")

        @block.sync
        def _(s):
            s.wait_ge(sems["o"], cnt["o"])

    if dbg:
        barrier()
        avail = {}
        if stage < 3:
            avail.update({
                "d_klatT": (klatT[:, :], True), "d_vlat": (vlat[:, :], True),
                "d_qtl": (qtl[:, :], True),
            })
        avail.update({
            "d_xnT": (xnT[:, :], True),
            "d_scoresP": (scoresP[:, :], False),
            "d_scoresT": (scoresT[:, :], True),
            "d_acc": (accb[:, :], False),
            "d_big": (big[:, :], True),
        })
        for name in dbg:
            if name not in avail:
                continue
            src_ap, cast = avail[name]
            if cast:
                @block.gpsimd
                def _(g, name=name, src_ap=src_ap):
                    g.dma_start(dbg_d[name], src_ap).then_inc(sems["w"], 16)
                cnt["w"] += 16
            else:
                @block.sync
                def _(s, name=name, src_ap=src_ap):
                    s.dma_start(dbg_d[name], src_ap).then_inc(sems["l"], 16)
                cnt["l"] += 16

        @block.sync
        def _(s):
            s.wait_ge(sems["l"], cnt["l"])

        @block.gpsimd
        def _(g):
            g.wait_ge(sems["w"], cnt["w"])

    for st_ in reversed(open_stacks):
        st_.close()
    return nc


def tile_w(w):
    n = w.shape[1]
    a = w.reshape(16, 128, n // 512, 512)
    a = a.transpose(1, 2, 0, 3)
    return np.ascontiguousarray(a.reshape(128, -1))


def prep_inputs(ii):
    x = np.asarray(ii["L0_attn_x"], np.float32)
    wq = np.asarray(ii["L0_attn_wq"], np.float32)
    wk = np.asarray(ii["L0_attn_wk_lat"], np.float32)
    wv = np.asarray(ii["L0_attn_wv_lat"], np.float32)
    kexpand = np.asarray(ii["L0_attn_k_expand"], np.float32)
    vexpand = np.asarray(ii["L0_attn_v_expand"], np.float32)
    projw = np.asarray(ii["L0_attn_proj_w"], np.float32)
    rw = np.asarray(ii["L0_ffn_router_weight"], np.float32)
    rb = np.asarray(ii["L0_ffn_router_bias"], np.float32)
    gup = np.asarray(ii["L0_ffn_gate_up_proj"], np.float32)
    gupb = np.asarray(ii["L0_ffn_gate_up_proj_bias"], np.float32)
    dw = np.asarray(ii["L0_ffn_down_proj"], np.float32)
    db = np.asarray(ii["L0_ffn_down_proj_bias"], np.float32)

    P128 = np.concatenate([np.arange(0, 128, 2), np.arange(1, 128, 2)])
    P64 = np.concatenate([np.arange(0, 64, 2), np.arange(1, 64, 2)])

    shared = {
        "jrow": np.broadcast_to(np.arange(64, dtype=np.float32), (128, 64)).copy(),
        "eye_f": np.eye(128, dtype=np.float32),
        "pos_ctx": (np.arange(32, dtype=np.float32)[None, :] * 128
                    + np.arange(128, dtype=np.float32)[:, None]).astype(np.float32),
        "wq_t": tile_w(wq),
        "kvw_t": np.ascontiguousarray(np.concatenate([wk, wv], axis=1).reshape(16, 128, 128).transpose(1, 0, 2).reshape(128, 2048)),
        "kexp_t": np.ascontiguousarray(
            kexpand.transpose(0, 2, 1)[:, P128][:, :, P64].transpose(1, 0, 2).reshape(128, 16 * 64)),
        "vexp_t": np.ascontiguousarray(vexpand.transpose(1, 0, 2).reshape(64, 16 * 128)),
        "proj_t": tile_w(projw),
        "rwT_t": np.ascontiguousarray(rw.T.reshape(16, 128, 16).transpose(1, 0, 2).reshape(128, 256)),
        "rbias": rb.reshape(1, 16).copy(),
        "wg_t": np.stack([tile_w(np.ascontiguousarray(gup[e][:, 0::2])) for e in range(16)]),
        "wu_t": np.stack([tile_w(np.ascontiguousarray(gup[e][:, 1::2])) for e in range(16)]),
        "biasg": np.ascontiguousarray(gupb[:, 0::2]),
        "biasu": np.ascontiguousarray(gupb[:, 1::2]),
        "down_t": np.stack([tile_w(dw[e]) for e in range(16)]),
        "dbias": db.copy(),
        "ltri": np.triu(np.ones((128, 128), np.float32)),
        "iota384": np.broadcast_to(np.arange(384, dtype=np.float32), (128, 384)).copy(),
        "piotak": (np.arange(128, dtype=np.float32)[:, None]
                   + np.array([0, 128, 256, 0], np.float32)[None, :]).copy(),
    }

    in_maps = []
    own_tiles = []
    for c in range(8):
        b, cp = c // 4, c % 4
        tiles = [4 * jj + cp for jj in range(8)]
        own_tiles.append(tiles)
        rows = np.concatenate([np.arange(g * 128, (g + 1) * 128) for g in tiles])
        mask = np.zeros((128, 4096), np.float32)
        for jj, g in enumerate(tiles):
            gq = g * 128 + np.arange(128)[:, None]
            gk = jj * 512 + np.arange(512)[None, :]
            mask[:, jj * 512:(jj + 1) * 512] = np.where(gk <= gq, 0.0, NEG)
        posn = (np.array([g * 128 for g in tiles], np.float32)[None, :]
                + np.arange(128, dtype=np.float32)[:, None])
        m = dict(shared)
        m["x_ctx"] = np.ascontiguousarray(x[b])
        m["x_own"] = np.ascontiguousarray(x[b][rows])
        m["maskd"] = mask
        m["pos_own"] = posn.astype(np.float32)
        in_maps.append(m)
    return in_maps, own_tiles


_CACHE = {}


def kernel(**inputs):
    in_maps, own_tiles = prep_inputs(inputs)
    if "nc" not in _CACHE:
        _CACHE["nc"] = build(stage=3, dbg=())
    res = run_bass_kernel_spmd(_CACHE["nc"], in_maps, core_ids=list(range(8)))
    out = np.zeros((2, 4096, 2048), np.float32)
    for c in range(8):
        b = c // 4
        o = res.results[c]["out"].reshape(8, 128, 2048)
        for jj, g in enumerate(own_tiles[c]):
            out[b, g * 128:(g + 1) * 128, :] = o[jj]
    return out


# version 07
# speedup vs baseline: 1.1223x; 1.0516x over previous
import math
import numpy as np
from contextlib import ExitStack

import concourse.bass as bass
import concourse.mybir as mybir
from concourse.bass_utils import run_bass_kernel_spmd

F32 = mybir.dt.float32
BF16 = mybir.dt.bfloat16
I32 = mybir.dt.int32
ALU = mybir.AluOpType
AF = mybir.ActivationFunctionType
AX = mybir.AxisListType.X

PI = math.pi
LN1E4 = math.log(10000.0)
SCALE = 1.0 / math.sqrt(128.0)
KAPPA = 1.702
LIMIT = 7.0
NEG = -1.0e30
EPS = 1e-5

NT_CTX, NT_OWN, FC, H, CD = 32, 8, 16, 16, 2048


def build(stage=3, dbg=()):
    nc = bass.Bass("TRN2", target_bir_lowering=False)

    x_ctx = nc.dram_tensor("x_ctx", [4096, CD], F32, kind="ExternalInput").ap()
    x_own = nc.dram_tensor("x_own", [1024, CD], F32, kind="ExternalInput").ap()
    maskd = nc.dram_tensor("maskd", [128, 4096], F32, kind="ExternalInput").ap()
    pos_own = nc.dram_tensor("pos_own", [128, 8], F32, kind="ExternalInput").ap()
    pos_ctx = nc.dram_tensor("pos_ctx", [128, 32], F32, kind="ExternalInput").ap()
    jrow = nc.dram_tensor("jrow", [128, 64], F32, kind="ExternalInput").ap()
    eye_f = nc.dram_tensor("eye_f", [128, 128], F32, kind="ExternalInput").ap()
    wq_d = nc.dram_tensor("wq_t", [128, FC * CD], F32, kind="ExternalInput").ap()
    kvw_d = nc.dram_tensor("kvw_t", [128, FC * 128], F32, kind="ExternalInput").ap()
    kexp_d = nc.dram_tensor("kexp_t", [128, H * 64], F32, kind="ExternalInput").ap()
    vexp_d = nc.dram_tensor("vexp_t", [64, H * 128], F32, kind="ExternalInput").ap()
    proj_d = nc.dram_tensor("proj_t", [128, 4 * FC * 512], F32, kind="ExternalInput").ap()
    rwT_d = nc.dram_tensor("rwT_t", [128, FC * 16], F32, kind="ExternalInput").ap()
    rbias_d = nc.dram_tensor("rbias", [1, 16], F32, kind="ExternalInput").ap()
    wg_d = nc.dram_tensor("wg_t", [16, 128, 4 * FC * 512], F32, kind="ExternalInput").ap()
    wu_d = nc.dram_tensor("wu_t", [16, 128, 4 * FC * 512], F32, kind="ExternalInput").ap()
    biasg_d = nc.dram_tensor("biasg", [16, CD], F32, kind="ExternalInput").ap()
    biasu_d = nc.dram_tensor("biasu", [16, CD], F32, kind="ExternalInput").ap()
    down_d = nc.dram_tensor("down_t", [16, 128, 4 * FC * 512], F32, kind="ExternalInput").ap()
    dbias_d = nc.dram_tensor("dbias", [16, CD], F32, kind="ExternalInput").ap()
    ltri_d = nc.dram_tensor("ltri", [128, 128], F32, kind="ExternalInput").ap()
    iota_d = nc.dram_tensor("iota384", [128, 384], F32, kind="ExternalInput").ap()
    piotak_d = nc.dram_tensor("piotak", [128, 4], F32, kind="ExternalInput").ap()
    out_d = nc.dram_tensor("out", [1024, CD], F32, kind="ExternalOutput").ap()

    dbg_d = {}
    for name, shape in [
        ("d_klatT", [64, 4096]), ("d_vlat", [128, NT_CTX * 64]),
        ("d_xnT", [128, FC * 1024]), ("d_qtl", [128, 8 * 1024]),
        ("d_scoresP", [128, 128]), ("d_scoresT", [16, 1024]),
        ("d_acc", [128, NT_OWN * CD]), ("d_big", [128, FC * 1024]),
    ]:
        if name in dbg:
            dbg_d[name] = nc.dram_tensor(name, shape, F32, kind="ExternalOutput").ap()

    gst = ExitStack()
    open_stacks = [gst]

    def sb(stk, name, shape, dt):
        return stk.enter_context(nc.sbuf_tensor(name, shape, dt))

    def psmt(stk, name, shape, dt):
        return stk.enter_context(nc.psum_tensor(name, shape, dt))

    xnT = sb(gst, "xnT", [128, FC * 1024], BF16)
    accb = sb(gst, "accb", [128, NT_OWN * CD], F32)
    big = sb(gst, "big", [128, FC * 1024], BF16)
    eyeb = sb(gst, "eyeb", [128, 128], BF16)
    eyef = sb(gst, "eyef", [128, 128], F32)
    scoresP = sb(gst, "scoresP", [128, 128], F32)
    scoresR = sb(gst, "scoresR", [128, 16], F32)
    scoresT = sb(gst, "scoresT", [16, 1024], BF16)
    rbias_bc = sb(gst, "rbias_bc", [128, 16], F32)
    maskRf = sb(gst, "maskRf", [128, 128], F32)
    maskRb = sb(gst, "maskRb", [128, 128], BF16)

    sems = {}
    for k in ("l", "w", "o", "t", "v", "s"):
        sems[k] = gst.enter_context(nc.semaphore("sem_" + k))
    cnt = {k: 0 for k in sems}

    block = gst.enter_context(nc.Block())

    def inc(key, ins, k=1):
        cnt[key] += k
        ins.then_inc(sems[key], k)
        return ins

    guards = {}

    def need(e, key):
        if key in guards:
            k, v = guards[key]
            if v > 0:
                e.wait_ge(sems[k], v)

    def set_guard(key, k):
        guards[key] = (k, cnt[k])

    def vseq(v, ins):
        inc("v", ins)
        v.wait_ge(sems["v"], cnt["v"])
        return ins

    def wait(e, *keys):
        for k in keys:
            if cnt[k] > 0:
                e.wait_ge(sems[k], cnt[k])

    def barrier():
        vals = dict(cnt)

        def wa(e):
            for k, v in vals.items():
                if v > 0:
                    e.wait_ge(sems[k], v)
        for name in ("sync", "gpsimd", "tensor", "vector", "scalar"):
            getattr(block, name)(wa)

    ast = ExitStack()
    open_stacks.append(ast)
    klatT = sb(ast, "klatT", [128, 4096], BF16)
    vlat = sb(ast, "vlat", [128, NT_CTX * 64], BF16)
    qtl = sb(ast, "qtl", [128, 8 * 1024], BF16)
    kexp = sb(ast, "kexp", [128, H * 64], BF16)
    vexpb = sb(ast, "vexpb", [64, H * 128], BF16)


    s12 = ExitStack()
    open_stacks.append(s12)
    kvw = sb(s12, "kvw", [128, FC * 128], BF16)
    rwT = sb(s12, "rwT", [128, FC * 16], BF16)
    invf = sb(s12, "invf", [128, 64], F32)
    invfk = sb(s12, "invfk", [128, 32], F32)
    jrow_sb = sb(s12, "jrow_sb", [128, 64], F32)
    poso = sb(s12, "poso", [128, 8], F32)
    posc = sb(s12, "posc", [128, 32], F32)
    red = sb(s12, "red", [128, 64], F32)
    kib = sb(s12, "kib", [128, 64], I32)
    cosqa = sb(s12, "cosqa", [128, 512], F32)
    sinqa = sb(s12, "sinqa", [128, 512], F32)
    cosk = sb(s12, "cosk", [128, 32], F32)
    sink = sb(s12, "sink", [128, 32], F32)
    lnst = sb(s12, "lnst", [128, 16], F32)
    tA = sb(s12, "tA", [128, 128], F32)
    tB = sb(s12, "tB", [128, 128], F32)
    xt = sb(s12, "xt", [128, 2 * CD], F32)
    xnt = sb(s12, "xnt", [128, CD], BF16)
    xnTs = sb(s12, "xnTs", [128, CD], BF16)
    qro4 = sb(s12, "qro4", [128, 512], BF16)
    qT4 = sb(s12, "qT4", [128, 512], BF16)
    kro = sb(s12, "kro", [128, 64], BF16)


    @block.sync
    def _(s):
        s.dma_start(eyef[:, :], eye_f).then_inc(sems["l"], 16)
        s.dma_start(jrow_sb[:, :], jrow).then_inc(sems["l"], 16)
        s.dma_start(poso[:, :], pos_own).then_inc(sems["l"], 16)
        s.dma_start(posc[:, :], pos_ctx).then_inc(sems["l"], 16)
        s.dma_start(rbias_bc[:, :], bass.AP(rbias_d.tensor, 0, [[0, 128], [1, 16]])).then_inc(sems["l"], 16)
    cnt["l"] += 16 * 5
    P0_L = cnt["l"]

    @block.scalar
    def _(sc):
        sc.wait_ge(sems["l"], P0_L)
        sc.activation(invf[:, :], jrow_sb[:, :], AF.Exp, scale=-LN1E4 / 64.0)
        inc("s", sc.activation(invfk[:, :], jrow_sb[:, :32], AF.Exp, scale=-LN1E4 / 32.0))

    def sincos(pos_col, invf_ap, n, cos_dst, sin_dst):
        @block.vector
        def _(v):
            wait(v, "s", "l")
            vseq(v, v.tensor_scalar(tA[:, :n], invf_ap, pos_col, None, ALU.mult))
            vseq(v, v.tensor_scalar_mul(red[:, :n], tA[:, :n], 1.0 / (2 * PI)))
            inc("v", v.tensor_copy(kib[:, :n], red[:, :n]))

        @block.scalar
        def _(sc):
            wait(sc, "v")
            inc("s", sc.copy(red[:, :n], kib[:, :n]))

        @block.vector
        def _(v):
            wait(v, "s")
            inc("v", v.scalar_tensor_tensor(red[:, :n], red[:, :n], -2 * PI, tA[:, :n], ALU.mult, ALU.add))

        @block.scalar
        def _(sc):
            wait(sc, "v")
            inc("s", sc.activation(sin_dst, red[:, :n], AF.Sin))

        @block.vector
        def _(v):
            wait(v, "s")
            vseq(v, v.tensor_scalar_add(tB[:, :n], tA[:, :n], PI / 2))
            vseq(v, v.tensor_scalar_mul(red[:, :n], tB[:, :n], 1.0 / (2 * PI)))
            inc("v", v.tensor_copy(kib[:, :n], red[:, :n]))

        @block.scalar
        def _(sc):
            wait(sc, "v")
            inc("s", sc.copy(red[:, :n], kib[:, :n]))

        @block.vector
        def _(v):
            wait(v, "s")
            inc("v", v.scalar_tensor_tensor(red[:, :n], red[:, :n], -2 * PI, tB[:, :n], ALU.mult, ALU.add))

        @block.scalar
        def _(sc):
            wait(sc, "v")
            inc("s", sc.activation(cos_dst, red[:, :n], AF.Sin))

    @block.gpsimd
    def _(g):
        g.dma_start(eyeb[:, :], eye_f).then_inc(sems["w"], 16)
        g.dma_start(kvw[:, :], kvw_d).then_inc(sems["w"], 16)
        g.dma_start(kexp[:, :], kexp_d).then_inc(sems["w"], 16)
        g.dma_start(vexpb[:, :], vexp_d).then_inc(sems["w"], 16)
        g.dma_start(rwT[:, :], rwT_d).then_inc(sems["w"], 16)
    cnt["w"] += 16 * 5
    P0_W = cnt["w"]

    p12 = ExitStack()
    open_stacks.append(p12)
    pk0 = psmt(p12, "pk0", [128, 1024], BF16)
    pk1 = psmt(p12, "pk1", [128, 1024], BF16)
    pq0 = psmt(p12, "pq0", [128, 512], F32)
    pq1 = psmt(p12, "pq1", [128, 512], F32)
    pqt = psmt(p12, "pqt", [128, 128], F32)
    prt = psmt(p12, "prt", [128, 128], F32)
    ptrash = psmt(p12, "ptrash", [128, 512], F32)

    def ln_tile(half):
        xv = xt[:, half * CD:(half + 1) * CD]

        @block.vector
        def _(v):
            wait(v, "l")
            inc("v", v.reduce_sum(lnst[:, 0:1], xv, axis=AX))

        @block.scalar
        def _(sc):
            wait(sc, "l")
            ins = None
            for c4 in range(4):
                ins = sc.activation(ptrash[:, :], xv[:, c4 * 512:(c4 + 1) * 512], AF.Square,
                                    accum_out=lnst[:, 8 + c4:9 + c4])
            inc("s", ins)

        @block.vector
        def _(v):
            wait(v, "s", "v")
            vseq(v, v.tensor_scalar_mul(lnst[:, 1:2], lnst[:, 0:1], 1.0 / CD))
            vseq(v, v.reduce_sum(lnst[:, 2:3], lnst[:, 8:12], axis=AX))
            vseq(v, v.tensor_scalar(lnst[:, 3:4], lnst[:, 2:3], 1.0 / CD, EPS, ALU.mult, ALU.add))
            vseq(v, v.tensor_mul(lnst[:, 4:5], lnst[:, 1:2], lnst[:, 1:2]))
            inc("v", v.scalar_tensor_tensor(lnst[:, 5:6], lnst[:, 4:5], -1.0, lnst[:, 3:4],
                                            ALU.mult, ALU.add))

        @block.scalar
        def _(sc):
            wait(sc, "v")
            sc.activation(lnst[:, 6:7], lnst[:, 5:6], AF.Ln)
            inc("s", sc.activation(lnst[:, 7:8], lnst[:, 6:7], AF.Exp, scale=-0.5))

        @block.vector
        def _(v):
            wait(v, "s")
            inc("v", v.tensor_scalar(xnt[:, :], xv, lnst[:, 1:2], lnst[:, 7:8],
                                     ALU.subtract, ALU.mult))

    def xn_transposes(own_j):
        @block.tensor
        def _(t):
            wait(t, "v")
            t.wait_ge(sems["w"], 16)
            need(t, "pk")
            ins = None
            for fc in range(FC):
                pk = pk0 if fc < 8 else pk1
                ins = t.transpose(pk[:, (fc % 8) * 128:(fc % 8 + 1) * 128],
                                  xnt[:, fc * 128:(fc + 1) * 128], eyeb[:, :])
            inc("t", ins)

        @block.vector
        def _(v):
            wait(v, "t")
            pv0 = pk0[:, :].rearrange("p (f t) -> p f t", f=8)
            pv1 = pk1[:, :].rearrange("p (f t) -> p f t", f=8)
            if own_j is None:
                v.tensor_copy(xnTs[:, 0:1024], pk0[:, :])
                inc("v", v.tensor_copy(xnTs[:, 1024:2048], pk1[:, :]))
            else:
                xv = xnT[:, :].rearrange("p (f jt t) -> p f jt t", f=FC, jt=8)
                v.tensor_copy(xv[:, 0:8, own_j, :], pv0)
                inc("v", v.tensor_copy(xv[:, 8:16, own_j, :], pv1))
        set_guard("pk", "v")

    if stage >= 1:
        for tau in range(NT_CTX):
            half = tau % 2

            @block.sync
            def _(s, tau=tau, half=half):
                need(s, ("xt", half))
                s.dma_start(xt[:, half * CD:(half + 1) * CD],
                            x_ctx[tau * 128:(tau + 1) * 128, :]).then_inc(sems["l"], 16)
            cnt["l"] += 16
            ln_tile(half)
            set_guard(("xt", half), "v")
            xn_transposes(None)

            @block.tensor
            def _(t):
                wait(t, "v")
                t.wait_ge(sems["w"], 32)
                need(t, ("pq", 0))
                ins = None
                for fc in range(FC):
                    ins = t.matmul(pq0[:, 0:128], xnTs[:, fc * 128:(fc + 1) * 128],
                                   kvw[:, fc * 128:(fc + 1) * 128],
                                   start=(fc == 0), stop=(fc == FC - 1))
                inc("t", ins)

            sincos(posc[:, tau:tau + 1], invfk[:, :], 32, cosk[:, :], sink[:, :])

            @block.vector
            def _(v, tau=tau):
                wait(v, "t", "s")
                v.tensor_mul(tA[:, 0:32], pq0[:, 0:64:2], cosk[:, :])
                vseq(v, v.tensor_mul(tB[:, 0:32], pq0[:, 1:64:2], sink[:, :]))
                v.tensor_sub(kro[:, 0:32], tA[:, 0:32], tB[:, 0:32])
                v.tensor_mul(tA[:, 32:64], pq0[:, 0:64:2], sink[:, :])
                vseq(v, v.tensor_mul(tB[:, 32:64], pq0[:, 1:64:2], cosk[:, :]))
                v.tensor_add(kro[:, 32:64], tA[:, 32:64], tB[:, 32:64])
                inc("v", v.tensor_copy(vlat[:, tau * 64:(tau + 1) * 64], pq0[:, 64:128]))
            set_guard(("pq", 0), "v")

            @block.tensor
            def _(t):
                wait(t, "v")
                need(t, "pk")
                t.transpose(pk0[0:64, 0:128], kro[:, :], eyeb[:, :])
                inc("t", t.transpose(pk0[64:128, 0:128], kro[:, :], eyeb[:, :],
                                     tile_position=(0, 64)))

            @block.vector
            def _(v, tau=tau):
                wait(v, "t")
                inc("v", v.tensor_copy(klatT[:, tau * 128:(tau + 1) * 128], pk0[:, 0:128]))
            set_guard("pk", "v")

    if stage >= 1:
        for j in range(NT_OWN):
            half = j % 2

            @block.sync
            def _(s, j=j, half=half):
                need(s, ("xt", half))
                s.dma_start(xt[:, half * CD:(half + 1) * CD],
                            x_own[j * 128:(j + 1) * 128, :]).then_inc(sems["l"], 16)
            cnt["l"] += 16
            ln_tile(half)
            set_guard(("xt", half), "v")
            xn_transposes(own_j=j)

            sincos(poso[:, j:j + 1], invf[:, :], 64,
                   cosqa[:, j * 64:(j + 1) * 64], sinqa[:, j * 64:(j + 1) * 64])

            @block.tensor
            def _(t, j=j):
                t.wait_ge(sems["w"], P0_W)
                wait(t, "v")
                need(t, "prt")
                ins = None
                for fc in range(FC):
                    ins = t.matmul(prt[:, 0:16], xnT[:, fc * 1024 + j * 128: fc * 1024 + (j + 1) * 128],
                                   rwT[:, fc * 16:(fc + 1) * 16],
                                   start=(fc == 0), stop=(fc == FC - 1))
                inc("t", ins)

            @block.vector
            def _(v):
                wait(v, "t", "v")
                v.wait_ge(sems["l"], P0_L)
                lg, cur = tA[:, 0:16], tA[:, 16:32]
                nm1, nmk, cmp = tA[:, 32:33], tA[:, 33:34], tA[:, 34:50]
                vseq(v, v.tensor_add(lg, prt[:, 0:16], rbias_bc[:, :]))
                vseq(v, v.reduce_max(nm1, lg, axis=AX, negate=True))
                vseq(v, v.tensor_copy(cur, lg))
                for it in range(3):
                    nm = nm1 if it == 0 else nmk
                    vseq(v, v.tensor_scalar(cmp, cur, nm, 0.0, ALU.add, ALU.is_ge))
                    vseq(v, v.scalar_tensor_tensor(cur, cmp, NEG, cur, ALU.mult, ALU.add))
                    vseq(v, v.reduce_max(nmk, cur, axis=AX, negate=True))
                inc("v", v.tensor_scalar(cmp, lg, nmk, 0.0, ALU.add, ALU.is_ge))

            @block.scalar
            def _(sc):
                wait(sc, "v")
                inc("s", sc.activation(tA[:, 50:66], tA[:, 0:16], AF.Exp, bias=tA[:, 32:33]))

            @block.vector
            def _(v, j=j):
                wait(v, "s")
                ew, ssum, rs = tA[:, 50:66], tA[:, 66:67], tA[:, 67:68]
                vseq(v, v.tensor_mul(ew, ew, tA[:, 34:50]))
                vseq(v, v.reduce_sum(ssum, ew, axis=AX))
                vseq(v, v.reciprocal(rs, ssum))
                vseq(v, v.tensor_scalar(scoresR[:, :], ew, rs, None, ALU.mult))
                v.tensor_copy(maskRf[:, j * 16:(j + 1) * 16], tA[:, 34:50])
                v.tensor_copy(maskRb[:, j * 16:(j + 1) * 16], tA[:, 34:50])
                inc("v", v.tensor_scalar_mul(scoresP[:, j * 16:(j + 1) * 16], scoresR[:, :],
                                             1.0 / KAPPA))

            @block.tensor
            def _(t):
                wait(t, "v")
                inc("t", t.transpose(prt[0:16, :], scoresR[:, :], eyef[:, :]))

            @block.vector
            def _(v, j=j):
                wait(v, "t")
                inc("v", v.tensor_copy(scoresT[:, j * 128:(j + 1) * 128], prt[0:16, :]))
            set_guard("prt", "v")

    if stage >= 1:
        XNT_DONE_V = cnt["v"]
        for cc in range(4):
            bh = cc % 2

            @block.gpsimd
            def _(g, cc=cc, bh=bh):
                need(g, ("bigq", bh))
                g.dma_start(big[:, bh * 8192:(bh + 1) * 8192],
                            wq_d[:, cc * 8192:(cc + 1) * 8192]).then_inc(sems["w"], 16)
            cnt["w"] += 16
            WQ_W = cnt["w"]

            for j in range(NT_OWN):
                ji = j % 2
                pq = pq0 if ji == 0 else pq1

                @block.tensor
                def _(t, j=j, cc=cc, bh=bh, pq=pq, ji=ji, WQ_W=WQ_W):
                    t.wait_ge(sems["w"], WQ_W)
                    t.wait_ge(sems["v"], XNT_DONE_V)
                    need(t, ("pq", ji))
                    ins = None
                    for fc in range(FC):
                        ins = t.matmul(pq[:, :], xnT[:, fc * 1024 + j * 128: fc * 1024 + (j + 1) * 128],
                                       big[:, bh * 8192 + fc * 512: bh * 8192 + (fc + 1) * 512],
                                       start=(fc == 0), stop=(fc == FC - 1))
                    inc("t", ins)

                @block.vector
                def _(v, j=j, pq=pq):
                    wait(v, "t")
                    need(v, "qro4")
                    cq = cosqa[:, j * 64:(j + 1) * 64]
                    sq = sinqa[:, j * 64:(j + 1) * 64]
                    for hh in range(4):
                        x1 = pq[:, hh * 128 + 0: hh * 128 + 128: 2]
                        x2 = pq[:, hh * 128 + 1: hh * 128 + 128: 2]
                        v.tensor_mul(tA[:, 0:64], x1, cq)
                        vseq(v, v.tensor_mul(tB[:, 0:64], x2, sq))
                        v.tensor_sub(qro4[:, hh * 128: hh * 128 + 64], tA[:, 0:64], tB[:, 0:64])
                        v.tensor_mul(tA[:, 64:128], x1, sq)
                        vseq(v, v.tensor_mul(tB[:, 64:128], x2, cq))
                        v.tensor_add(qro4[:, hh * 128 + 64: hh * 128 + 128], tA[:, 64:128], tB[:, 64:128])
                    inc("v", v.engine_nop())
                set_guard(("pq", ji), "v")

                @block.tensor
                def _(t):
                    wait(t, "v")
                    need(t, "pk")
                    ins = None
                    for hh in range(4):
                        ins = t.transpose(pk0[:, hh * 128:(hh + 1) * 128],
                                          qro4[:, hh * 128:(hh + 1) * 128], eyeb[:, :])
                    inc("t", ins)
                set_guard("qro4", "t")

                @block.vector
                def _(v):
                    wait(v, "t")
                    need(v, "qT4")
                    inc("v", v.tensor_copy(qT4[:, :], pk0[:, 0:512]))
                set_guard("pk", "v")

                for hh in range(4):
                    h = cc * 4 + hh
                    po = h % 2

                    @block.tensor
                    def _(t, h=h, hh=hh, po=po):
                        wait(t, "v")
                        need(t, "pqt")
                        dst = pqt[64:128, :] if po else pqt[0:64, :]
                        inc("t", t.matmul(dst, kexp[:, h * 64:(h + 1) * 64],
                                          qT4[:, hh * 128:(hh + 1) * 128], start=True, stop=True,
                                          tile_position=(0, 64) if po else None))

                    @block.vector
                    def _(v, h=h, j=j, po=po):
                        wait(v, "t")
                        src = pqt[64:128, :] if po else pqt[0:64, :]
                        dst = qtl[po * 64:po * 64 + 64,
                                  (h // 2) * 1024 + j * 128:(h // 2) * 1024 + (j + 1) * 128]
                        inc("v", v.tensor_copy(dst, src))
                    set_guard("pqt", "v")
                set_guard("qT4", "t")
            set_guard(("bigq", bh), "t")

    if stage >= 2:
        p12.close()
        s12.close()
        open_stacks.remove(p12)
        open_stacks.remove(s12)
        s3 = ExitStack()
        open_stacks.append(s3)
        mask_sb = sb(s3, "mask_sb", [128, 512], F32)
        scores2 = sb(s3, "scores2", [128, 2 * 4096], F32)
        att = [sb(s3, "att0", [128, 512], BF16), sb(s3, "att1", [128, 512], BF16)]
        attT = [sb(s3, "attT0", [128, 512], BF16), sb(s3, "attT1", [128, 512], BF16)]
        nm2 = sb(s3, "nm2", [128, 32], F32)
        ybarb = sb(s3, "ybarb", [128, 128], BF16)
        ybT_sb = sb(s3, "ybT_sb", [64, 128], BF16)

        p3 = ExitStack()
        open_stacks.append(p3)
        ps_s = [psmt(p3, "ps_s0", [128, 512], F32), psmt(p3, "ps_s1", [128, 512], F32)]
        ps_aT = [psmt(p3, "ps_aT0", [128, 512], BF16), psmt(p3, "ps_aT1", [128, 512], BF16)]
        ps_y = [psmt(p3, "ps_y0", [128, 64], F32), psmt(p3, "ps_y1", [128, 64], F32)]
        ps_bt = psmt(p3, "ps_bt", [64, 128], BF16)
        ps_yh = psmt(p3, "ps_yh", [128, 128], F32)

        barrier()

        for j in range(NT_OWN):
            ext = j + 1

            @block.sync
            def _(s_, j=j):
                need(s_, "mask_sb")
                s_.dma_start(mask_sb[:, :], maskd[:, j * 512:(j + 1) * 512]).then_inc(sems["l"], 16)
            cnt["l"] += 16
            MASK_L = cnt["l"]

            for hp in range(H // 2):
                strm = []
                for st_ in (0, 1):
                    h = 2 * hp + st_
                    strm.append({"h": h, "hpart": (h % 2) * 64,
                                 "hcol": (h // 2) * 1024 + j * 128,
                                 "sc": scores2[:, st_ * 4096:(st_ + 1) * 4096],
                                 "nmo": st_ * 16})

                for c in range(ext):
                    last = (c == j)
                    for st_ in (0, 1):
                        d = strm[st_]

                        @block.tensor
                        def _(t, d=d, c=c, st_=st_):
                            need(t, ("ps_s", st_))
                            need(t, ("scores", st_))
                            inc("t", t.matmul(ps_s[st_][:, :],
                                              qtl[d["hpart"]:d["hpart"] + 64, d["hcol"]:d["hcol"] + 128],
                                              klatT[d["hpart"]:d["hpart"] + 64, c * 512:(c + 1) * 512],
                                              start=True, stop=True))
                        d["mmv"] = cnt["t"]

                        @block.vector
                        def _(v, d=d, c=c, st_=st_, last=last, MASK_L=MASK_L):
                            v.wait_ge(sems["t"], d["mmv"])
                            dst = d["sc"][:, c * 512:(c + 1) * 512]
                            if last:
                                v.wait_ge(sems["l"], MASK_L)
                                inc("v", v.tensor_add(dst, ps_s[st_][:, :], mask_sb[:, :]))
                            else:
                                inc("v", v.tensor_copy(dst, ps_s[st_][:, :]))
                        d["evv"] = cnt["v"]
                        guards[("ps_s", st_)] = ("v", cnt["v"])

                for st_ in (0, 1):
                    d = strm[st_]
                    nmo = d["nmo"]

                    @block.vector
                    def _(v, d=d, nmo=nmo, ext=ext):
                        v.wait_ge(sems["v"], d["evv"])
                        need(v, ("nmg", d["nmo"]))
                        vseq(v, v.reduce_max(nm2[:, nmo:nmo + 1], d["sc"][:, :ext * 512],
                                             axis=AX, negate=True))
                        inc("v", v.tensor_scalar_mul(nm2[:, nmo + 1:nmo + 2], nm2[:, nmo:nmo + 1], SCALE))
                    d["biasv"] = cnt["v"]

                for c in range(ext):
                    @block.scalar
                    def _(sc_, c=c, strm=strm):
                        for st_ in (0, 1):
                            d = strm[st_]
                            nmo = d["nmo"]
                            sc_.wait_ge(sems["v"], d["biasv"])
                            need(sc_, ("att", st_))
                            inc("s", sc_.activation(att[st_][:, :], d["sc"][:, c * 512:(c + 1) * 512],
                                                    AF.Exp, scale=SCALE, bias=nm2[:, nmo + 1:nmo + 2],
                                                    accum_out=nm2[:, nmo + 8 + c:nmo + 9 + c]))
                            d["expv"] = cnt["s"]
                    for st_ in (0, 1):
                        d = strm[st_]
                        nmo = d["nmo"]

                        @block.tensor
                        def _(t, d=d, st_=st_):
                            t.wait_ge(sems["s"], d["expv"])
                            need(t, ("ps_aT", st_))
                            ins = None
                            for kk in range(4):
                                ins = t.transpose(ps_aT[st_][:, kk * 128:(kk + 1) * 128],
                                                  att[st_][:, kk * 128:(kk + 1) * 128], eyeb[:, :])
                            inc("t", ins)
                        d["aTv"] = cnt["t"]
                        guards[("att", st_)] = ("t", cnt["t"])

                        @block.vector
                        def _(v, d=d, st_=st_):
                            v.wait_ge(sems["t"], d["aTv"])
                            need(v, ("attT", st_))
                            inc("v", v.tensor_copy(attT[st_][:, :], ps_aT[st_][:, :]))
                        d["evv2"] = cnt["v"]
                        guards[("ps_aT", st_)] = ("v", cnt["v"])

                        @block.tensor
                        def _(t, d=d, st_=st_, c=c, j=j):
                            t.wait_ge(sems["v"], d["evv2"])
                            if c == 0:
                                need(t, ("ps_y", st_))
                            ins = None
                            for kk in range(4):
                                ins = t.matmul(ps_y[st_][:, :], attT[st_][:, kk * 128:(kk + 1) * 128],
                                               vlat[:, (4 * c + kk) * 64:(4 * c + kk + 1) * 64],
                                               start=(c == 0 and kk == 0), stop=(c == j and kk == 3))
                            inc("t", ins)
                        d["ytv"] = cnt["t"]
                        guards[("attT", st_)] = ("t", cnt["t"])
                    if c == j:
                        for st_ in (0, 1):
                            guards[("scores", st_)] = ("s", strm[st_]["expv"])

                for st_ in (0, 1):
                    d = strm[st_]
                    nmo = d["nmo"]

                    @block.vector
                    def _(v, d=d, st_=st_, nmo=nmo, ext=ext):
                        v.wait_ge(sems["s"], d["expv"])
                        v.wait_ge(sems["t"], d["ytv"])
                        vseq(v, v.reduce_sum(nm2[:, nmo + 2:nmo + 3], nm2[:, nmo + 8:nmo + 8 + ext], axis=AX))
                        vseq(v, v.reciprocal(nm2[:, nmo + 3:nmo + 4], nm2[:, nmo + 2:nmo + 3]))
                        inc("v", v.tensor_scalar(ybarb[:, st_ * 64:(st_ + 1) * 64], ps_y[st_][:, :],
                                                 nm2[:, nmo + 3:nmo + 4], None, ALU.mult))
                    d["tl1"] = cnt["v"]
                    guards[("ps_y", st_)] = ("v", cnt["v"])
                    guards[("nmg", nmo)] = ("v", cnt["v"])

                    @block.tensor
                    def _(t, d=d, st_=st_):
                        t.wait_ge(sems["v"], d["tl1"])
                        need(t, "ps_bt")
                        inc("t", t.transpose(ps_bt[:, :], ybarb[:, st_ * 64:(st_ + 1) * 64], eyeb[:, :]))
                    d["tl2"] = cnt["t"]

                    @block.vector
                    def _(v, d=d):
                        v.wait_ge(sems["t"], d["tl2"])
                        inc("v", v.tensor_copy(ybT_sb[:, :], ps_bt[:, :]))
                    d["tl3"] = cnt["v"]
                    guards["ps_bt"] = ("v", cnt["v"])

                    @block.tensor
                    def _(t, d=d):
                        t.wait_ge(sems["v"], d["tl3"])
                        need(t, "ps_yh")
                        inc("t", t.matmul(ps_yh[:, :], vexpb[:, d["h"] * 128:(d["h"] + 1) * 128],
                                          ybT_sb[:, :], start=True, stop=True))
                    d["tl4"] = cnt["t"]

                    @block.vector
                    def _(v, d=d, j=j):
                        v.wait_ge(sems["t"], d["tl4"])
                        inc("v", v.tensor_copy(big[:, d["h"] * 1024 + j * 128: d["h"] * 1024 + (j + 1) * 128],
                                               ps_yh[:, :]))
                    guards["ps_yh"] = ("v", cnt["v"])
            set_guard("mask_sb", "v")

        p3.close()
        s3.close()
        open_stacks.remove(p3)
        open_stacks.remove(s3)
        barrier()
        s3p = ExitStack()
        open_stacks.append(s3p)
        pslot = sb(s3p, "pslot", [128, FC * 512], BF16)
        pso = [psmt(s3p, "pso0", [128, 512], F32), psmt(s3p, "pso1", [128, 512], F32)]
        for ccp in range(4):
            @block.gpsimd
            def _(g, ccp=ccp):
                need(g, "pslot")
                g.dma_start(pslot[:, :], proj_d[:, ccp * 8192:(ccp + 1) * 8192]).then_inc(sems["w"], 16)
            cnt["w"] += 16
            PW = cnt["w"]

            for j in range(NT_OWN):
                @block.tensor
                def _(t, j=j, PW=PW):
                    t.wait_ge(sems["w"], PW)
                    wait(t, "v")
                    need(t, ("pso", j % 2))
                    ins = None
                    for hh in range(H):
                        ins = t.matmul(pso[j % 2][:, :],
                                       big[:, hh * 1024 + j * 128: hh * 1024 + (j + 1) * 128],
                                       pslot[:, hh * 512:(hh + 1) * 512],
                                       start=(hh == 0), stop=(hh == H - 1))
                    inc("t", ins)

                @block.vector
                def _(v, j=j, ccp=ccp):
                    wait(v, "t")
                    inc("v", v.tensor_copy(accb[:, j * CD + ccp * 512: j * CD + (ccp + 1) * 512],
                                           pso[j % 2][:, :]))
                set_guard(("pso", j % 2), "v")
            set_guard("pslot", "t")

    if stage >= 3:
        s3p.close()
        ast.close()
        for st_ in (s3p, ast):
            open_stacks.remove(st_)
        s4 = ExitStack()
        open_stacks.append(s4)
        ws = sb(s4, "ws", [128, 4 * 4096], BF16)
        bbg = sb(s4, "bbg", [128, 2 * 512], F32)
        bbu = sb(s4, "bbu", [128, 2 * 512], F32)
        ga = sb(s4, "ga", [128, 512], F32)
        gs = sb(s4, "gs", [128, 512], F32)
        ua = sb(s4, "ua", [128, 512], F32)
        actw = sb(s4, "actw", [128, 3 * 512], BF16)
        dbias_sb = sb(s4, "dbias_sb", [16, CD], BF16)
        ltri_sb = sb(s4, "ltri_sb", [128, 128], BF16)
        allon = sb(s4, "allon", [128, 128], BF16)
        onesf = sb(s4, "onesf", [1, 128], F32)
        iota_sb = sb(s4, "iota_sb", [128, 384], F32)
        piotak_sb = sb(s4, "piotak_sb", [128, 4], F32)
        rank_tok = sb(s4, "rank_tok", [128, 128], F32)
        rm1m = sb(s4, "rm1m", [128, 128], F32)
        rm1row = sb(s4, "rm1row", [1, 1024], F32)
        scorow = sb(s4, "scorow", [1, 1024], F32)
        qbuf = sb(s4, "qbuf", [128, 8 * 384], BF16)
        dtb = sb(s4, "dtb", [128, 3 * 512], BF16)
        cmpb = sb(s4, "cmpb", [128, 512], F32)
        actwT = xnT[:, 0:16 * 384]
        gfm = xnT[:, 16 * 384:32 * 384]
        qwt = xnT[:, 32 * 384:32 * 384 + 3 * 1024]

        p4 = ExitStack()
        open_stacks.append(p4)
        pf = [psmt(p4, "pf%d" % i, [128, 512], F32) for i in range(6)]
        pat = [psmt(p4, "pat0", [128, 512], BF16), psmt(p4, "pat1", [128, 512], BF16)]

        barrier()

        @block.gpsimd
        def _(g):
            g.dma_start(dbias_sb[:, :], dbias_d).then_inc(sems["w"], 16)
            g.dma_start(ltri_sb[:, :], ltri_d).then_inc(sems["w"], 16)
        cnt["w"] += 32
        LT_W = cnt["w"]

        @block.sync
        def _(s_):
            s_.dma_start(iota_sb[:, :], iota_d).then_inc(sems["l"], 16)
            s_.dma_start(piotak_sb[:, :], piotak_d).then_inc(sems["l"], 16)
        cnt["l"] += 32
        P4_L = cnt["l"]

        @block.vector
        def _(v):
            v.memset(allon[:, :], 1.0)
            inc("v", v.memset(onesf[:, :], 1.0))

        for j in range(NT_OWN):
            for fcg in range(4):
                pi4 = fcg % 2

                @block.tensor
                def _(t, j=j, fcg=fcg, pi4=pi4):
                    need(t, ("pat", pi4))
                    ins = None
                    for i4 in range(4):
                        fc = fcg * 4 + i4
                        ins = t.transpose(pat[pi4][:, i4 * 128:(i4 + 1) * 128],
                                          xnT[:, fc * 1024 + j * 128: fc * 1024 + (j + 1) * 128],
                                          eyeb[:, :])
                    inc("t", ins)
                XT_T = cnt["t"]

                @block.vector
                def _(v, j=j, fcg=fcg, pi4=pi4, XT_T=XT_T):
                    v.wait_ge(sems["t"], XT_T)
                    inc("v", v.tensor_copy(big[:, j * CD + fcg * 512: j * CD + (fcg + 1) * 512],
                                           pat[pi4][:, :]))
                guards[("pat", pi4)] = ("v", cnt["v"])
        XNTOK_V = cnt["v"]

        for J in range(NT_OWN):
            @block.tensor
            def _(t, J=J, LT_W=LT_W):
                t.wait_ge(sems["w"], LT_W)
                wait(t, "v")
                need(t, ("pf", 0))
                ins = None
                for I in range(J + 1):
                    rhs = maskRb[:, I * 16:(I + 1) * 16]
                    lhs = ltri_sb[:, :] if I == J else allon[:, :]
                    ins = t.matmul(pf[0][:, 0:16], lhs, rhs, start=(I == 0), stop=(I == J))
                inc("t", ins)
            RK_T = cnt["t"]

            @block.vector
            def _(v, J=J, RK_T=RK_T):
                v.wait_ge(sems["t"], RK_T)
                inc("v", v.tensor_copy(rank_tok[:, J * 16:(J + 1) * 16], pf[0][:, 0:16]))
            guards[("pf", 0)] = ("v", cnt["v"])

        @block.vector
        def _(v):
            wait(v, "v")
            vseq(v, v.scalar_tensor_tensor(rm1m[:, :], rank_tok[:, :], 1.0, maskRf[:, :],
                                           ALU.mult, ALU.mult))
            inc("v", v.tensor_scalar_add(rm1m[:, :], rm1m[:, :], -1.0))
        RM_V = cnt["v"]

        slot_i = [0]

        def load_half(src_ap):
            si = slot_i[0] % 4
            slot_i[0] += 1

            @block.gpsimd
            def _(g, si=si):
                need(g, ("ws", si))
                g.dma_start(ws[:, si * 4096:(si + 1) * 4096], src_ap).then_inc(sems["w"], 16)
            cnt["w"] += 16
            return si, cnt["w"]

        prev_aT = [None]

        def emit_prev_aT():
            if prev_aT[0] is None:
                return
            dcp_, avs = prev_aT[0]
            prev_aT[0] = None
            for k in range(3):
                pi4 = k % 2

                @block.tensor
                def _(t, k=k, pi4=pi4, av=avs[k]):
                    t.wait_ge(sems["v"], av)
                    need(t, ("pat", pi4))
                    ins = None
                    for db in range(4):
                        ins = t.transpose(pat[pi4][:, db * 128:(db + 1) * 128],
                                          actw[:, k * 512 + db * 128: k * 512 + (db + 1) * 128],
                                          eyeb[:, :])
                    inc("t", ins)
                AT_T = cnt["t"]

                @block.vector
                def _(v, dcp_=dcp_, k=k, pi4=pi4, AT_T=AT_T):
                    v.wait_ge(sems["t"], AT_T)
                    bv = actwT.rearrange("p (D kt s) -> p D kt s", D=FC, kt=3)
                    inc("v", v.tensor_copy(bv[:, dcp_ * 4:(dcp_ + 1) * 4, k, :],
                                           pat[pi4][:, :].rearrange("p (db s) -> p db s", db=4)))
                guards[("pat", pi4)] = ("v", cnt["v"])
                guards[("actw", k)] = ("v", cnt["v"])

        for e in range(16):
            for Jg in range(2):
                @block.tensor
                def _(t, e=e, Jg=Jg, RM_V=RM_V):
                    t.wait_ge(sems["v"], RM_V)
                    need(t, ("pf", 0))
                    need(t, ("pf", 1))
                    ins = None
                    for i4 in range(4):
                        J = Jg * 4 + i4
                        t.transpose(pf[0][0:1, i4 * 128:(i4 + 1) * 128],
                                    rm1m[:, J * 16 + e:J * 16 + e + 1], eyef[:, :])
                        ins = t.transpose(pf[1][0:1, i4 * 128:(i4 + 1) * 128],
                                          scoresP[:, J * 16 + e:J * 16 + e + 1], eyef[:, :])
                    inc("t", ins)
                RW_T = cnt["t"]

                @block.vector
                def _(v, Jg=Jg, RW_T=RW_T):
                    v.wait_ge(sems["t"], RW_T)
                    v.tensor_copy(rm1row[0:1, Jg * 512:(Jg + 1) * 512], pf[0][0:1, 0:512])
                    inc("v", v.tensor_copy(scorow[0:1, Jg * 512:(Jg + 1) * 512], pf[1][0:1, 0:512]))
                guards[("pf", 0)] = ("v", cnt["v"])
                guards[("pf", 1)] = ("v", cnt["v"])
            ROW_V = cnt["v"]

            @block.tensor
            def _(t, ROW_V=ROW_V):
                t.wait_ge(sems["v"], ROW_V)
                for i in range(4):
                    need(t, ("pf", i))
                t.matmul(pf[0][:, :], onesf[:, :], rm1row[0:1, 0:512], start=True, stop=True)
                t.matmul(pf[1][:, :], onesf[:, :], rm1row[0:1, 512:1024], start=True, stop=True)
                t.matmul(pf[2][:, :], onesf[:, :], scorow[0:1, 0:512], start=True, stop=True)
                inc("t", t.matmul(pf[3][:, :], onesf[:, :], scorow[0:1, 512:1024],
                                  start=True, stop=True))
            BC_T = cnt["t"]

            @block.vector
            def _(v, BC_T=BC_T):
                v.wait_ge(sems["t"], BC_T)
                need(v, "qwt")
                for k in range(3):
                    for hf in range(2):
                        vseq(v, v.tensor_scalar(cmpb[:, :], pf[hf][:, :],
                                                piotak_sb[:, k:k + 1], 0.0,
                                                ALU.subtract, ALU.is_equal))
                        v.tensor_mul(qwt[:, k * 1024 + hf * 512: k * 1024 + (hf + 1) * 512],
                                     cmpb[:, :], pf[2 + hf][:, :])
                inc("v", v.engine_nop())
            for i in range(4):
                guards[("pf", i)] = ("v", cnt["v"])

            @block.vector
            def _(v, e=e):
                v.wait_ge(sems["l"], P4_L)
                need(v, "qbuf")
                ins = None
                for J in range(NT_OWN):
                    ins = v.tensor_scalar(qbuf[:, J * 384:(J + 1) * 384], iota_sb[:, :],
                                          rm1m[:, J * 16 + e:J * 16 + e + 1], 0.0,
                                          ALU.subtract, ALU.is_equal)
                inc("v", ins)
            QB_V = cnt["v"]

            for fc in range(FC):
                gi = fc % 2

                @block.tensor
                def _(t, fc=fc, gi=gi, QB_V=QB_V):
                    t.wait_ge(sems["v"], max(QB_V, XNTOK_V))
                    need(t, ("pf", gi))
                    ins = None
                    for J in range(NT_OWN):
                        ins = t.matmul(pf[gi][:, 0:384],
                                       big[:, J * CD + fc * 128: J * CD + (fc + 1) * 128],
                                       qbuf[:, J * 384:(J + 1) * 384],
                                       start=(J == 0), stop=(J == NT_OWN - 1))
                    inc("t", ins)
                GA_T = cnt["t"]

                @block.vector
                def _(v, fc=fc, gi=gi, GA_T=GA_T):
                    v.wait_ge(sems["t"], GA_T)
                    inc("v", v.tensor_copy(gfm[:, fc * 384:(fc + 1) * 384], pf[gi][:, 0:384]))
                guards[("pf", gi)] = ("v", cnt["v"])
            guards["qbuf"] = ("t", cnt["t"])
            GF_V = cnt["v"]

            for dc in range(4):
                sgA, gA_n = load_half(wg_d[e, :, dc * 8192: dc * 8192 + 4096])
                sgB, gB_n = load_half(wg_d[e, :, dc * 8192 + 4096: (dc + 1) * 8192])
                suA, uA_n = load_half(wu_d[e, :, dc * 8192: dc * 8192 + 4096])
                suB, uB_n = load_half(wu_d[e, :, dc * 8192 + 4096: (dc + 1) * 8192])
                dcp = dc % 2

                @block.sync
                def _(s_, e=e, dc=dc, dcp=dcp):
                    need(s_, ("bb", dcp))
                    s_.dma_start(bbg[:, dcp * 512:(dcp + 1) * 512],
                                 bass.AP(biasg_d.tensor, e * CD + dc * 512, [[0, 128], [1, 512]])).then_inc(sems["l"], 16)
                    s_.dma_start(bbu[:, dcp * 512:(dcp + 1) * 512],
                                 bass.AP(biasu_d.tensor, e * CD + dc * 512, [[0, 128], [1, 512]])).then_inc(sems["l"], 16)
                cnt["l"] += 32
                BB_L = cnt["l"]

                emit_prev_aT()

                @block.tensor
                def _(t, sgA=sgA, sgB=sgB, suA=suA, suB=suB,
                      gA_n=gA_n, gB_n=gB_n, uA_n=uA_n, uB_n=uB_n, GF_V=GF_V):
                    t.wait_ge(sems["v"], GF_V)
                    for i in range(6):
                        need(t, ("pf", i))
                    ins = None
                    for fc in range(FC):
                        if fc == 0:
                            t.wait_ge(sems["w"], max(gA_n, uA_n))
                        if fc == 8:
                            t.wait_ge(sems["w"], max(gB_n, uB_n))
                        sg = sgA if fc < 8 else sgB
                        su = suA if fc < 8 else suB
                        col = (fc % 8) * 512
                        for k in range(3):
                            t.matmul(pf[k][:, :],
                                     gfm[:, fc * 384 + k * 128: fc * 384 + (k + 1) * 128],
                                     ws[:, sg * 4096 + col: sg * 4096 + col + 512],
                                     start=(fc == 0), stop=(fc == FC - 1))
                        for k in range(3):
                            ins = t.matmul(pf[3 + k][:, :],
                                           gfm[:, fc * 384 + k * 128: fc * 384 + (k + 1) * 128],
                                           ws[:, su * 4096 + col: su * 4096 + col + 512],
                                           start=(fc == 0), stop=(fc == FC - 1))
                        if fc == 7:
                            inc("t", ins)
                            guards[("ws", sgA)] = ("t", cnt["t"])
                            guards[("ws", suA)] = ("t", cnt["t"])
                    inc("t", ins)
                GU_T = cnt["t"]
                set_guard(("ws", sgB), "t")
                set_guard(("ws", suB), "t")

                avs = []
                for k in range(3):
                    @block.vector
                    def _(v, k=k, dcp=dcp, BB_L=BB_L, GU_T=GU_T):
                        v.wait_ge(sems["t"], GU_T)
                        v.wait_ge(sems["l"], BB_L)
                        need(v, ("actw", k))
                        vseq(v, v.tensor_add(ga[:, :], pf[k][:, :], bbg[:, dcp * 512:(dcp + 1) * 512]))
                        inc("v", v.tensor_scalar_min(ga[:, :], ga[:, :], LIMIT))
                    guards[("pf", k)] = ("v", cnt["v"])

                    @block.scalar
                    def _(sc):
                        wait(sc, "v")
                        inc("s", sc.activation(gs[:, :], ga[:, :], AF.Silu, scale=KAPPA))

                    @block.vector
                    def _(v, k=k, dcp=dcp):
                        wait(v, "s")
                        v.tensor_add(ua[:, :], pf[3 + k][:, :], bbu[:, dcp * 512:(dcp + 1) * 512])
                        vseq(v, v.tensor_scalar(ua[:, :], ua[:, :], LIMIT, -LIMIT, ALU.min, ALU.max))
                        inc("v", v.scalar_tensor_tensor(actw[:, k * 512:(k + 1) * 512], ua[:, :],
                                                        1.0, gs[:, :], ALU.add, ALU.mult))
                    guards[("pf", 3 + k)] = ("v", cnt["v"])
                    avs.append(cnt["v"])
                    if k == 2:
                        set_guard(("bb", dcp), "v")
                prev_aT[0] = (dc, avs)

            emit_prev_aT()
            ACT_DONE_V = cnt["v"]

            for hc in range(4):
                sdA, dA_n = load_half(down_d[e, :, hc * 8192: hc * 8192 + 4096])
                sdB, dB_n = load_half(down_d[e, :, hc * 8192 + 4096: (hc + 1) * 8192])

                @block.tensor
                def _(t, sdA=sdA, sdB=sdB, dA_n=dA_n, dB_n=dB_n, ACT_DONE_V=ACT_DONE_V):
                    t.wait_ge(sems["v"], ACT_DONE_V)
                    for k in range(3):
                        need(t, ("pf", k))
                    ins = None
                    for D in range(FC):
                        if D == 0:
                            t.wait_ge(sems["w"], dA_n)
                        if D == 8:
                            t.wait_ge(sems["w"], dB_n)
                        sd = sdA if D < 8 else sdB
                        col = (D % 8) * 512
                        for k in range(3):
                            ins = t.matmul(pf[k][:, :],
                                           actwT[:, D * 384 + k * 128: D * 384 + (k + 1) * 128],
                                           ws[:, sd * 4096 + col: sd * 4096 + col + 512],
                                           start=(D == 0), stop=(D == FC - 1))
                        if D == 7:
                            inc("t", ins)
                            guards[("ws", sdA)] = ("t", cnt["t"])
                    inc("t", ins)
                DW_T = cnt["t"]
                set_guard(("ws", sdB), "t")

                @block.vector
                def _(v, DW_T=DW_T):
                    v.wait_ge(sems["t"], DW_T)
                    need(v, "dtb")
                    for k in range(3):
                        v.tensor_copy(dtb[:, k * 512:(k + 1) * 512], pf[k][:, :])
                    inc("v", v.engine_nop())
                DT_V = cnt["v"]
                for k in range(3):
                    guards[("pf", k)] = ("v", cnt["v"])

                for t8 in range(NT_OWN):
                    @block.tensor
                    def _(t, t8=t8, DT_V=DT_V):
                        t.wait_ge(sems["v"], DT_V)
                        need(t, ("pf", 3))
                        ins = None
                        for k in range(3):
                            ins = t.matmul(pf[3][:, :],
                                           qwt[:, k * 1024 + t8 * 128: k * 1024 + (t8 + 1) * 128],
                                           dtb[:, k * 512:(k + 1) * 512],
                                           start=(k == 0), stop=(k == 2))
                        inc("t", ins)
                    SC_T = cnt["t"]
                    if t8 == NT_OWN - 1:
                        guards["dtb"] = ("t", cnt["t"])

                    @block.vector
                    def _(v, t8=t8, hc=hc, SC_T=SC_T):
                        v.wait_ge(sems["t"], SC_T)
                        dst = accb[:, t8 * CD + hc * 512: t8 * CD + (hc + 1) * 512]
                        inc("v", v.tensor_add(dst, dst, pf[3][:, :]))
                    guards[("pf", 3)] = ("v", cnt["v"])
            guards["qwt"] = ("t", cnt["t"])

        for t8 in range(8):
            for hc in range(4):
                @block.tensor
                def _(t, t8=t8, hc=hc):
                    wait(t, "w", "v")
                    need(t, ("pf", 0))
                    inc("t", t.matmul(pf[0][:, :], scoresT[:, t8 * 128:(t8 + 1) * 128],
                                      dbias_sb[:, hc * 512:(hc + 1) * 512], start=True, stop=True))
                DB_T = cnt["t"]

                @block.vector
                def _(v, t8=t8, hc=hc, DB_T=DB_T):
                    v.wait_ge(sems["t"], DB_T)
                    dst = accb[:, t8 * CD + hc * 512: t8 * CD + (hc + 1) * 512]
                    inc("v", v.tensor_add(dst, dst, pf[0][:, :]))
                guards[("pf", 0)] = ("v", cnt["v"])

    if stage >= 3:
        p4.close()
        s4.close()
        open_stacks.remove(p4)
        open_stacks.remove(s4)
        barrier()
        s5 = ExitStack()
        open_stacks.append(s5)
        xf = sb(s5, "xf", [128, 2 * CD], F32)
        ob = sb(s5, "ob", [128, 2 * CD], F32)

        for j in range(NT_OWN):
            half = j % 2

            @block.sync
            def _(s, j=j, half=half):
                need(s, ("xf", half))
                s.dma_start(xf[:, half * CD:(half + 1) * CD],
                            x_own[j * 128:(j + 1) * 128, :]).then_inc(sems["l"], 16)
            cnt["l"] += 16

            @block.vector
            def _(v, j=j, half=half):
                wait(v, "l")
                need(v, ("ob", half))
                inc("v", v.tensor_add(ob[:, half * CD:(half + 1) * CD],
                                      accb[:, j * CD:(j + 1) * CD],
                                      xf[:, half * CD:(half + 1) * CD]))
            set_guard(("xf", half), "v")

            @block.sync
            def _(s, j=j, half=half):
                wait(s, "v")
                s.dma_start(out_d[j * 128:(j + 1) * 128, :],
                            ob[:, half * CD:(half + 1) * CD]).then_inc(sems["o"], 16)
            cnt["o"] += 16
            set_guard(("ob", half), "o")

        @block.sync
        def _(s):
            s.wait_ge(sems["o"], cnt["o"])

    if dbg:
        barrier()
        avail = {}
        if stage < 3:
            avail.update({
                "d_klatT": (klatT[:, :], True), "d_vlat": (vlat[:, :], True),
                "d_qtl": (qtl[:, :], True),
            })
        avail.update({
            "d_xnT": (xnT[:, :], True),
            "d_scoresP": (scoresP[:, :], False),
            "d_scoresT": (scoresT[:, :], True),
            "d_acc": (accb[:, :], False),
            "d_big": (big[:, :], True),
        })
        for name in dbg:
            if name not in avail:
                continue
            src_ap, cast = avail[name]
            if cast:
                @block.gpsimd
                def _(g, name=name, src_ap=src_ap):
                    g.dma_start(dbg_d[name], src_ap).then_inc(sems["w"], 16)
                cnt["w"] += 16
            else:
                @block.sync
                def _(s, name=name, src_ap=src_ap):
                    s.dma_start(dbg_d[name], src_ap).then_inc(sems["l"], 16)
                cnt["l"] += 16

        @block.sync
        def _(s):
            s.wait_ge(sems["l"], cnt["l"])

        @block.gpsimd
        def _(g):
            g.wait_ge(sems["w"], cnt["w"])

    for st_ in reversed(open_stacks):
        st_.close()
    return nc


def tile_w(w):
    n = w.shape[1]
    a = w.reshape(16, 128, n // 512, 512)
    a = a.transpose(1, 2, 0, 3)
    return np.ascontiguousarray(a.reshape(128, -1))


def prep_inputs(ii):
    x = np.asarray(ii["L0_attn_x"], np.float32)
    wq = np.asarray(ii["L0_attn_wq"], np.float32)
    wk = np.asarray(ii["L0_attn_wk_lat"], np.float32)
    wv = np.asarray(ii["L0_attn_wv_lat"], np.float32)
    kexpand = np.asarray(ii["L0_attn_k_expand"], np.float32)
    vexpand = np.asarray(ii["L0_attn_v_expand"], np.float32)
    projw = np.asarray(ii["L0_attn_proj_w"], np.float32)
    rw = np.asarray(ii["L0_ffn_router_weight"], np.float32)
    rb = np.asarray(ii["L0_ffn_router_bias"], np.float32)
    gup = np.asarray(ii["L0_ffn_gate_up_proj"], np.float32)
    gupb = np.asarray(ii["L0_ffn_gate_up_proj_bias"], np.float32)
    dw = np.asarray(ii["L0_ffn_down_proj"], np.float32)
    db = np.asarray(ii["L0_ffn_down_proj_bias"], np.float32)

    P128 = np.concatenate([np.arange(0, 128, 2), np.arange(1, 128, 2)])
    P64 = np.concatenate([np.arange(0, 64, 2), np.arange(1, 64, 2)])

    shared = {
        "jrow": np.broadcast_to(np.arange(64, dtype=np.float32), (128, 64)).copy(),
        "eye_f": np.eye(128, dtype=np.float32),
        "pos_ctx": (np.arange(32, dtype=np.float32)[None, :] * 128
                    + np.arange(128, dtype=np.float32)[:, None]).astype(np.float32),
        "wq_t": tile_w(wq),
        "kvw_t": np.ascontiguousarray(np.concatenate([wk, wv], axis=1).reshape(16, 128, 128).transpose(1, 0, 2).reshape(128, 2048)),
        "kexp_t": np.ascontiguousarray(
            kexpand.transpose(0, 2, 1)[:, P128][:, :, P64].transpose(1, 0, 2).reshape(128, 16 * 64)),
        "vexp_t": np.ascontiguousarray(vexpand.transpose(1, 0, 2).reshape(64, 16 * 128)),
        "proj_t": tile_w(projw),
        "rwT_t": np.ascontiguousarray(rw.T.reshape(16, 128, 16).transpose(1, 0, 2).reshape(128, 256)),
        "rbias": rb.reshape(1, 16).copy(),
        "wg_t": np.stack([tile_w(np.ascontiguousarray(gup[e][:, 0::2])) for e in range(16)]),
        "wu_t": np.stack([tile_w(np.ascontiguousarray(gup[e][:, 1::2])) for e in range(16)]),
        "biasg": np.ascontiguousarray(gupb[:, 0::2]),
        "biasu": np.ascontiguousarray(gupb[:, 1::2]),
        "down_t": np.stack([tile_w(dw[e]) for e in range(16)]),
        "dbias": db.copy(),
        "ltri": np.triu(np.ones((128, 128), np.float32)),
        "iota384": np.broadcast_to(np.arange(384, dtype=np.float32), (128, 384)).copy(),
        "piotak": (np.arange(128, dtype=np.float32)[:, None]
                   + np.array([0, 128, 256, 0], np.float32)[None, :]).copy(),
    }

    in_maps = []
    own_tiles = []
    for c in range(8):
        b, cp = c // 4, c % 4
        tiles = [4 * jj + cp for jj in range(8)]
        own_tiles.append(tiles)
        rows = np.concatenate([np.arange(g * 128, (g + 1) * 128) for g in tiles])
        mask = np.zeros((128, 4096), np.float32)
        for jj, g in enumerate(tiles):
            gq = g * 128 + np.arange(128)[:, None]
            gk = jj * 512 + np.arange(512)[None, :]
            mask[:, jj * 512:(jj + 1) * 512] = np.where(gk <= gq, 0.0, NEG)
        posn = (np.array([g * 128 for g in tiles], np.float32)[None, :]
                + np.arange(128, dtype=np.float32)[:, None])
        m = dict(shared)
        m["x_ctx"] = np.ascontiguousarray(x[b])
        m["x_own"] = np.ascontiguousarray(x[b][rows])
        m["maskd"] = mask
        m["pos_own"] = posn.astype(np.float32)
        in_maps.append(m)
    return in_maps, own_tiles


_CACHE = {}


def kernel(**inputs):
    in_maps, own_tiles = prep_inputs(inputs)
    if "nc" not in _CACHE:
        _CACHE["nc"] = build(stage=3, dbg=())
    res = run_bass_kernel_spmd(_CACHE["nc"], in_maps, core_ids=list(range(8)))
    out = np.zeros((2, 4096, 2048), np.float32)
    for c in range(8):
        b = c // 4
        o = res.results[c]["out"].reshape(8, 128, 2048)
        for jj, g in enumerate(own_tiles[c]):
            out[b, g * 128:(g + 1) * 128, :] = o[jj]
    return out


# version 08
# speedup vs baseline: 1.2269x; 1.0932x over previous
import math
import numpy as np
from contextlib import ExitStack

import concourse.bass as bass
import concourse.mybir as mybir
from concourse.bass_utils import run_bass_kernel_spmd

F32 = mybir.dt.float32
BF16 = mybir.dt.bfloat16
I32 = mybir.dt.int32
ALU = mybir.AluOpType
AF = mybir.ActivationFunctionType
AX = mybir.AxisListType.X

PI = math.pi
LN1E4 = math.log(10000.0)
SCALE = 1.0 / math.sqrt(128.0)
KAPPA = 1.702
LIMIT = 7.0
NEG = -1.0e30
EPS = 1e-5

NT_CTX, NT_OWN, FC, H, CD = 32, 8, 16, 16, 2048


def build(stage=3, dbg=()):
    nc = bass.Bass("TRN2", target_bir_lowering=False)

    x_ctx = nc.dram_tensor("x_ctx", [4096, CD], F32, kind="ExternalInput").ap()
    x_own = nc.dram_tensor("x_own", [1024, CD], F32, kind="ExternalInput").ap()
    maskd = nc.dram_tensor("maskd", [128, 4096], F32, kind="ExternalInput").ap()
    pos_own = nc.dram_tensor("pos_own", [128, 8], F32, kind="ExternalInput").ap()
    pos_ctx = nc.dram_tensor("pos_ctx", [128, 32], F32, kind="ExternalInput").ap()
    jrow = nc.dram_tensor("jrow", [128, 64], F32, kind="ExternalInput").ap()
    eye_f = nc.dram_tensor("eye_f", [128, 128], F32, kind="ExternalInput").ap()
    wq_d = nc.dram_tensor("wq_t", [128, FC * CD], F32, kind="ExternalInput").ap()
    kvw_d = nc.dram_tensor("kvw_t", [128, FC * 128], F32, kind="ExternalInput").ap()
    kexp_d = nc.dram_tensor("kexp_t", [128, H * 64], F32, kind="ExternalInput").ap()
    vexp_d = nc.dram_tensor("vexp_t", [64, H * 128], F32, kind="ExternalInput").ap()
    proj_d = nc.dram_tensor("proj_t", [128, 4 * FC * 512], F32, kind="ExternalInput").ap()
    rwT_d = nc.dram_tensor("rwT_t", [128, FC * 16], F32, kind="ExternalInput").ap()
    rbias_d = nc.dram_tensor("rbias", [1, 16], F32, kind="ExternalInput").ap()
    wg_d = nc.dram_tensor("wg_t", [16, 128, 4 * FC * 512], F32, kind="ExternalInput").ap()
    wu_d = nc.dram_tensor("wu_t", [16, 128, 4 * FC * 512], F32, kind="ExternalInput").ap()
    biasg_d = nc.dram_tensor("biasg", [16, CD], F32, kind="ExternalInput").ap()
    biasu_d = nc.dram_tensor("biasu", [16, CD], F32, kind="ExternalInput").ap()
    down_d = nc.dram_tensor("down_t", [16, 128, 4 * FC * 512], F32, kind="ExternalInput").ap()
    dbias_d = nc.dram_tensor("dbias", [16, CD], F32, kind="ExternalInput").ap()
    ltri_d = nc.dram_tensor("ltri", [128, 128], F32, kind="ExternalInput").ap()
    iota_d = nc.dram_tensor("iota384", [128, 384], F32, kind="ExternalInput").ap()
    piotak_d = nc.dram_tensor("piotak", [128, 4], F32, kind="ExternalInput").ap()
    out_d = nc.dram_tensor("out", [1024, CD], F32, kind="ExternalOutput").ap()

    dbg_d = {}
    for name, shape in [
        ("d_klatT", [64, 4096]), ("d_vlat", [128, NT_CTX * 64]),
        ("d_xnT", [128, FC * 1024]), ("d_qtl", [128, 8 * 1024]),
        ("d_scoresP", [128, 128]), ("d_scoresT", [16, 1024]),
        ("d_acc", [128, NT_OWN * CD]), ("d_big", [128, FC * 1024]),
    ]:
        if name in dbg:
            dbg_d[name] = nc.dram_tensor(name, shape, F32, kind="ExternalOutput").ap()

    gst = ExitStack()
    open_stacks = [gst]

    def sb(stk, name, shape, dt):
        return stk.enter_context(nc.sbuf_tensor(name, shape, dt))

    def psmt(stk, name, shape, dt):
        return stk.enter_context(nc.psum_tensor(name, shape, dt))

    xnT = sb(gst, "xnT", [128, FC * 1024], BF16)
    accb = sb(gst, "accb", [128, NT_OWN * CD], F32)
    big = sb(gst, "big", [128, FC * 1024], BF16)
    eyeb = sb(gst, "eyeb", [128, 128], BF16)
    eyef = sb(gst, "eyef", [128, 128], F32)
    scoresP = sb(gst, "scoresP", [128, 128], F32)
    scoresR = sb(gst, "scoresR", [128, 16], F32)
    scoresT = sb(gst, "scoresT", [16, 1024], BF16)
    rbias_bc = sb(gst, "rbias_bc", [128, 16], F32)
    maskRf = sb(gst, "maskRf", [128, 128], F32)
    maskRb = sb(gst, "maskRb", [128, 128], BF16)

    sems = {}
    for k in ("l", "w", "o", "t", "v", "s"):
        sems[k] = gst.enter_context(nc.semaphore("sem_" + k))
    cnt = {k: 0 for k in sems}

    block = gst.enter_context(nc.Block())

    def inc(key, ins, k=1):
        cnt[key] += k
        ins.then_inc(sems[key], k)
        return ins

    guards = {}

    def need(e, key):
        if key in guards:
            k, v = guards[key]
            if v > 0:
                e.wait_ge(sems[k], v)

    def set_guard(key, k):
        guards[key] = (k, cnt[k])

    def vseq(v, ins):
        inc("v", ins)
        v.wait_ge(sems["v"], cnt["v"])
        return ins

    def wait(e, *keys):
        for k in keys:
            if cnt[k] > 0:
                e.wait_ge(sems[k], cnt[k])

    def barrier():
        vals = dict(cnt)

        def wa(e):
            for k, v in vals.items():
                if v > 0:
                    e.wait_ge(sems[k], v)
        for name in ("sync", "gpsimd", "tensor", "vector", "scalar"):
            getattr(block, name)(wa)

    ast = ExitStack()
    open_stacks.append(ast)
    klatT = sb(ast, "klatT", [128, 4096], BF16)
    vlat = sb(ast, "vlat", [128, NT_CTX * 64], BF16)
    qtl = sb(ast, "qtl", [128, 8 * 1024], BF16)
    kexp = sb(ast, "kexp", [128, H * 64], BF16)
    vexpb = sb(ast, "vexpb", [64, H * 128], BF16)


    s12 = ExitStack()
    open_stacks.append(s12)
    kvw = sb(s12, "kvw", [128, FC * 128], BF16)
    rwT = sb(s12, "rwT", [128, FC * 16], BF16)
    invf = sb(s12, "invf", [128, 64], F32)
    invfk = sb(s12, "invfk", [128, 32], F32)
    jrow_sb = sb(s12, "jrow_sb", [128, 64], F32)
    poso = sb(s12, "poso", [128, 8], F32)
    posc = sb(s12, "posc", [128, 32], F32)
    red = sb(s12, "red", [128, 64], F32)
    kib = sb(s12, "kib", [128, 64], I32)
    cosqa = sb(s12, "cosqa", [128, 512], F32)
    sinqa = sb(s12, "sinqa", [128, 512], F32)
    cosk = sb(s12, "cosk", [128, 32], F32)
    sink = sb(s12, "sink", [128, 32], F32)
    lnst = sb(s12, "lnst", [128, 16], F32)
    tA = sb(s12, "tA", [128, 128], F32)
    tB = sb(s12, "tB", [128, 128], F32)
    xt = sb(s12, "xt", [128, 2 * CD], F32)
    xnt = sb(s12, "xnt", [128, CD], BF16)
    xnTs = sb(s12, "xnTs", [128, CD], BF16)
    qro4 = sb(s12, "qro4", [128, 512], BF16)
    qT4 = sb(s12, "qT4", [128, 512], BF16)
    kro = sb(s12, "kro", [128, 64], BF16)


    @block.sync
    def _(s):
        s.dma_start(eyef[:, :], eye_f).then_inc(sems["l"], 16)
        s.dma_start(jrow_sb[:, :], jrow).then_inc(sems["l"], 16)
        s.dma_start(poso[:, :], pos_own).then_inc(sems["l"], 16)
        s.dma_start(posc[:, :], pos_ctx).then_inc(sems["l"], 16)
        s.dma_start(rbias_bc[:, :], bass.AP(rbias_d.tensor, 0, [[0, 128], [1, 16]])).then_inc(sems["l"], 16)
    cnt["l"] += 16 * 5
    P0_L = cnt["l"]

    @block.scalar
    def _(sc):
        sc.wait_ge(sems["l"], P0_L)
        sc.activation(invf[:, :], jrow_sb[:, :], AF.Exp, scale=-LN1E4 / 64.0)
        inc("s", sc.activation(invfk[:, :], jrow_sb[:, :32], AF.Exp, scale=-LN1E4 / 32.0))

    def sincos(pos_col, invf_ap, n, cos_dst, sin_dst):
        @block.vector
        def _(v):
            wait(v, "s", "l")
            vseq(v, v.tensor_scalar(tA[:, :n], invf_ap, pos_col, None, ALU.mult))
            vseq(v, v.tensor_scalar_mul(red[:, :n], tA[:, :n], 1.0 / (2 * PI)))
            inc("v", v.tensor_copy(kib[:, :n], red[:, :n]))

        @block.scalar
        def _(sc):
            wait(sc, "v")
            inc("s", sc.copy(red[:, :n], kib[:, :n]))

        @block.vector
        def _(v):
            wait(v, "s")
            inc("v", v.scalar_tensor_tensor(red[:, :n], red[:, :n], -2 * PI, tA[:, :n], ALU.mult, ALU.add))

        @block.scalar
        def _(sc):
            wait(sc, "v")
            inc("s", sc.activation(sin_dst, red[:, :n], AF.Sin))

        @block.vector
        def _(v):
            wait(v, "s")
            vseq(v, v.tensor_scalar_add(tB[:, :n], tA[:, :n], PI / 2))
            vseq(v, v.tensor_scalar_mul(red[:, :n], tB[:, :n], 1.0 / (2 * PI)))
            inc("v", v.tensor_copy(kib[:, :n], red[:, :n]))

        @block.scalar
        def _(sc):
            wait(sc, "v")
            inc("s", sc.copy(red[:, :n], kib[:, :n]))

        @block.vector
        def _(v):
            wait(v, "s")
            inc("v", v.scalar_tensor_tensor(red[:, :n], red[:, :n], -2 * PI, tB[:, :n], ALU.mult, ALU.add))

        @block.scalar
        def _(sc):
            wait(sc, "v")
            inc("s", sc.activation(cos_dst, red[:, :n], AF.Sin))

    @block.gpsimd
    def _(g):
        g.dma_start(eyeb[:, :], eye_f).then_inc(sems["w"], 16)
        g.dma_start(kvw[:, :], kvw_d).then_inc(sems["w"], 16)
        g.dma_start(kexp[:, :], kexp_d).then_inc(sems["w"], 16)
        g.dma_start(vexpb[:, :], vexp_d).then_inc(sems["w"], 16)
        g.dma_start(rwT[:, :], rwT_d).then_inc(sems["w"], 16)
    cnt["w"] += 16 * 5
    P0_W = cnt["w"]

    p12 = ExitStack()
    open_stacks.append(p12)
    pk0 = psmt(p12, "pk0", [128, 1024], BF16)
    pk1 = psmt(p12, "pk1", [128, 1024], BF16)
    pq0 = psmt(p12, "pq0", [128, 512], F32)
    pq1 = psmt(p12, "pq1", [128, 512], F32)
    pqt = psmt(p12, "pqt", [128, 128], F32)
    prt = psmt(p12, "prt", [128, 128], F32)
    ptrash = psmt(p12, "ptrash", [128, 512], F32)

    def ln_tile(half):
        xv = xt[:, half * CD:(half + 1) * CD]

        @block.vector
        def _(v):
            wait(v, "l")
            inc("v", v.reduce_sum(lnst[:, 0:1], xv, axis=AX))

        @block.scalar
        def _(sc):
            wait(sc, "l")
            ins = None
            for c4 in range(4):
                ins = sc.activation(ptrash[:, :], xv[:, c4 * 512:(c4 + 1) * 512], AF.Square,
                                    accum_out=lnst[:, 8 + c4:9 + c4])
            inc("s", ins)

        @block.vector
        def _(v):
            wait(v, "s", "v")
            vseq(v, v.tensor_scalar_mul(lnst[:, 1:2], lnst[:, 0:1], 1.0 / CD))
            vseq(v, v.reduce_sum(lnst[:, 2:3], lnst[:, 8:12], axis=AX))
            vseq(v, v.tensor_scalar(lnst[:, 3:4], lnst[:, 2:3], 1.0 / CD, EPS, ALU.mult, ALU.add))
            vseq(v, v.tensor_mul(lnst[:, 4:5], lnst[:, 1:2], lnst[:, 1:2]))
            inc("v", v.scalar_tensor_tensor(lnst[:, 5:6], lnst[:, 4:5], -1.0, lnst[:, 3:4],
                                            ALU.mult, ALU.add))

        @block.scalar
        def _(sc):
            wait(sc, "v")
            sc.activation(lnst[:, 6:7], lnst[:, 5:6], AF.Ln)
            inc("s", sc.activation(lnst[:, 7:8], lnst[:, 6:7], AF.Exp, scale=-0.5))

        @block.vector
        def _(v):
            wait(v, "s")
            inc("v", v.tensor_scalar(xnt[:, :], xv, lnst[:, 1:2], lnst[:, 7:8],
                                     ALU.subtract, ALU.mult))

    def xn_transposes(own_j):
        @block.tensor
        def _(t):
            wait(t, "v")
            t.wait_ge(sems["w"], 16)
            need(t, "pk")
            ins = None
            for fc in range(FC):
                pk = pk0 if fc < 8 else pk1
                ins = t.transpose(pk[:, (fc % 8) * 128:(fc % 8 + 1) * 128],
                                  xnt[:, fc * 128:(fc + 1) * 128], eyeb[:, :])
            inc("t", ins)

        @block.vector
        def _(v):
            wait(v, "t")
            pv0 = pk0[:, :].rearrange("p (f t) -> p f t", f=8)
            pv1 = pk1[:, :].rearrange("p (f t) -> p f t", f=8)
            if own_j is None:
                v.tensor_copy(xnTs[:, 0:1024], pk0[:, :])
                inc("v", v.tensor_copy(xnTs[:, 1024:2048], pk1[:, :]))
            else:
                xv = xnT[:, :].rearrange("p (f jt t) -> p f jt t", f=FC, jt=8)
                v.tensor_copy(xv[:, 0:8, own_j, :], pv0)
                inc("v", v.tensor_copy(xv[:, 8:16, own_j, :], pv1))
        set_guard("pk", "v")

    if stage >= 1:
        for tau in range(NT_CTX):
            half = tau % 2

            @block.sync
            def _(s, tau=tau, half=half):
                need(s, ("xt", half))
                s.dma_start(xt[:, half * CD:(half + 1) * CD],
                            x_ctx[tau * 128:(tau + 1) * 128, :]).then_inc(sems["l"], 16)
            cnt["l"] += 16
            ln_tile(half)
            set_guard(("xt", half), "v")
            xn_transposes(None)

            @block.tensor
            def _(t):
                wait(t, "v")
                t.wait_ge(sems["w"], 32)
                need(t, ("pq", 0))
                ins = None
                for fc in range(FC):
                    ins = t.matmul(pq0[:, 0:128], xnTs[:, fc * 128:(fc + 1) * 128],
                                   kvw[:, fc * 128:(fc + 1) * 128],
                                   start=(fc == 0), stop=(fc == FC - 1))
                inc("t", ins)

            sincos(posc[:, tau:tau + 1], invfk[:, :], 32, cosk[:, :], sink[:, :])

            @block.vector
            def _(v, tau=tau):
                wait(v, "t", "s")
                v.tensor_mul(tA[:, 0:32], pq0[:, 0:64:2], cosk[:, :])
                vseq(v, v.tensor_mul(tB[:, 0:32], pq0[:, 1:64:2], sink[:, :]))
                v.tensor_sub(kro[:, 0:32], tA[:, 0:32], tB[:, 0:32])
                v.tensor_mul(tA[:, 32:64], pq0[:, 0:64:2], sink[:, :])
                vseq(v, v.tensor_mul(tB[:, 32:64], pq0[:, 1:64:2], cosk[:, :]))
                v.tensor_add(kro[:, 32:64], tA[:, 32:64], tB[:, 32:64])
                inc("v", v.tensor_copy(vlat[:, tau * 64:(tau + 1) * 64], pq0[:, 64:128]))
            set_guard(("pq", 0), "v")

            @block.tensor
            def _(t):
                wait(t, "v")
                need(t, "pk")
                t.transpose(pk0[0:64, 0:128], kro[:, :], eyeb[:, :])
                inc("t", t.transpose(pk0[64:128, 0:128], kro[:, :], eyeb[:, :],
                                     tile_position=(0, 64)))

            @block.vector
            def _(v, tau=tau):
                wait(v, "t")
                inc("v", v.tensor_copy(klatT[:, tau * 128:(tau + 1) * 128], pk0[:, 0:128]))
            set_guard("pk", "v")

    if stage >= 1:
        for j in range(NT_OWN):
            half = j % 2

            @block.sync
            def _(s, j=j, half=half):
                need(s, ("xt", half))
                s.dma_start(xt[:, half * CD:(half + 1) * CD],
                            x_own[j * 128:(j + 1) * 128, :]).then_inc(sems["l"], 16)
            cnt["l"] += 16
            ln_tile(half)
            set_guard(("xt", half), "v")
            xn_transposes(own_j=j)

            sincos(poso[:, j:j + 1], invf[:, :], 64,
                   cosqa[:, j * 64:(j + 1) * 64], sinqa[:, j * 64:(j + 1) * 64])

            @block.tensor
            def _(t, j=j):
                t.wait_ge(sems["w"], P0_W)
                wait(t, "v")
                need(t, "prt")
                ins = None
                for fc in range(FC):
                    ins = t.matmul(prt[:, 0:16], xnT[:, fc * 1024 + j * 128: fc * 1024 + (j + 1) * 128],
                                   rwT[:, fc * 16:(fc + 1) * 16],
                                   start=(fc == 0), stop=(fc == FC - 1))
                inc("t", ins)

            @block.vector
            def _(v):
                wait(v, "t", "v")
                v.wait_ge(sems["l"], P0_L)
                lg, cur = tA[:, 0:16], tA[:, 16:32]
                nm1, nmk, cmp = tA[:, 32:33], tA[:, 33:34], tA[:, 34:50]
                vseq(v, v.tensor_add(lg, prt[:, 0:16], rbias_bc[:, :]))
                vseq(v, v.reduce_max(nm1, lg, axis=AX, negate=True))
                vseq(v, v.tensor_copy(cur, lg))
                for it in range(3):
                    nm = nm1 if it == 0 else nmk
                    vseq(v, v.tensor_scalar(cmp, cur, nm, 0.0, ALU.add, ALU.is_ge))
                    vseq(v, v.scalar_tensor_tensor(cur, cmp, NEG, cur, ALU.mult, ALU.add))
                    vseq(v, v.reduce_max(nmk, cur, axis=AX, negate=True))
                inc("v", v.tensor_scalar(cmp, lg, nmk, 0.0, ALU.add, ALU.is_ge))

            @block.scalar
            def _(sc):
                wait(sc, "v")
                inc("s", sc.activation(tA[:, 50:66], tA[:, 0:16], AF.Exp, bias=tA[:, 32:33]))

            @block.vector
            def _(v, j=j):
                wait(v, "s")
                ew, ssum, rs = tA[:, 50:66], tA[:, 66:67], tA[:, 67:68]
                vseq(v, v.tensor_mul(ew, ew, tA[:, 34:50]))
                vseq(v, v.reduce_sum(ssum, ew, axis=AX))
                vseq(v, v.reciprocal(rs, ssum))
                vseq(v, v.tensor_scalar(scoresR[:, :], ew, rs, None, ALU.mult))
                v.tensor_copy(maskRf[:, j * 16:(j + 1) * 16], tA[:, 34:50])
                v.tensor_copy(maskRb[:, j * 16:(j + 1) * 16], tA[:, 34:50])
                inc("v", v.tensor_scalar_mul(scoresP[:, j * 16:(j + 1) * 16], scoresR[:, :],
                                             1.0 / KAPPA))

            @block.tensor
            def _(t):
                wait(t, "v")
                inc("t", t.transpose(prt[0:16, :], scoresR[:, :], eyef[:, :]))

            @block.vector
            def _(v, j=j):
                wait(v, "t")
                inc("v", v.tensor_copy(scoresT[:, j * 128:(j + 1) * 128], prt[0:16, :]))
            set_guard("prt", "v")

    if stage >= 1:
        XNT_DONE_V = cnt["v"]
        for cc in range(4):
            bh = cc % 2

            @block.gpsimd
            def _(g, cc=cc, bh=bh):
                need(g, ("bigq", bh))
                g.dma_start(big[:, bh * 8192:(bh + 1) * 8192],
                            wq_d[:, cc * 8192:(cc + 1) * 8192]).then_inc(sems["w"], 16)
            cnt["w"] += 16
            WQ_W = cnt["w"]

            for j in range(NT_OWN):
                ji = j % 2
                pq = pq0 if ji == 0 else pq1

                @block.tensor
                def _(t, j=j, cc=cc, bh=bh, pq=pq, ji=ji, WQ_W=WQ_W):
                    t.wait_ge(sems["w"], WQ_W)
                    t.wait_ge(sems["v"], XNT_DONE_V)
                    need(t, ("pq", ji))
                    ins = None
                    for fc in range(FC):
                        ins = t.matmul(pq[:, :], xnT[:, fc * 1024 + j * 128: fc * 1024 + (j + 1) * 128],
                                       big[:, bh * 8192 + fc * 512: bh * 8192 + (fc + 1) * 512],
                                       start=(fc == 0), stop=(fc == FC - 1))
                    inc("t", ins)

                @block.vector
                def _(v, j=j, pq=pq):
                    wait(v, "t")
                    need(v, "qro4")
                    cq = cosqa[:, j * 64:(j + 1) * 64]
                    sq = sinqa[:, j * 64:(j + 1) * 64]
                    for hh in range(4):
                        x1 = pq[:, hh * 128 + 0: hh * 128 + 128: 2]
                        x2 = pq[:, hh * 128 + 1: hh * 128 + 128: 2]
                        v.tensor_mul(tA[:, 0:64], x1, cq)
                        vseq(v, v.tensor_mul(tB[:, 0:64], x2, sq))
                        v.tensor_sub(qro4[:, hh * 128: hh * 128 + 64], tA[:, 0:64], tB[:, 0:64])
                        v.tensor_mul(tA[:, 64:128], x1, sq)
                        vseq(v, v.tensor_mul(tB[:, 64:128], x2, cq))
                        v.tensor_add(qro4[:, hh * 128 + 64: hh * 128 + 128], tA[:, 64:128], tB[:, 64:128])
                    inc("v", v.engine_nop())
                set_guard(("pq", ji), "v")

                @block.tensor
                def _(t):
                    wait(t, "v")
                    need(t, "pk")
                    ins = None
                    for hh in range(4):
                        ins = t.transpose(pk0[:, hh * 128:(hh + 1) * 128],
                                          qro4[:, hh * 128:(hh + 1) * 128], eyeb[:, :])
                    inc("t", ins)
                set_guard("qro4", "t")

                @block.vector
                def _(v):
                    wait(v, "t")
                    need(v, "qT4")
                    inc("v", v.tensor_copy(qT4[:, :], pk0[:, 0:512]))
                set_guard("pk", "v")

                for hh in range(4):
                    h = cc * 4 + hh
                    po = h % 2

                    @block.tensor
                    def _(t, h=h, hh=hh, po=po):
                        wait(t, "v")
                        need(t, "pqt")
                        dst = pqt[64:128, :] if po else pqt[0:64, :]
                        inc("t", t.matmul(dst, kexp[:, h * 64:(h + 1) * 64],
                                          qT4[:, hh * 128:(hh + 1) * 128], start=True, stop=True,
                                          tile_position=(0, 64) if po else None))

                    @block.vector
                    def _(v, h=h, j=j, po=po):
                        wait(v, "t")
                        src = pqt[64:128, :] if po else pqt[0:64, :]
                        dst = qtl[po * 64:po * 64 + 64,
                                  (h // 2) * 1024 + j * 128:(h // 2) * 1024 + (j + 1) * 128]
                        inc("v", v.tensor_copy(dst, src))
                    set_guard("pqt", "v")
                set_guard("qT4", "t")
            set_guard(("bigq", bh), "t")

    if stage >= 2:
        p12.close()
        s12.close()
        open_stacks.remove(p12)
        open_stacks.remove(s12)
        s3 = ExitStack()
        open_stacks.append(s3)
        mask_sb = sb(s3, "mask_sb", [128, 512], F32)
        scores2 = sb(s3, "scores2", [128, 2 * 4096], F32)
        att = [sb(s3, "att0", [128, 512], BF16), sb(s3, "att1", [128, 512], BF16)]
        attT = [sb(s3, "attT0", [128, 512], BF16), sb(s3, "attT1", [128, 512], BF16)]
        nm2 = sb(s3, "nm2", [128, 32], F32)
        ybarb = sb(s3, "ybarb", [128, 128], BF16)
        ybT_sb = sb(s3, "ybT_sb", [64, 128], BF16)

        p3 = ExitStack()
        open_stacks.append(p3)
        ps_s = [psmt(p3, "ps_s0", [128, 512], F32), psmt(p3, "ps_s1", [128, 512], F32)]
        ps_aT = [psmt(p3, "ps_aT0", [128, 512], BF16), psmt(p3, "ps_aT1", [128, 512], BF16)]
        ps_y = [psmt(p3, "ps_y0", [128, 64], F32), psmt(p3, "ps_y1", [128, 64], F32)]
        ps_bt = psmt(p3, "ps_bt", [64, 128], BF16)
        ps_yh = psmt(p3, "ps_yh", [128, 128], F32)

        barrier()

        for j in range(NT_OWN):
            ext = j + 1

            @block.sync
            def _(s_, j=j):
                need(s_, "mask_sb")
                s_.dma_start(mask_sb[:, :], maskd[:, j * 512:(j + 1) * 512]).then_inc(sems["l"], 16)
            cnt["l"] += 16
            MASK_L = cnt["l"]

            for hp in range(H // 2):
                strm = []
                for st_ in (0, 1):
                    h = 2 * hp + st_
                    strm.append({"h": h, "hpart": (h % 2) * 64,
                                 "hcol": (h // 2) * 1024 + j * 128,
                                 "sc": scores2[:, st_ * 4096:(st_ + 1) * 4096],
                                 "nmo": st_ * 16})

                for c in range(ext):
                    last = (c == j)
                    for st_ in (0, 1):
                        d = strm[st_]

                        @block.tensor
                        def _(t, d=d, c=c, st_=st_):
                            need(t, ("ps_s", st_))
                            need(t, ("scores", st_))
                            inc("t", t.matmul(ps_s[st_][:, :],
                                              qtl[d["hpart"]:d["hpart"] + 64, d["hcol"]:d["hcol"] + 128],
                                              klatT[d["hpart"]:d["hpart"] + 64, c * 512:(c + 1) * 512],
                                              start=True, stop=True))
                        d["mmv"] = cnt["t"]

                        @block.vector
                        def _(v, d=d, c=c, st_=st_, last=last, MASK_L=MASK_L):
                            v.wait_ge(sems["t"], d["mmv"])
                            dst = d["sc"][:, c * 512:(c + 1) * 512]
                            if last:
                                v.wait_ge(sems["l"], MASK_L)
                                inc("v", v.tensor_add(dst, ps_s[st_][:, :], mask_sb[:, :]))
                            else:
                                inc("v", v.tensor_copy(dst, ps_s[st_][:, :]))
                        d["evv"] = cnt["v"]
                        guards[("ps_s", st_)] = ("v", cnt["v"])

                for st_ in (0, 1):
                    d = strm[st_]
                    nmo = d["nmo"]

                    @block.vector
                    def _(v, d=d, nmo=nmo, ext=ext):
                        v.wait_ge(sems["v"], d["evv"])
                        need(v, ("nmg", d["nmo"]))
                        vseq(v, v.reduce_max(nm2[:, nmo:nmo + 1], d["sc"][:, :ext * 512],
                                             axis=AX, negate=True))
                        inc("v", v.tensor_scalar_mul(nm2[:, nmo + 1:nmo + 2], nm2[:, nmo:nmo + 1], SCALE))
                    d["biasv"] = cnt["v"]

                for c in range(ext):
                    @block.scalar
                    def _(sc_, c=c, strm=strm):
                        for st_ in (0, 1):
                            d = strm[st_]
                            nmo = d["nmo"]
                            sc_.wait_ge(sems["v"], d["biasv"])
                            need(sc_, ("att", st_))
                            inc("s", sc_.activation(att[st_][:, :], d["sc"][:, c * 512:(c + 1) * 512],
                                                    AF.Exp, scale=SCALE, bias=nm2[:, nmo + 1:nmo + 2],
                                                    accum_out=nm2[:, nmo + 8 + c:nmo + 9 + c]))
                            d["expv"] = cnt["s"]
                    for st_ in (0, 1):
                        d = strm[st_]
                        nmo = d["nmo"]

                        @block.tensor
                        def _(t, d=d, st_=st_):
                            t.wait_ge(sems["s"], d["expv"])
                            need(t, ("ps_aT", st_))
                            ins = None
                            for kk in range(4):
                                ins = t.transpose(ps_aT[st_][:, kk * 128:(kk + 1) * 128],
                                                  att[st_][:, kk * 128:(kk + 1) * 128], eyeb[:, :])
                            inc("t", ins)
                        d["aTv"] = cnt["t"]
                        guards[("att", st_)] = ("t", cnt["t"])

                        @block.vector
                        def _(v, d=d, st_=st_):
                            v.wait_ge(sems["t"], d["aTv"])
                            need(v, ("attT", st_))
                            inc("v", v.tensor_copy(attT[st_][:, :], ps_aT[st_][:, :]))
                        d["evv2"] = cnt["v"]
                        guards[("ps_aT", st_)] = ("v", cnt["v"])

                        @block.tensor
                        def _(t, d=d, st_=st_, c=c, j=j):
                            t.wait_ge(sems["v"], d["evv2"])
                            if c == 0:
                                need(t, ("ps_y", st_))
                            ins = None
                            for kk in range(4):
                                ins = t.matmul(ps_y[st_][:, :], attT[st_][:, kk * 128:(kk + 1) * 128],
                                               vlat[:, (4 * c + kk) * 64:(4 * c + kk + 1) * 64],
                                               start=(c == 0 and kk == 0), stop=(c == j and kk == 3))
                            inc("t", ins)
                        d["ytv"] = cnt["t"]
                        guards[("attT", st_)] = ("t", cnt["t"])
                    if c == j:
                        for st_ in (0, 1):
                            guards[("scores", st_)] = ("s", strm[st_]["expv"])

                for st_ in (0, 1):
                    d = strm[st_]
                    nmo = d["nmo"]

                    @block.vector
                    def _(v, d=d, st_=st_, nmo=nmo, ext=ext):
                        v.wait_ge(sems["s"], d["expv"])
                        v.wait_ge(sems["t"], d["ytv"])
                        vseq(v, v.reduce_sum(nm2[:, nmo + 2:nmo + 3], nm2[:, nmo + 8:nmo + 8 + ext], axis=AX))
                        vseq(v, v.reciprocal(nm2[:, nmo + 3:nmo + 4], nm2[:, nmo + 2:nmo + 3]))
                        inc("v", v.tensor_scalar(ybarb[:, st_ * 64:(st_ + 1) * 64], ps_y[st_][:, :],
                                                 nm2[:, nmo + 3:nmo + 4], None, ALU.mult))
                    d["tl1"] = cnt["v"]
                    guards[("ps_y", st_)] = ("v", cnt["v"])
                    guards[("nmg", nmo)] = ("v", cnt["v"])

                    @block.tensor
                    def _(t, d=d, st_=st_):
                        t.wait_ge(sems["v"], d["tl1"])
                        need(t, "ps_bt")
                        inc("t", t.transpose(ps_bt[:, :], ybarb[:, st_ * 64:(st_ + 1) * 64], eyeb[:, :]))
                    d["tl2"] = cnt["t"]

                    @block.vector
                    def _(v, d=d):
                        v.wait_ge(sems["t"], d["tl2"])
                        inc("v", v.tensor_copy(ybT_sb[:, :], ps_bt[:, :]))
                    d["tl3"] = cnt["v"]
                    guards["ps_bt"] = ("v", cnt["v"])

                    @block.tensor
                    def _(t, d=d):
                        t.wait_ge(sems["v"], d["tl3"])
                        need(t, "ps_yh")
                        inc("t", t.matmul(ps_yh[:, :], vexpb[:, d["h"] * 128:(d["h"] + 1) * 128],
                                          ybT_sb[:, :], start=True, stop=True))
                    d["tl4"] = cnt["t"]

                    @block.vector
                    def _(v, d=d, j=j):
                        v.wait_ge(sems["t"], d["tl4"])
                        inc("v", v.tensor_copy(big[:, d["h"] * 1024 + j * 128: d["h"] * 1024 + (j + 1) * 128],
                                               ps_yh[:, :]))
                    guards["ps_yh"] = ("v", cnt["v"])
            set_guard("mask_sb", "v")

        p3.close()
        s3.close()
        open_stacks.remove(p3)
        open_stacks.remove(s3)
        barrier()
        s3p = ExitStack()
        open_stacks.append(s3p)
        pslot = sb(s3p, "pslot", [128, FC * 512], BF16)
        pso = [psmt(s3p, "pso0", [128, 512], F32), psmt(s3p, "pso1", [128, 512], F32)]
        for ccp in range(4):
            @block.gpsimd
            def _(g, ccp=ccp):
                need(g, "pslot")
                g.dma_start(pslot[:, :], proj_d[:, ccp * 8192:(ccp + 1) * 8192]).then_inc(sems["w"], 16)
            cnt["w"] += 16
            PW = cnt["w"]

            for j in range(NT_OWN):
                @block.tensor
                def _(t, j=j, PW=PW):
                    t.wait_ge(sems["w"], PW)
                    wait(t, "v")
                    need(t, ("pso", j % 2))
                    ins = None
                    for hh in range(H):
                        ins = t.matmul(pso[j % 2][:, :],
                                       big[:, hh * 1024 + j * 128: hh * 1024 + (j + 1) * 128],
                                       pslot[:, hh * 512:(hh + 1) * 512],
                                       start=(hh == 0), stop=(hh == H - 1))
                    inc("t", ins)

                @block.vector
                def _(v, j=j, ccp=ccp):
                    wait(v, "t")
                    inc("v", v.tensor_copy(accb[:, j * CD + ccp * 512: j * CD + (ccp + 1) * 512],
                                           pso[j % 2][:, :]))
                set_guard(("pso", j % 2), "v")
            set_guard("pslot", "t")

    if stage >= 3:
        s3p.close()
        ast.close()
        for st_ in (s3p, ast):
            open_stacks.remove(st_)
        s4 = ExitStack()
        open_stacks.append(s4)
        ws = sb(s4, "ws", [128, 4 * 4096], BF16)
        bbg = sb(s4, "bbg", [128, 512], F32)
        bbu = sb(s4, "bbu", [128, 512], F32)
        ga = sb(s4, "ga", [128, 3 * 512], F32)
        gs = sb(s4, "gs", [128, 3 * 512], F32)
        ua = sb(s4, "ua", [128, 3 * 512], F32)
        actw = sb(s4, "actw", [128, 3 * 512], BF16)
        ltri_sb = sb(s4, "ltri_sb", [128, 128], BF16)
        allon = sb(s4, "allon", [128, 128], BF16)
        onesf = sb(s4, "onesf", [1, 128], F32)
        iota_sb = sb(s4, "iota_sb", [128, 384], F32)
        piotak_sb = sb(s4, "piotak_sb", [128, 4], F32)
        rank_tok = sb(s4, "rank_tok", [128, 128], F32)
        rm1m = sb(s4, "rm1m", [128, 128], F32)
        qbuf = sb(s4, "qbuf", [128, 8 * 384], BF16)
        dtb = sb(s4, "dtb", [128, 3 * 512], BF16)
        actwT = xnT[:, 0:16 * 384]
        gfm = xnT[:, 16 * 384:32 * 384]
        qwt = xnT[:, 32 * 384:32 * 384 + 3 * 1024]

        p4 = ExitStack()
        open_stacks.append(p4)
        pf = [psmt(p4, "pf%d" % i, [128, 512], F32) for i in range(6)]
        rm1row = ga[0:1, 0:1024]
        scorow = ua[0:1, 512:1536]
        pat = [psmt(p4, "pat0", [128, 512], BF16), psmt(p4, "pat1", [128, 512], BF16)]

        barrier()

        @block.gpsimd
        def _(g):
            g.dma_start(ltri_sb[:, :], ltri_d).then_inc(sems["w"], 16)
        cnt["w"] += 16
        LT_W = cnt["w"]

        @block.sync
        def _(s_):
            s_.dma_start(iota_sb[:, :], iota_d).then_inc(sems["l"], 16)
            s_.dma_start(piotak_sb[:, :], piotak_d).then_inc(sems["l"], 16)
        cnt["l"] += 32
        P4_L = cnt["l"]

        @block.vector
        def _(v):
            v.memset(allon[:, :], 1.0)
            inc("v", v.memset(onesf[:, :], 1.0))

        for j in range(NT_OWN):
            for fcg in range(4):
                pi4 = fcg % 2

                @block.tensor
                def _(t, j=j, fcg=fcg, pi4=pi4):
                    need(t, ("pat", pi4))
                    ins = None
                    for i4 in range(4):
                        fc = fcg * 4 + i4
                        ins = t.transpose(pat[pi4][:, i4 * 128:(i4 + 1) * 128],
                                          xnT[:, fc * 1024 + j * 128: fc * 1024 + (j + 1) * 128],
                                          eyeb[:, :])
                    inc("t", ins)
                XT_T = cnt["t"]

                @block.vector
                def _(v, j=j, fcg=fcg, pi4=pi4, XT_T=XT_T):
                    v.wait_ge(sems["t"], XT_T)
                    inc("v", v.tensor_copy(big[:, j * CD + fcg * 512: j * CD + (fcg + 1) * 512],
                                           pat[pi4][:, :]))
                guards[("pat", pi4)] = ("v", cnt["v"])
        XNTOK_V = cnt["v"]

        for J in range(NT_OWN):
            @block.tensor
            def _(t, J=J, LT_W=LT_W):
                t.wait_ge(sems["w"], LT_W)
                wait(t, "v")
                need(t, ("pf", 0))
                ins = None
                for I in range(J + 1):
                    rhs = maskRb[:, I * 16:(I + 1) * 16]
                    lhs = ltri_sb[:, :] if I == J else allon[:, :]
                    ins = t.matmul(pf[0][:, 0:16], lhs, rhs, start=(I == 0), stop=(I == J))
                inc("t", ins)
            RK_T = cnt["t"]

            @block.vector
            def _(v, J=J, RK_T=RK_T):
                v.wait_ge(sems["t"], RK_T)
                inc("v", v.tensor_copy(rank_tok[:, J * 16:(J + 1) * 16], pf[0][:, 0:16]))
            guards[("pf", 0)] = ("v", cnt["v"])

        @block.vector
        def _(v):
            wait(v, "v")
            vseq(v, v.scalar_tensor_tensor(rm1m[:, :], rank_tok[:, :], 1.0, maskRf[:, :],
                                           ALU.mult, ALU.mult))
            inc("v", v.tensor_scalar_add(rm1m[:, :], rm1m[:, :], -1.0))
        RM_V = cnt["v"]

        slot_i = [0]

        def load_half(src_ap):
            si = slot_i[0] % 4
            slot_i[0] += 1

            @block.gpsimd
            def _(g, si=si):
                need(g, ("ws", si))
                g.dma_start(ws[:, si * 4096:(si + 1) * 4096], src_ap).then_inc(sems["w"], 16)
            cnt["w"] += 16
            return si, cnt["w"]

        prev_aT = [None]

        def emit_prev_aT():
            if prev_aT[0] is None:
                return
            dcp_, avs = prev_aT[0]
            prev_aT[0] = None
            for k in range(3):
                pi4 = k % 2

                @block.tensor
                def _(t, k=k, pi4=pi4, av=avs[k]):
                    t.wait_ge(sems["v"], av)
                    need(t, ("pat", pi4))
                    ins = None
                    for db in range(4):
                        ins = t.transpose(pat[pi4][:, db * 128:(db + 1) * 128],
                                          actw[:, k * 512 + db * 128: k * 512 + (db + 1) * 128],
                                          eyeb[:, :])
                    inc("t", ins)
                AT_T = cnt["t"]

                @block.vector
                def _(v, dcp_=dcp_, k=k, pi4=pi4, AT_T=AT_T):
                    v.wait_ge(sems["t"], AT_T)
                    bv = actwT.rearrange("p (D kt s) -> p D kt s", D=FC, kt=3)
                    inc("v", v.tensor_copy(bv[:, dcp_ * 4:(dcp_ + 1) * 4, k, :],
                                           pat[pi4][:, :].rearrange("p (db s) -> p db s", db=4)))
                guards[("pat", pi4)] = ("v", cnt["v"])
                guards[("actw", k)] = ("v", cnt["v"])

        for e in range(16):
            for Jg in range(2):
                @block.tensor
                def _(t, e=e, Jg=Jg, RM_V=RM_V):
                    t.wait_ge(sems["v"], RM_V)
                    need(t, ("pf", 0))
                    need(t, ("pf", 1))
                    ins = None
                    for i4 in range(4):
                        J = Jg * 4 + i4
                        t.transpose(pf[0][0:1, i4 * 128:(i4 + 1) * 128],
                                    rm1m[:, J * 16 + e:J * 16 + e + 1], eyef[:, :])
                        ins = t.transpose(pf[1][0:1, i4 * 128:(i4 + 1) * 128],
                                          scoresP[:, J * 16 + e:J * 16 + e + 1], eyef[:, :])
                    inc("t", ins)
                RW_T = cnt["t"]

                @block.vector
                def _(v, Jg=Jg, RW_T=RW_T):
                    v.wait_ge(sems["t"], RW_T)
                    v.tensor_copy(ga[0:1, Jg * 512:(Jg + 1) * 512], pf[0][0:1, 0:512])
                    inc("v", v.tensor_copy(ua[0:1, 512 + Jg * 512:512 + (Jg + 1) * 512], pf[1][0:1, 0:512]))
                guards[("pf", 0)] = ("v", cnt["v"])
                guards[("pf", 1)] = ("v", cnt["v"])
            ROW_V = cnt["v"]

            @block.tensor
            def _(t, ROW_V=ROW_V):
                t.wait_ge(sems["v"], ROW_V)
                for i in range(4):
                    need(t, ("pf", i))
                t.matmul(pf[0][:, :], onesf[:, :], ga[0:1, 0:512], start=True, stop=True)
                t.matmul(pf[1][:, :], onesf[:, :], ga[0:1, 512:1024], start=True, stop=True)
                t.matmul(pf[2][:, :], onesf[:, :], ua[0:1, 512:1024], start=True, stop=True)
                inc("t", t.matmul(pf[3][:, :], onesf[:, :], ua[0:1, 1024:1536],
                                  start=True, stop=True))
            BC_T = cnt["t"]

            @block.vector
            def _(v, BC_T=BC_T):
                v.wait_ge(sems["t"], BC_T)
                need(v, "qwt")
                for k in range(3):
                    for hf in range(2):
                        vseq(v, v.tensor_scalar(ua[:, 0:512], pf[hf][:, :],
                                                piotak_sb[:, k:k + 1], 0.0,
                                                ALU.subtract, ALU.is_equal))
                        v.tensor_mul(qwt[:, k * 1024 + hf * 512: k * 1024 + (hf + 1) * 512],
                                     ua[:, 0:512], pf[2 + hf][:, :])
                inc("v", v.engine_nop())
            for i in range(4):
                guards[("pf", i)] = ("v", cnt["v"])

            @block.vector
            def _(v, e=e):
                v.wait_ge(sems["l"], P4_L)
                need(v, "qbuf")
                ins = None
                for J in range(NT_OWN):
                    ins = v.tensor_scalar(qbuf[:, J * 384:(J + 1) * 384], iota_sb[:, :],
                                          rm1m[:, J * 16 + e:J * 16 + e + 1], 0.0,
                                          ALU.subtract, ALU.is_equal)
                inc("v", ins)
            QB_V = cnt["v"]

            for fc in range(FC):
                gi = fc % 2

                @block.tensor
                def _(t, fc=fc, gi=gi, QB_V=QB_V):
                    t.wait_ge(sems["v"], max(QB_V, XNTOK_V))
                    need(t, ("pf", gi))
                    ins = None
                    for J in range(NT_OWN):
                        ins = t.matmul(pf[gi][:, 0:384],
                                       big[:, J * CD + fc * 128: J * CD + (fc + 1) * 128],
                                       qbuf[:, J * 384:(J + 1) * 384],
                                       start=(J == 0), stop=(J == NT_OWN - 1))
                    inc("t", ins)
                GA_T = cnt["t"]

                @block.vector
                def _(v, fc=fc, gi=gi, GA_T=GA_T):
                    v.wait_ge(sems["t"], GA_T)
                    inc("v", v.tensor_copy(gfm[:, fc * 384:(fc + 1) * 384], pf[gi][:, 0:384]))
                guards[("pf", gi)] = ("v", cnt["v"])
            guards["qbuf"] = ("t", cnt["t"])
            GF_V = cnt["v"]

            for dc in range(4):
                sgA, gA_n = load_half(wg_d[e, :, dc * 8192: dc * 8192 + 4096])
                sgB, gB_n = load_half(wg_d[e, :, dc * 8192 + 4096: (dc + 1) * 8192])
                suA, uA_n = load_half(wu_d[e, :, dc * 8192: dc * 8192 + 4096])
                suB, uB_n = load_half(wu_d[e, :, dc * 8192 + 4096: (dc + 1) * 8192])
                dcp = dc % 2

                @block.sync
                def _(s_, e=e, dc=dc):
                    need(s_, "bb")
                    s_.dma_start(bbg[:, :],
                                 bass.AP(biasg_d.tensor, e * CD + dc * 512, [[0, 128], [1, 512]])).then_inc(sems["l"], 16)
                    s_.dma_start(bbu[:, :],
                                 bass.AP(biasu_d.tensor, e * CD + dc * 512, [[0, 128], [1, 512]])).then_inc(sems["l"], 16)
                cnt["l"] += 32
                BB_L = cnt["l"]

                @block.tensor
                def _(t, sgA=sgA, sgB=sgB, suA=suA, suB=suB,
                      gA_n=gA_n, gB_n=gB_n, uA_n=uA_n, uB_n=uB_n, GF_V=GF_V):
                    t.wait_ge(sems["v"], GF_V)
                    for i in range(6):
                        need(t, ("pf", i))
                    ins = None
                    for fc in range(FC):
                        if fc == 0:
                            t.wait_ge(sems["w"], max(gA_n, uA_n))
                        if fc == 8:
                            t.wait_ge(sems["w"], max(gB_n, uB_n))
                        sg = sgA if fc < 8 else sgB
                        su = suA if fc < 8 else suB
                        col = (fc % 8) * 512
                        for k in range(3):
                            t.matmul(pf[k][:, :],
                                     gfm[:, fc * 384 + k * 128: fc * 384 + (k + 1) * 128],
                                     ws[:, sg * 4096 + col: sg * 4096 + col + 512],
                                     start=(fc == 0), stop=(fc == FC - 1))
                        for k in range(3):
                            ins = t.matmul(pf[3 + k][:, :],
                                           gfm[:, fc * 384 + k * 128: fc * 384 + (k + 1) * 128],
                                           ws[:, su * 4096 + col: su * 4096 + col + 512],
                                           start=(fc == 0), stop=(fc == FC - 1))
                        if fc == 7:
                            inc("t", ins)
                            guards[("ws", sgA)] = ("t", cnt["t"])
                            guards[("ws", suA)] = ("t", cnt["t"])
                    inc("t", ins)
                GU_T = cnt["t"]
                set_guard(("ws", sgB), "t")
                set_guard(("ws", suB), "t")

                emit_prev_aT()

                @block.vector
                def _(v, dcp=dcp, BB_L=BB_L, GU_T=GU_T):
                    v.wait_ge(sems["t"], GU_T)
                    v.wait_ge(sems["l"], BB_L)
                    for k in range(3):
                        need(v, ("actw", k))
                    for k in range(3):
                        v.tensor_add(ga[:, k * 512:(k + 1) * 512], pf[k][:, :], bbg[:, :])
                        v.tensor_add(ua[:, k * 512:(k + 1) * 512], pf[3 + k][:, :], bbu[:, :])
                    inc("v", v.engine_nop())
                ADDS_V = cnt["v"]
                for k in range(3):
                    guards[("pf", k)] = ("v", cnt["v"])
                    guards[("pf", 3 + k)] = ("v", cnt["v"])
                set_guard("bb", "v")

                @block.vector
                def _(v, ADDS_V=ADDS_V):
                    for k in range(3):
                        v.tensor_scalar_min(ga[:, k * 512:(k + 1) * 512],
                                            ga[:, k * 512:(k + 1) * 512], LIMIT)
                    inc("v", v.engine_nop())
                GAC_V = cnt["v"]

                @block.scalar
                def _(sc, GAC_V=GAC_V):
                    sc.wait_ge(sems["v"], GAC_V)
                    ins = None
                    for k in range(3):
                        ins = sc.activation(gs[:, k * 512:(k + 1) * 512],
                                            ga[:, k * 512:(k + 1) * 512], AF.Silu, scale=KAPPA)
                    inc("s", ins)
                SIL_S = cnt["s"]

                avs = []
                for k in range(3):
                    @block.vector
                    def _(v, k=k, SIL_S=SIL_S):
                        v.wait_ge(sems["s"], SIL_S)
                        v.tensor_scalar(ua[:, k * 512:(k + 1) * 512], ua[:, k * 512:(k + 1) * 512],
                                        LIMIT, -LIMIT, ALU.min, ALU.max)
                        inc("v", v.scalar_tensor_tensor(actw[:, k * 512:(k + 1) * 512],
                                                        ua[:, k * 512:(k + 1) * 512],
                                                        1.0, gs[:, k * 512:(k + 1) * 512],
                                                        ALU.add, ALU.mult))
                    avs.append(cnt["v"])
                prev_aT[0] = (dc, avs)

            emit_prev_aT()
            ACT_DONE_V = cnt["v"]

            for hc in range(4):
                sdA, dA_n = load_half(down_d[e, :, hc * 8192: hc * 8192 + 4096])
                sdB, dB_n = load_half(down_d[e, :, hc * 8192 + 4096: (hc + 1) * 8192])

                @block.tensor
                def _(t, sdA=sdA, sdB=sdB, dA_n=dA_n, dB_n=dB_n, ACT_DONE_V=ACT_DONE_V):
                    t.wait_ge(sems["v"], ACT_DONE_V)
                    for k in range(3):
                        need(t, ("pf", k))
                    ins = None
                    for D in range(FC):
                        if D == 0:
                            t.wait_ge(sems["w"], dA_n)
                        if D == 8:
                            t.wait_ge(sems["w"], dB_n)
                        sd = sdA if D < 8 else sdB
                        col = (D % 8) * 512
                        for k in range(3):
                            ins = t.matmul(pf[k][:, :],
                                           actwT[:, D * 384 + k * 128: D * 384 + (k + 1) * 128],
                                           ws[:, sd * 4096 + col: sd * 4096 + col + 512],
                                           start=(D == 0), stop=(D == FC - 1))
                        if D == 7:
                            inc("t", ins)
                            guards[("ws", sdA)] = ("t", cnt["t"])
                    inc("t", ins)
                DW_T = cnt["t"]
                set_guard(("ws", sdB), "t")

                @block.vector
                def _(v, DW_T=DW_T):
                    v.wait_ge(sems["t"], DW_T)
                    need(v, "dtb")
                    for k in range(3):
                        v.tensor_copy(dtb[:, k * 512:(k + 1) * 512], pf[k][:, :])
                    inc("v", v.engine_nop())
                DT_V = cnt["v"]
                for k in range(3):
                    guards[("pf", k)] = ("v", cnt["v"])

                for t8 in range(NT_OWN):
                    sb_ = 3 + t8 % 2

                    @block.tensor
                    def _(t, t8=t8, sb_=sb_, DT_V=DT_V):
                        t.wait_ge(sems["v"], DT_V)
                        need(t, ("pf", sb_))
                        ins = None
                        for k in range(3):
                            ins = t.matmul(pf[sb_][:, :],
                                           qwt[:, k * 1024 + t8 * 128: k * 1024 + (t8 + 1) * 128],
                                           dtb[:, k * 512:(k + 1) * 512],
                                           start=(k == 0), stop=(k == 2))
                        inc("t", ins)
                    SC_T = cnt["t"]
                    if t8 == NT_OWN - 1:
                        guards["dtb"] = ("t", cnt["t"])

                    @block.vector
                    def _(v, t8=t8, hc=hc, sb_=sb_, SC_T=SC_T):
                        v.wait_ge(sems["t"], SC_T)
                        dst = accb[:, t8 * CD + hc * 512: t8 * CD + (hc + 1) * 512]
                        inc("v", v.tensor_add(dst, dst, pf[sb_][:, :]))
                    guards[("pf", sb_)] = ("v", cnt["v"])
            guards["qwt"] = ("t", cnt["t"])

        s4c = ExitStack()
        open_stacks.append(s4c)
        dbias_sb = sb(s4c, "dbias_sb", [16, CD], BF16)

        @block.gpsimd
        def _(g):
            g.dma_start(dbias_sb[:, :], dbias_d).then_inc(sems["w"], 16)
        cnt["w"] += 16

        for t8 in range(8):
            for hc in range(4):
                @block.tensor
                def _(t, t8=t8, hc=hc):
                    wait(t, "w", "v")
                    need(t, ("pf", 0))
                    inc("t", t.matmul(pf[0][:, :], scoresT[:, t8 * 128:(t8 + 1) * 128],
                                      dbias_sb[:, hc * 512:(hc + 1) * 512], start=True, stop=True))
                DB_T = cnt["t"]

                @block.vector
                def _(v, t8=t8, hc=hc, DB_T=DB_T):
                    v.wait_ge(sems["t"], DB_T)
                    dst = accb[:, t8 * CD + hc * 512: t8 * CD + (hc + 1) * 512]
                    inc("v", v.tensor_add(dst, dst, pf[0][:, :]))
                guards[("pf", 0)] = ("v", cnt["v"])

    if stage >= 3:
        s4c.close()
        p4.close()
        s4.close()
        for st_ in (s4c, p4, s4):
            open_stacks.remove(st_)
        barrier()
        s5 = ExitStack()
        open_stacks.append(s5)
        xf = sb(s5, "xf", [128, 2 * CD], F32)
        ob = sb(s5, "ob", [128, 2 * CD], F32)

        for j in range(NT_OWN):
            half = j % 2

            @block.sync
            def _(s, j=j, half=half):
                need(s, ("xf", half))
                s.dma_start(xf[:, half * CD:(half + 1) * CD],
                            x_own[j * 128:(j + 1) * 128, :]).then_inc(sems["l"], 16)
            cnt["l"] += 16

            @block.vector
            def _(v, j=j, half=half):
                wait(v, "l")
                need(v, ("ob", half))
                inc("v", v.tensor_add(ob[:, half * CD:(half + 1) * CD],
                                      accb[:, j * CD:(j + 1) * CD],
                                      xf[:, half * CD:(half + 1) * CD]))
            set_guard(("xf", half), "v")

            @block.sync
            def _(s, j=j, half=half):
                wait(s, "v")
                s.dma_start(out_d[j * 128:(j + 1) * 128, :],
                            ob[:, half * CD:(half + 1) * CD]).then_inc(sems["o"], 16)
            cnt["o"] += 16
            set_guard(("ob", half), "o")

        @block.sync
        def _(s):
            s.wait_ge(sems["o"], cnt["o"])

    if dbg:
        barrier()
        avail = {}
        if stage < 3:
            avail.update({
                "d_klatT": (klatT[:, :], True), "d_vlat": (vlat[:, :], True),
                "d_qtl": (qtl[:, :], True),
            })
        avail.update({
            "d_xnT": (xnT[:, :], True),
            "d_scoresP": (scoresP[:, :], False),
            "d_scoresT": (scoresT[:, :], True),
            "d_acc": (accb[:, :], False),
            "d_big": (big[:, :], True),
        })
        for name in dbg:
            if name not in avail:
                continue
            src_ap, cast = avail[name]
            if cast:
                @block.gpsimd
                def _(g, name=name, src_ap=src_ap):
                    g.dma_start(dbg_d[name], src_ap).then_inc(sems["w"], 16)
                cnt["w"] += 16
            else:
                @block.sync
                def _(s, name=name, src_ap=src_ap):
                    s.dma_start(dbg_d[name], src_ap).then_inc(sems["l"], 16)
                cnt["l"] += 16

        @block.sync
        def _(s):
            s.wait_ge(sems["l"], cnt["l"])

        @block.gpsimd
        def _(g):
            g.wait_ge(sems["w"], cnt["w"])

    for st_ in reversed(open_stacks):
        st_.close()
    return nc


def tile_w(w):
    n = w.shape[1]
    a = w.reshape(16, 128, n // 512, 512)
    a = a.transpose(1, 2, 0, 3)
    return np.ascontiguousarray(a.reshape(128, -1))


def prep_inputs(ii):
    x = np.asarray(ii["L0_attn_x"], np.float32)
    wq = np.asarray(ii["L0_attn_wq"], np.float32)
    wk = np.asarray(ii["L0_attn_wk_lat"], np.float32)
    wv = np.asarray(ii["L0_attn_wv_lat"], np.float32)
    kexpand = np.asarray(ii["L0_attn_k_expand"], np.float32)
    vexpand = np.asarray(ii["L0_attn_v_expand"], np.float32)
    projw = np.asarray(ii["L0_attn_proj_w"], np.float32)
    rw = np.asarray(ii["L0_ffn_router_weight"], np.float32)
    rb = np.asarray(ii["L0_ffn_router_bias"], np.float32)
    gup = np.asarray(ii["L0_ffn_gate_up_proj"], np.float32)
    gupb = np.asarray(ii["L0_ffn_gate_up_proj_bias"], np.float32)
    dw = np.asarray(ii["L0_ffn_down_proj"], np.float32)
    db = np.asarray(ii["L0_ffn_down_proj_bias"], np.float32)

    P128 = np.concatenate([np.arange(0, 128, 2), np.arange(1, 128, 2)])
    P64 = np.concatenate([np.arange(0, 64, 2), np.arange(1, 64, 2)])

    shared = {
        "jrow": np.broadcast_to(np.arange(64, dtype=np.float32), (128, 64)).copy(),
        "eye_f": np.eye(128, dtype=np.float32),
        "pos_ctx": (np.arange(32, dtype=np.float32)[None, :] * 128
                    + np.arange(128, dtype=np.float32)[:, None]).astype(np.float32),
        "wq_t": tile_w(wq),
        "kvw_t": np.ascontiguousarray(np.concatenate([wk, wv], axis=1).reshape(16, 128, 128).transpose(1, 0, 2).reshape(128, 2048)),
        "kexp_t": np.ascontiguousarray(
            kexpand.transpose(0, 2, 1)[:, P128][:, :, P64].transpose(1, 0, 2).reshape(128, 16 * 64)),
        "vexp_t": np.ascontiguousarray(vexpand.transpose(1, 0, 2).reshape(64, 16 * 128)),
        "proj_t": tile_w(projw),
        "rwT_t": np.ascontiguousarray(rw.T.reshape(16, 128, 16).transpose(1, 0, 2).reshape(128, 256)),
        "rbias": rb.reshape(1, 16).copy(),
        "wg_t": np.stack([tile_w(np.ascontiguousarray(gup[e][:, 0::2])) for e in range(16)]),
        "wu_t": np.stack([tile_w(np.ascontiguousarray(gup[e][:, 1::2])) for e in range(16)]),
        "biasg": np.ascontiguousarray(gupb[:, 0::2]),
        "biasu": np.ascontiguousarray(gupb[:, 1::2]),
        "down_t": np.stack([tile_w(dw[e]) for e in range(16)]),
        "dbias": db.copy(),
        "ltri": np.triu(np.ones((128, 128), np.float32)),
        "iota384": np.broadcast_to(np.arange(384, dtype=np.float32), (128, 384)).copy(),
        "piotak": (np.arange(128, dtype=np.float32)[:, None]
                   + np.array([0, 128, 256, 0], np.float32)[None, :]).copy(),
    }

    in_maps = []
    own_tiles = []
    for c in range(8):
        b, cp = c // 4, c % 4
        tiles = [4 * jj + cp for jj in range(8)]
        own_tiles.append(tiles)
        rows = np.concatenate([np.arange(g * 128, (g + 1) * 128) for g in tiles])
        mask = np.zeros((128, 4096), np.float32)
        for jj, g in enumerate(tiles):
            gq = g * 128 + np.arange(128)[:, None]
            gk = jj * 512 + np.arange(512)[None, :]
            mask[:, jj * 512:(jj + 1) * 512] = np.where(gk <= gq, 0.0, NEG)
        posn = (np.array([g * 128 for g in tiles], np.float32)[None, :]
                + np.arange(128, dtype=np.float32)[:, None])
        m = dict(shared)
        m["x_ctx"] = np.ascontiguousarray(x[b])
        m["x_own"] = np.ascontiguousarray(x[b][rows])
        m["maskd"] = mask
        m["pos_own"] = posn.astype(np.float32)
        in_maps.append(m)
    return in_maps, own_tiles


_CACHE = {}


def kernel(**inputs):
    in_maps, own_tiles = prep_inputs(inputs)
    if "nc" not in _CACHE:
        _CACHE["nc"] = build(stage=3, dbg=())
    res = run_bass_kernel_spmd(_CACHE["nc"], in_maps, core_ids=list(range(8)))
    out = np.zeros((2, 4096, 2048), np.float32)
    for c in range(8):
        b = c // 4
        o = res.results[c]["out"].reshape(8, 128, 2048)
        for jj, g in enumerate(own_tiles[c]):
            out[b, g * 128:(g + 1) * 128, :] = o[jj]
    return out
